# version 12
# baseline (speedup 1.0000x reference)
import numpy as np
import ml_dtypes
from contextlib import ExitStack

import concourse.bass as bass
import concourse.mybir as mybir
from concourse.bass_utils import run_bass_kernel_spmd

F32 = mybir.dt.float32
BF16 = mybir.dt.bfloat16
AF = mybir.ActivationFunctionType
ALU = mybir.AluOpType
AX = mybir.AxisListType
BF = ml_dtypes.bfloat16

B = 2
T = 8192
D = 2048
NCORES = 8
CHUNKS = 4
OWN = T // CHUNKS
HALO1 = 1024
T1 = OWN + HALO1
HALO0 = 512
TLOCP = T1 + HALO0
NEG = -1.0e30
SEL_IMM = -2.0e30
SR = 512
NSR = T1 // SR
H0, HD0 = 16, 128
QC = 512
ROPE0 = 64
M0 = 16
NBL = T // M0
TOPK = 64
NIH, CI = 4, 64
G0, DG = 4, 512
WIN0 = 256
EPS0 = 1e-6
H1, HK1, HD1 = 16, 4, 128
WIN1 = 1024
EPS1 = 1e-6
E0, R0 = 8, 8
SCALE0 = 16.0 / 8.0
NE1, NEXP1, E1, S1, C1 = 8, 6, 1024, 2, 2
LN_EPS = 1e-5
TCH = 512
NTCH = OWN // TCH


class Ctr:
    __slots__ = ("sem", "val", "name")

    def __init__(self, sem, name):
        self.sem = sem
        self.val = 0
        self.name = name


def tok(ctr):
    return (ctr, ctr.val)


class Prog:
    ENGINES = ("sync", "tensor", "vector", "scalar", "gpsimd")
    SELF_EDGE = ("vector", "scalar", "gpsimd")

    def __init__(self, nc, ctx):
        self.nc = nc
        self.ctx = ctx
        self.q = {e: [] for e in self.ENGINES}
        self._ctrs = {}
        self._last = {e: None for e in self.ENGINES}

    def ctr(self, name):
        if name not in self._ctrs:
            sem = self.ctx.enter_context(self.nc.semaphore(name))
            self._ctrs[name] = Ctr(sem, name)
        return self._ctrs[name]

    def op(self, eng, fn, inc=None, k=1, dma=False, track=True):
        if eng in self.SELF_EDGE and not dma:
            prev = self._last[eng]
            if prev is not None:
                self.q[eng].append(("wait", prev[0], prev[1]))
            if inc is None and track:
                inc = self.ctr(eng + "_ser")
                k = 1
        if inc is not None:
            inc.val += k
            self.q[eng].append(("op", fn, inc, k))
            if eng in self.SELF_EDGE and not dma and track:
                self._last[eng] = (inc, inc.val)
        else:
            self.q[eng].append(("op", fn, None, 0))

    def wait(self, eng, ctr, val=None):
        self.q[eng].append(("wait", ctr, ctr.val if val is None else val))

    def wait_tok(self, eng, toks):
        for (ctr, val) in toks:
            if val > 0:
                self.q[eng].append(("wait", ctr, val))

    def barrier(self, engines=("tensor", "vector", "scalar", "gpsimd")):
        bar = self.ctr("bar")
        for e in engines:
            self.op(e, lambda eng: eng.nop(), inc=bar, track=False)
        tgt = bar.val
        for e in engines:
            self.q[e].append(("wait", bar, tgt))

    def check(self):
        ptr = {e: 0 for e in self.ENGINES}
        cur = {name: 0 for name in self._ctrs}
        progress = True
        while progress:
            progress = False
            for e in self.ENGINES:
                while ptr[e] < len(self.q[e]):
                    item = self.q[e][ptr[e]]
                    if item[0] == "wait":
                        if cur[item[1].name] < item[2]:
                            break
                    else:
                        _, fn, inc, k = item
                        if inc is not None:
                            cur[inc.name] += k
                    ptr[e] += 1
                    progress = True
        stuck = {e: (ptr[e], len(self.q[e])) for e in self.ENGINES if ptr[e] < len(self.q[e])}
        if stuck:
            detail = []
            for e, (p, n) in stuck.items():
                item = self.q[e][p]
                if item[0] == "wait":
                    detail.append(f"{e}@{p}/{n} waits {item[1].name}>={item[2]} (cur {cur[item[1].name]})")
                else:
                    detail.append(f"{e}@{p}/{n} op-blocked?")
            raise RuntimeError("deadlock in program build: " + "; ".join(detail))

    def run(self):
        self.check()
        with self.nc.Block() as block:
            for eng_name in self.ENGINES:
                items = self.q[eng_name]
                if not items:
                    continue
                dec = getattr(block, eng_name)

                def body(eng, items=items):
                    for item in items:
                        if item[0] == "wait":
                            eng.wait_ge(item[1].sem, item[2])
                        else:
                            _, fn, inc, k = item
                            inst = fn(eng)
                            if inc is not None:
                                inst.then_inc(inc.sem, k)

                dec(body)


def emit_dma(P, eng, dst, src, inc):
    P.op(eng, lambda e, d=dst, s=src: e.dma_start(out=d, in_=s), inc=inc, k=16, dma=True)


class WStream:
    RING = 4

    RING_U = 4

    def __init__(self, P, ctx, name="slab", ring=None):
        self.P = P
        self.RING = WStream.RING_U
        self.bufs = [ctx.enter_context(P.nc.sbuf_tensor(f"{name}{i}", [128, 2048], BF16))
                     for i in range(self.RING)]
        self.lds = [P.ctr(f"sl_ld{i}") for i in range(self.RING)]
        self.use = P.ctr("sl_use")
        self.seq = self.use.val
        self.pair = self.seq // 2
        self._half = None

    def chunk(self, dram2d, row0, col0, ncols, row1=None):
        j = self.seq
        self.seq += 1
        if self._half is not None:
            buf, tk, off = self._half
            self._half = None
            return {"stream": self, "buf": buf, "toks": [tk], "loff": off}
        pj = self.pair
        self.pair += 1
        slot = pj % self.RING
        buf = self.bufs[slot]
        if pj >= self.RING:
            self.P.wait("sync", self.use, 2 * (pj - self.RING) + 2)
        if row1 is not None:
            dst = buf[:, 0:2 * ncols].rearrange("p (a n) -> p a n", a=2)
            import concourse.bass as _bass
            t = dram2d.tensor
            rs = dram2d.ap[0][0]
            base = dram2d.offset + row0 * rs + col0
            sap = _bass.AP(t, base, [[rs, 128], [(row1 - row0) * rs, 2], [1, ncols]])
            emit_dma(self.P, "sync", dst, sap, self.lds[slot])
            tk = (self.lds[slot], self.lds[slot].val)
            self._half = (buf, tk, ncols)
            return {"stream": self, "buf": buf, "toks": [tk], "loff": 0}
        emit_dma(self.P, "sync", buf[:, 0:ncols],
                 dram2d[row0:row0 + 128, col0:col0 + ncols], self.lds[slot])
        self.seq += 1
        self.use.val += 0
        self._single_pad = True
        return {"stream": self, "buf": buf, "toks": [(self.lds[slot], self.lds[slot].val)],
                "loff": 0, "pad_use": True}


def gemm(P, name, *, chunks, ftiles, N, banks, evac, pre_toks=(), evac_eng="scalar"):
    mm = P.ctr("gmm") if chunks and ("stream" not in chunks[-1]) else None
    P.wait_tok("tensor", list(pre_toks))
    nk = len(chunks)
    comp_tok = None
    for kci, ch in enumerate(chunks):
        P.wait_tok("tensor", ch.get("toks", ()))
        rhs = ch["rhs"]
        strm = ch.get("stream")
        for fl, ft in enumerate(ftiles):
            last_of_chunk = fl == len(ftiles) - 1
            if "buf" in ch:
                c0 = ch["lcols"](fl, ft) + ch.get("loff", 0)
                lhsT = ch["buf"][:, c0:c0 + 128]
            else:
                lhsT = ch["lhsT"](fl, ft)
            inc = None
            k = 1
            if last_of_chunk and strm is not None:
                inc = strm.use
                if ch.get("pad_use"):
                    k = 2
            elif last_of_chunk and kci == nk - 1 and mm is not None:
                inc = mm
            P.op("tensor",
                 lambda e, o=banks[fl][:, 0:N], l=lhsT, r=rhs,
                        s=(kci == 0), p=(kci == nk - 1): e.matmul(o, l, r, start=s, stop=p),
                 inc=inc, k=k)
        if kci == nk - 1:
            comp_tok = tok(strm.use) if strm is not None else tok(mm)
    if evac is not None:
        P.wait_tok(evac_eng, [comp_tok])
        for fl, ft in enumerate(ftiles):
            evac(fl, ft, banks[fl][:, 0:N])
    return comp_tok


def layer_norm(P, name, *, rbuf, rT, t0, N, out_fn, out_dtype_note, banks, sb,
               sq_scratch, rows, done_ctr, r_toks, nf=16, eps_col=None,
               ones_key="ones_col_f_sb"):
    mm = P.ctr("ln_mm")
    sq = P.ctr("ln_sq")
    vv = P.ctr("ln_vv")
    onesf = sb[ones_key]
    P.wait("scalar", mm)
    P.wait("vector", mm)
    P.wait_tok("tensor", r_toks)
    for ft in range(nf):
        P.op("tensor", lambda e, o=banks[0][0:1, 0:N], l=onesf[:, 0:1],
             r=rbuf[:, ft * rT + t0: ft * rT + t0 + N], s=(ft == 0), p=(ft == nf - 1):
             e.matmul(o, l, r, start=s, stop=p), inc=(mm if ft == nf - 1 else None))
    P.wait_tok("scalar", r_toks)
    for ft in range(nf):
        par = (ft % 2) * 512
        if ft >= 2:
            P.wait("scalar", mm, mm.val - 1)
        P.op("scalar", lambda e, o=sq_scratch[:, par:par + N],
             i=rbuf[:, ft * rT + t0: ft * rT + t0 + N]:
             e.activation(out=o, in_=i, func=AF.Square, bias=0.0, scale=1.0), inc=sq)
        P.wait("tensor", sq)
        P.op("tensor", lambda e, o=banks[1][0:1, 0:N], l=onesf[:, 0:1],
             r=sq_scratch[:, par:par + N], s=(ft == 0), p=(ft == nf - 1):
             e.matmul(o, l, r, start=s, stop=p), inc=mm)
    m_row = rows[0:1, 0:N]
    ex2_row = rows[0:1, 512:512 + N]
    m2_row = rows[0:1, 1024:1024 + N]
    var_row = rows[0:1, 512:512 + N]
    rstd_row = rows[0:1, 1024:1024 + N]
    mr_row = rows[0:1, 512:512 + N]
    P.wait("vector", mm)
    P.op("vector", lambda e: e.tensor_scalar_mul(m_row, banks[0][0:1, 0:N], 1.0 / D))
    P.op("vector", lambda e: e.tensor_scalar_mul(ex2_row, banks[1][0:1, 0:N], 1.0 / D))
    P.op("vector", lambda e: e.tensor_mul(m2_row, m_row, m_row))
    P.op("vector", lambda e: e.tensor_sub(var_row, ex2_row, m2_row), inc=vv)
    P.wait("scalar", vv)
    P.op("scalar", lambda e: e.activation(out=var_row, in_=var_row, func=AF.Sqrt,
                                          bias=eps_col[0:1, 0:1], scale=1.0), inc=sq)
    P.wait("vector", sq)
    P.op("vector", lambda e: e.reciprocal(rstd_row, var_row))
    P.op("vector", lambda e: e.tensor_mul(mr_row, m_row, rstd_row), inc=vv)
    P.wait("tensor", vv)
    P.op("tensor", lambda e: e.matmul(banks[2][:, 0:N], sb["ones_row_f_sb"][0:1, :],
                                      rstd_row, start=True, stop=True))
    P.op("tensor", lambda e: e.matmul(banks[1][:, 0:N], sb["ones_row_f_sb"][0:1, :],
                                      mr_row, start=True, stop=True), inc=mm)
    P.wait("vector", mm)
    for ft in range(nf):
        dst = out_fn(ft)
        src = rbuf[:, ft * rT + t0: ft * rT + t0 + N]
        P.op("vector", lambda e, s=src: e.tensor_mul(s, s, banks[2][:, 0:N]))
        P.op("vector", lambda e, s=src, d=dst: e.tensor_sub(d, s, banks[1][:, 0:N]),
             inc=done_ctr)



class ActLoader:

    def __init__(self, P, ctx, name, nslots=3, width=512, eng="gpsimd", dtype=F32):
        self.P = P
        self.eng = eng
        self.bufs = [ctx.enter_context(P.nc.sbuf_tensor(f"{name}{i}", [128, width], dtype))
                     for i in range(nslots)]
        self.lds = [P.ctr(f"{name}_ld{i}") for i in range(nslots)]
        self.cons = P.ctr(name + "_cons")
        self.n = nslots
        self.seq = self.cons.val

    def load(self, dram2d, r0, c0, ncols):
        j = self.seq
        self.seq += 1
        sl = j % self.n
        if j >= self.n:
            self.P.wait(self.eng, self.cons, j - self.n + 1)
        emit_dma(self.P, self.eng, self.bufs[sl][:, 0:ncols],
                 dram2d[r0:r0 + 128, c0:c0 + ncols], self.lds[sl])
        return self.bufs[sl], [(self.lds[sl], self.lds[sl].val)]


def load_cast_slice(P, loader, dram2d, c0, dest, destT, ncols, nf=16):
    for ft in range(nf):
        buf, toks = loader.load(dram2d, ft * 128, c0, ncols)
        P.wait_tok("vector", toks)
        P.op("vector", lambda e, o=dest[:, ft * destT:ft * destT + ncols], i=buf[:, 0:ncols]:
             e.tensor_copy(out=o, in_=i), inc=loader.cons)
    return [tok(loader.cons)]


def rope_norm(P, sb, banks, *, src, Cs, Ss, qs, tmp, sq, rows, scalemat, out_bf,
              wrow, eps_col, extra_scale, half, N, statbank, bcbank, vv, ev):
    rn0 = P.ctr("rn_mm")
    P.wait("vector", rn0)
    P.wait("scalar", rn0)
    P.op("vector", lambda e: e.tensor_mul(qs[0:half, 0:N], src[half:2 * half, 0:N],
                                          Ss[0:half, 0:N]))
    P.op("vector", lambda e: e.tensor_mul(qs[half:2 * half, 0:N], src[0:half, 0:N],
                                          Ss[half:2 * half, 0:N]))
    P.op("vector", lambda e: e.tensor_mul(tmp[:, 0:N], src[:, 0:N], Cs[:, 0:N]))
    P.op("vector", lambda e: e.tensor_add(tmp[:, 0:N], tmp[:, 0:N], qs[:, 0:N]), inc=vv)
    P.wait("scalar", vv)
    P.op("scalar", lambda e: e.activation(out=sq[:, 0:N], in_=tmp[:, 0:N], func=AF.Square,
                                          bias=0.0, scale=1.0), inc=ev)
    P.wait_tok("tensor", [tok(ev)])
    P.op("tensor", lambda e: e.matmul(statbank[0:1, 0:N], sb["ones_col_f_sb"][:, 0:1],
                                      sq[:, 0:N], start=True, stop=True), inc=P.ctr("rn_mm"))
    rn = P.ctr("rn_mm")
    P.wait("scalar", rn)
    if eps_col is not None:
        P.op("scalar", lambda e: e.activation(out=rows[0:1, 0:N], in_=statbank[0:1, 0:N],
                                              func=AF.Sqrt, bias=eps_col[0:1, 0:1],
                                              scale=1.0), inc=ev)
        P.wait("vector", ev)
        P.op("vector", lambda e: e.reciprocal(rows[0:1, 0:N], rows[0:1, 0:N]))
    else:
        P.op("scalar", lambda e: e.activation(out=rows[0:1, 0:N], in_=statbank[0:1, 0:N],
                                              func=AF.Sqrt, bias=0.0, scale=1.0), inc=ev)
        P.wait("vector", ev)
        P.op("vector", lambda e: e.tensor_scalar_add(rows[0:1, 0:N], rows[0:1, 0:N], EPS1))
        P.op("vector", lambda e: e.reciprocal(rows[0:1, 0:N], rows[0:1, 0:N]))
    if extra_scale != 1.0:
        P.op("vector", lambda e: e.tensor_scalar_mul(rows[0:1, 0:N], rows[0:1, 0:N],
                                                     float(extra_scale)))
    P.op("vector", lambda e: e.nop(), inc=vv)
    P.wait_tok("tensor", [tok(vv)])
    P.op("tensor", lambda e: e.matmul(bcbank[:, 0:N], wrow[0:1, :], rows[0:1, 0:N],
                                      start=True, stop=True), inc=rn)
    P.wait("scalar", rn)
    P.op("scalar", lambda e: e.activation(out=scalemat[:, 0:N], in_=bcbank[:, 0:N],
                                          func=AF.Identity, bias=0.0, scale=1.0), inc=ev)
    P.wait("vector", ev)
    P.op("vector", lambda e: e.tensor_mul(out_bf[:, 0:N], tmp[:, 0:N], scalemat[:, 0:N]),
         inc=vv)



def _bf(x):
    return np.ascontiguousarray(np.asarray(x, dtype=np.float32).astype(BF))


def _f32(x):
    return np.ascontiguousarray(np.asarray(x, dtype=np.float32))


class BiasPack:
    def __init__(self):
        self.cols = []
        self.index = {}

    def add(self, name, vec):
        vec = np.asarray(vec, dtype=np.float32).reshape(-1)
        assert vec.size % 128 == 0
        k = vec.size // 128
        self.index[name] = (len(self.cols), k)
        for i in range(k):
            self.cols.append(vec[i * 128:(i + 1) * 128])

    def array(self):
        return np.ascontiguousarray(np.stack(self.cols, axis=1).astype(np.float32))

    @property
    def array_cols(self):
        return len(self.cols)

    def col(self, name, i=0):
        base, k = self.index[name]
        assert i < k
        return base + i


def host_prep(inputs):
    h = np.asarray(inputs["L0_attn_h"], dtype=np.float32)

    inv0 = 1.0 / (10000.0 ** (np.arange(0, ROPE0, 2, dtype=np.float32) / ROPE0))
    inv1 = 1.0 / (10000.0 ** (np.arange(0, HD1, 2, dtype=np.float32) / HD1))

    bp = BiasPack()
    bp.add("ln_eps", np.full(128, LN_EPS, np.float32))
    bp.add("rms_eps", np.full(128, 128.0 * EPS0, np.float32))
    bp.add("b_base", inputs["L0_ffn_b_base"])
    bp.add("q_norm_w", inputs["L0_attn_q_norm_w"])
    bp.add("k_norm_w", inputs["L0_attn_k_norm_w"])
    bp.add("gamma_q", inputs["L1_attn_gamma_q"])
    bp.add("gamma_k", inputs["L1_attn_gamma_k"])
    bp.add("bsu", inputs["L1_ffn_bsu"])
    bp.add("bsa", inputs["L1_ffn_bsa"])
    bp.add("bsd", inputs["L1_ffn_bsd"])
    for n in range(NEXP1):
        bp.add(f"bup{n}", np.asarray(inputs["L1_ffn_bup"])[n])
        bp.add(f"bsw{n}", np.asarray(inputs["L1_ffn_bsw"])[n])
    bpack = bp.array()

    W = {}
    W["w_qc"] = _f32(inputs["L0_attn_w_qc"])
    W["w_qup"] = _f32(inputs["L0_attn_w_qup"])
    W["kvc_w"] = _f32(inputs["L0_attn_kvc_w"])
    W["kvc_wz"] = _f32(inputs["L0_attn_kvc_wz"])
    W["idx_pack"] = _f32(np.concatenate([np.asarray(inputs["L0_attn_idx_c_w"]),
                                         np.asarray(inputs["L0_attn_idx_c_wz"])], axis=1))
    W["k_proj"] = _f32(inputs["L0_attn_k_proj_w"])
    W["v_proj"] = _f32(inputs["L0_attn_v_proj_w"])
    W["w_dq"] = _f32(inputs["L0_attn_w_dq"])
    W["w_iuq"] = _f32(inputs["L0_attn_w_iuq"])
    W["w_w"] = _f32(inputs["L0_attn_w_w"])
    W["group_w"] = _f32(np.asarray(inputs["L0_attn_group_w"]).reshape(G0 * DG, DG))
    W["final_w"] = _f32(inputs["L0_attn_final_w"])
    W["W_baseT"] = _f32(np.asarray(inputs["L0_ffn_W_base"]).T)
    W["W_routerT"] = _f32(np.asarray(inputs["L0_ffn_W_router"]).T)
    A_w = np.asarray(inputs["L0_ffn_A_w"])
    W["A_allT"] = _f32(np.transpose(A_w, (2, 0, 1)).reshape(D, E0 * R0))
    B_w = np.asarray(inputs["L0_ffn_B_w"])
    ballt = np.transpose(B_w, (0, 2, 1)).reshape(E0 * R0, D)
    W["B_allT_pad"] = _f32(np.concatenate([ballt, np.zeros((128 - E0 * R0, D), np.float32)], axis=0))
    W["wqT"] = _f32(np.asarray(inputs["L1_attn_wq"]).T)
    W["wkT"] = _f32(np.asarray(inputs["L1_attn_wk"]).T)
    W["wvT"] = _f32(np.asarray(inputs["L1_attn_wv"]).T)
    W["woT"] = _f32(np.asarray(inputs["L1_attn_wo"]).T)
    W["Wg0"] = _f32(np.asarray(inputs["L1_ffn_Wg"])[0])
    W["Wg1"] = _f32(np.asarray(inputs["L1_ffn_Wg"])[1])
    W["Wup2d"] = _f32(np.asarray(inputs["L1_ffn_Wup"]).reshape(NEXP1 * D, E1))
    W["Wsw2d"] = _f32(np.asarray(inputs["L1_ffn_Wsw"]).reshape(NEXP1 * E1, E1))
    W["Wdn2d"] = _f32(np.asarray(inputs["L1_ffn_Wdn"]).reshape(NEXP1 * E1, D))
    W["Wsu"] = _f32(inputs["L1_ffn_Wsu"])
    W["Wsa"] = _f32(inputs["L1_ffn_Wsa"])
    W["Wsd"] = _f32(inputs["L1_ffn_Wsd"])
    bdn = np.asarray(inputs["L1_ffn_bdn"], dtype=np.float32)
    W["bdn_pad"] = _f32(np.concatenate([bdn, np.zeros((128 - NEXP1, D), np.float32)], axis=0))
    bg = np.asarray(inputs["L1_ffn_bg"], dtype=np.float32)
    bg0 = np.ascontiguousarray(np.tile(bg[0][None, :], (128, 1)).astype(np.float32))
    bg1 = np.ascontiguousarray(np.tile(bg[1][None, :], (128, 1)).astype(np.float32))

    kvcb = np.asarray(inputs["L0_attn_kvc_bias"], dtype=np.float32)
    kvc_bias_rep = np.ascontiguousarray(np.tile(kvcb.T, (1, 32)).astype(np.float32))
    idxb = np.asarray(inputs["L0_attn_idx_c_bias"], dtype=np.float32)
    idx_bias_rep = np.zeros((128, 512), np.float32)
    idx_bias_rep[64:128, :] = np.tile(idxb.T, (1, 32))

    ident_bf = _bf(np.eye(128, dtype=np.float32))
    ident_f = _f32(np.eye(128, dtype=np.float32))
    ones_col_f = _f32(np.ones((128, 1), np.float32))
    ones_col_bf = _bf(np.ones((128, 1), np.float32))
    ones_row_f = _f32(np.ones((1, 128), np.float32))
    ones_row_bf = _bf(np.ones((1, 128), np.float32))
    bn_row = np.ascontiguousarray(np.tile((np.arange(NBL, dtype=np.float32) * M0 + (M0 - 1))[None, :],
                                          (128, 1)))
    gqrow = _f32(np.asarray(inputs["L1_attn_gamma_q"]).reshape(1, 128))
    gkrow = _f32(np.asarray(inputs["L1_attn_gamma_k"]).reshape(1, 128))
    qnrow = _f32(np.asarray(inputs["L0_attn_q_norm_w"]).reshape(1, 128))
    knrow = _f32(np.asarray(inputs["L0_attn_k_norm_w"]).reshape(1, 128))
    selbc = np.zeros((128, 8 * 128), np.float32)
    for n_ in range(8):
        selbc[n_, n_ * 128:(n_ + 1) * 128] = 1.0
    selbc = _bf(selbc)
    i0 = np.arange(128)[:, None]
    j0 = np.arange(384)[None, :]
    genmask0 = _bf(np.where((j0 > i0) & (j0 <= i0 + WIN0), 0.0, NEG))
    j1 = np.arange(1152)[None, :]
    genmask1 = _bf(np.where((j1 >= i0) & (j1 <= i0 + WIN1), 0.0, NEG))

    per_core = []
    for core in range(NCORES):
        b = core // CHUNKS
        ch = core % CHUNKS
        s = ch * OWN
        e = s + OWN
        hT = np.ascontiguousarray(h[b].T)

        lo = s - (HALO1 + HALO0)
        hT_loc = np.zeros((D, TLOCP), np.float32)
        a0 = max(lo, 0)
        hT_loc[:, a0 - lo:TLOCP] = hT[:, a0:e]
        hT_loc = np.ascontiguousarray(hT_loc)

        pos_t1 = np.arange(s - HALO1, e, dtype=np.float32)
        pos_t1[pos_t1 < 0] = -1.0
        tpos_cols = np.ascontiguousarray(pos_t1.reshape(NSR * 4, 128).T)

        posk0 = np.arange(lo, e, dtype=np.float32)
        kval0 = _f32(np.where(posk0 < 0, NEG, 0.0)[None, :])
        posk1 = np.arange(s - HALO1, e, dtype=np.float32)
        kval1 = _f32(np.where(posk1 < 0, NEG, 0.0)[None, :])

        p0 = np.maximum(np.arange(lo, e, dtype=np.float32), 0.0)
        ang0 = p0[None, :] * inv0[:, None]
        cos0 = np.cos(ang0)
        sin0 = np.sin(ang0)
        C0 = np.ones((128, TLOCP), np.float32)
        C0[0:32, :] = cos0
        C0[32:64, :] = cos0
        S0 = np.zeros((128, TLOCP), np.float32)
        S0[0:32, :] = -sin0
        S0[32:64, :] = sin0
        p1 = np.maximum(np.arange(s - HALO1, e, dtype=np.float32), 0.0)
        ang1 = p1[None, :] * inv1[:, None]
        cos1 = np.cos(ang1)
        sin1 = np.sin(ang1)
        C1f = np.empty((128, T1), np.float32)
        C1f[0:64, :] = cos1
        C1f[64:128, :] = cos1
        S1f = np.empty((128, T1), np.float32)
        S1f[0:64, :] = -sin1
        S1f[64:128, :] = sin1

        d = {
            "hT_full": hT, "hT_loc": hT_loc,
            "bpack": bpack, "bg0": bg0, "bg1": bg1,
            "kvc_bias_rep": kvc_bias_rep, "idx_bias_rep": idx_bias_rep,
            "ident_bf": ident_bf, "ident_f": ident_f,
            "ones_col_f": ones_col_f, "ones_col_bf": ones_col_bf,
            "ones_row_f": ones_row_f, "ones_row_bf": ones_row_bf,
            "bn_row": bn_row, "genmask0": genmask0, "genmask1": genmask1, "selbc": selbc,
            "gqrow": gqrow, "gkrow": gkrow, "qnrow": qnrow, "knrow": knrow,
            "tpos_cols": tpos_cols, "kval0": kval0, "kval1": kval1,
            "kval1b": _bf(kval1), "kval0b": _bf(kval0),
            "C0": _f32(C0), "S0": _f32(S0), "C1": _f32(C1f), "S1": _f32(S1f),
        }
        d.update(W)
        per_core.append(d)
    return per_core, bp



CONST_SPECS = [
    ("bg0", [128, 8], F32), ("bg1", [128, 8], F32),
    ("ident_bf", [128, 128], BF16), ("ident_f", [128, 128], F32),
    ("ones_col_f", [128, 1], F32), ("ones_col_bf", [128, 1], BF16),
    ("ones_row_f", [1, 128], F32), ("ones_row_bf", [1, 128], BF16),
    ("bn_row", [128, 512], F32), ("genmask0", [128, 384], BF16),
    ("genmask1", [128, 1152], BF16), ("tpos_cols", [128, NSR * 4], F32),
    ("kval0", [1, TLOCP], F32), ("kval1", [1, T1], F32),
    ("C0", [128, TLOCP], F32), ("S0", [128, TLOCP], F32),
    ("C1", [128, T1], F32), ("S1", [128, T1], F32),
    ("kvc_bias_rep", [128, 512], F32), ("idx_bias_rep", [128, 512], F32),
    ("selbc", [128, 8 * 128], BF16),
    ("gqrow", [1, 128], F32), ("gkrow", [1, 128], F32),
    ("qnrow", [1, 128], F32), ("knrow", [1, 128], F32),
]


def load_consts(P, ctx, nc, names, bp_cols, with_bpack=True):
    sb = {}
    ld = P.ctr("const_ld")
    specs = [s for s in CONST_SPECS if s[0] in names]
    if with_bpack:
        specs.append(("bpack", [128, bp_cols], F32))
    for nm, shape, dt in specs:
        dr = nc.dram_tensor(nm, list(shape), dt, kind="ExternalInput").ap()
        t = ctx.enter_context(nc.sbuf_tensor(nm + "_sb", list(shape), dt))
        emit_dma(P, "sync", t[:], dr, ld)
        sb[nm + "_sb"] = t
    for e in ("tensor", "vector", "scalar", "gpsimd"):
        P.wait(e, ld)
    return sb


def emit_wconv(P, ctx, jobs, done_ctr):
    nstage = 4
    stg = [ctx.enter_context(P.nc.sbuf_tensor(f"wcs{i}", [128, 2048], F32)) for i in range(nstage)]
    bst = [ctx.enter_context(P.nc.sbuf_tensor(f"wcb{i}", [128, 2048], BF16)) for i in range(nstage)]
    lds = [P.ctr(f"wconv_ld{i}") for i in range(nstage)]
    sts = [P.ctr(f"wconv_st{i}") for i in range(nstage)]
    cast = P.ctr("wconv_cast")
    i = 0
    for (src, dst, Rr, Cc) in jobs:
        for r0 in range(0, Rr, 128):
            sl = i % nstage
            s_ = stg[sl]
            b_ = bst[sl]
            if i >= nstage:
                P.wait("gpsimd", cast, i - nstage + 1)
            emit_dma(P, "gpsimd", s_[:, 0:Cc], src[r0:r0 + 128, 0:Cc], lds[sl])
            P.wait("vector", lds[sl])
            if i >= nstage:
                P.wait("vector", sts[sl])
            P.op("vector", lambda e, o=b_[:, 0:Cc], ii=s_[:, 0:Cc]: e.tensor_copy(out=o, in_=ii),
                 inc=cast)
            P.wait("sync", cast)
            P.op("sync", lambda e, d=dst[r0:r0 + 128, 0:Cc], s2=b_[:, 0:Cc]:
                 e.dma_start(out=d, in_=s2), inc=sts[sl], k=16, dma=True)
            done_ctr.val += 0
            i += 1
    for sl in range(nstage):
        P.wait("gpsimd", sts[sl])
        P.wait("sync", sts[sl])
    P.op("gpsimd", lambda e: e.nop(), inc=done_ctr)
    return i


class WconvPump:

    def __init__(self, P, jobs, stg, bst, width=1024):
        self.P = P
        self.nstage = len(stg)
        self.stg = stg
        self.bst = bst
        self.lds = [P.ctr(f"wp_ld{i}") for i in range(self.nstage)]
        self.sts = [P.ctr(f"wp_st{i}") for i in range(self.nstage)]
        self.cast = P.ctr("wp_cast")
        self.chunks = []
        for (src, dst, Rr, Cc) in jobs:
            for r0 in range(0, Rr, 128):
                for c0 in range(0, Cc, width):
                    self.chunks.append((src, dst, r0, c0, min(width, Cc - c0)))
        self.i = 0

    def pump(self, n):
        P = self.P
        for _ in range(n):
            if self.i >= len(self.chunks):
                return
            src, dst, r0, c0, w = self.chunks[self.i]
            sl = self.i % self.nstage
            q = ("sync", "gpsimd")[self.i % 2]
            if self.i >= self.nstage:
                P.wait(q, self.cast, self.i - self.nstage + 1)
            emit_dma(P, q, self.stg[sl][:, 0:w], src[r0:r0 + 128, c0:c0 + w],
                     self.lds[sl])
            P.wait("scalar", self.lds[sl])
            if self.i >= self.nstage:
                P.wait("scalar", self.sts[sl])
            P.op("scalar", lambda e, o=self.bst[sl][:, 0:w], ii=self.stg[sl][:, 0:w]:
                 e.activation(out=o, in_=ii, func=AF.Copy, bias=0.0, scale=1.0),
                 inc=self.cast)
            P.wait(q, self.cast)
            P.op(q, lambda e, d=dst[r0:r0 + 128, c0:c0 + w], s2=self.bst[sl][:, 0:w]:
                 e.dma_start(out=d, in_=s2), inc=self.sts[sl], k=16, dma=True)
            self.i += 1

    def finish(self):
        P = self.P
        self.pump(len(self.chunks) - self.i)
        for sl in range(self.nstage):
            P.wait("sync", self.sts[sl])
            P.wait("gpsimd", self.sts[sl])
            P.wait("scalar", self.sts[sl])
            P.wait("vector", self.sts[sl])


P3_NUCLEAR = [False]


def emit_p3(P, ctx, io, sb, banks, bp, ws):
    nc = P.nc

    def nbar():
        if P3_NUCLEAR[0]:
            P.barrier()
    dbg = io.get("dbg")
    dbg_ctr = P.ctr("dbg") if dbg is not None else None

    def dump(idx, ap, cols=512):
        if dbg is None:
            return
        P.barrier()
        emit_dma(P, "gpsimd", dbg[:, idx * 512: idx * 512 + cols], ap, dbg_ctr)
        P.wait("gpsimd", dbg_ctr)
        P.barrier()

    x2T, xtT, outT = io["x2T"], io["xtT"], io["outT"]
    WupB, WswB, WdnB = io["WupB"], io["WswB"], io["WdnB"]
    WsuB, WsaB, WsdB = io["WsuB"], io["WsaB"], io["WsdB"]
    bpk = sb["bpack_sb"]

    xsl = ctx.enter_context(nc.sbuf_tensor("p3_xsl", [128, 16 * TCH], BF16))
    hn = ctx.enter_context(nc.sbuf_tensor("p3_hn", [128, 2 * 8 * TCH], BF16))
    zb = ctx.enter_context(nc.sbuf_tensor("p3_zb", [128, 2 * 8 * TCH], BF16))
    HH = ctx.enter_context(nc.sbuf_tensor("p3_hh", [128, NEXP1 * 8 * TCH], BF16))
    hsh = ctx.enter_context(nc.sbuf_tensor("p3_hsh", [128, 16 * TCH], BF16))
    w6p = ctx.enter_context(nc.sbuf_tensor("p3_w6p", [128, TCH], BF16))
    wbsb = ctx.enter_context(nc.sbuf_tensor("p3_wbsb", [128, 2 * TCH], BF16))
    gt = ctx.enter_context(nc.sbuf_tensor("p3_gt", [128, 8], F32))
    gsc = ctx.enter_context(nc.sbuf_tensor("p3_gsc", [128, 8], F32))
    gcol = ctx.enter_context(nc.sbuf_tensor("p3_gcol", [128, 4], F32))
    rbuf = ctx.enter_context(nc.sbuf_tensor("p3_rbuf", [128, 16 * TCH], F32))
    tmpf = ctx.enter_context(nc.sbuf_tensor("p3_tmpf", [128, 2 * TCH], F32))
    sqs = ctx.enter_context(nc.sbuf_tensor("p3_sqs", [128, 2 * 512], F32))
    rows = ctx.enter_context(nc.sbuf_tensor("p3_rows", [1, 3 * 512], F32))
    x2s = [ctx.enter_context(nc.sbuf_tensor(f"p3_x2s{i}", [128, TCH], BF16)) for i in range(3)]
    wg = [sb["Wg0_bf"], sb["Wg1_bf"]]
    bdnp = sb["bdn_pad_bf"]

    xld = P.ctr("xld")
    x2lds = [P.ctr(f"p3_x2ld{i}") for i in range(3)]
    cst = P.ctr("cst")
    ev = P.ctr("ev")
    vv = P.ctr("vv")
    st = P.ctr("st")
    x2c = P.ctr("x2c")
    x2seq = [x2c.val]

    P.op("gpsimd", lambda e: e.memset(w6p[:], 0.0), inc=P.ctr("init"))
    P.wait("scalar", P.ctr("init"))

    def x2_stage(ft, c0):
        j = x2seq[0]
        x2seq[0] += 1
        sl = j % 3
        buf = x2s[sl]
        if j >= 3:
            P.wait("gpsimd", x2c, j - 2)
        emit_dma(P, "gpsimd", buf[:, 0:TCH], x2T[ft * 128:(ft + 1) * 128, c0:c0 + TCH], x2lds[sl])
        return buf, [(x2lds[sl], x2lds[sl].val)]

    def load_xslice(j, c0):
        if j == 0:
            P.wait("vector", st)
            for ft in range(16):
                buf, toks = x2_stage(ft, c0)
                P.wait_tok("vector", toks)
                P.op("vector", lambda e, o=xsl[:, ft * TCH:(ft + 1) * TCH], i=buf[:, 0:TCH]:
                     e.tensor_copy(out=o, in_=i), inc=x2c)
            P.op("vector", lambda e: e.nop(), inc=cst)
        else:
            P.wait("gpsimd", st)
            P.wait("vector", st)
            for ft in range(16):
                emit_dma(P, "gpsimd", xsl[:, ft * TCH:(ft + 1) * TCH],
                         xtT[ft * 128:(ft + 1) * 128, c0:c0 + TCH], xld)
        return [(xld, xld.val), (cst, cst.val)]

    for j in range(C1):
        for tch in range(NTCH):
            c0 = tch * TCH
            xtoks = load_xslice(j, c0)
            P.barrier()
            if j == 0 and tch == 0:
                dump(0, xsl[:, 0:512])

            for tt in range(4):
                chunks = [{"lhsT": (lambda fl, ft, kc=kc, tt=tt:
                                    xsl[:, kc * TCH + tt * 128: kc * TCH + tt * 128 + 128]),
                           "rhs": wg[j][:, kc * 8:(kc + 1) * 8],
                           "toks": xtoks if kc == 0 else ()} for kc in range(16)]
                mmt = gemm(P, f"g{j}{tch}{tt}", chunks=chunks, ftiles=[0], N=8,
                           banks=[banks[0]], evac=None, pre_toks=[tok(ev), tok(vv)])
                bgt = sb["bg0_sb"] if j == 0 else sb["bg1_sb"]
                P.wait_tok("vector", [mmt])
                P.op("vector", lambda e, b2=bgt: e.tensor_add(gt[:, 0:8], banks[0][:, 0:8], b2[:, 0:8]))
                P.op("vector", lambda e: e.tensor_reduce(gcol[:, 0:1], gt[:, 0:8], AX.X, ALU.min))
                P.op("vector", lambda e: e.tensor_scalar(out=gsc[:, 0:8], in0=gt[:, 0:8],
                                                         scalar1=gcol[:, 0:1], scalar2=None,
                                                         op0=ALU.is_equal))
                P.op("vector", lambda e: e.scalar_tensor_tensor(out=gsc[:, 0:8], in0=gsc[:, 0:8],
                                                                scalar=1.0e30, in1=gt[:, 0:8],
                                                                op0=ALU.mult, op1=ALU.add))
                P.op("vector", lambda e: e.tensor_reduce(gcol[:, 1:2], gsc[:, 0:8], AX.X, ALU.min))
                P.op("vector", lambda e: e.tensor_scalar(out=gsc[:, 0:8], in0=gt[:, 0:8],
                                                         scalar1=gcol[:, 1:2], scalar2=None,
                                                         op0=ALU.is_gt))
                P.op("vector", lambda e: e.reduce_max(gcol[:, 2:3], gt[:, 0:8], axis=AX.X,
                                                      negate=True), inc=vv)
                P.wait("scalar", vv)
                P.op("scalar", lambda e: e.activation(out=gt[:, 0:8], in_=gt[:, 0:8], func=AF.Exp,
                                                      bias=gcol[:, 2:3], scale=1.0), inc=ev)
                P.wait("vector", ev)
                P.op("vector", lambda e: e.tensor_mul(gt[:, 0:8], gt[:, 0:8], gsc[:, 0:8]))
                P.op("vector", lambda e: e.reduce_sum(gcol[:, 3:4], gt[:, 0:8], axis=AX.X))
                P.op("vector", lambda e: e.reciprocal(gcol[:, 3:4], gcol[:, 3:4]))
                P.op("vector", lambda e: e.tensor_scalar_mul(gt[:, 0:8], gt[:, 0:8], gcol[:, 3:4]),
                     inc=vv)
                P.wait_tok("tensor", [tok(vv), tok(ev)])
                P.op("tensor", lambda e: e.transpose(banks[1][0:8, 0:128], gt[:, 0:8],
                                                     sb["ident_f_sb"][:]), inc=P.ctr("tp"))
                P.wait("scalar", P.ctr("tp"))
                P.op("scalar", lambda e, tt=tt: e.activation(out=w6p[0:6, tt * 128:(tt + 1) * 128],
                                                             in_=banks[1][0:6, 0:128],
                                                             func=AF.Copy, bias=0.0, scale=1.0),
                     inc=ev)
            evt_gates = tok(ev)
            if j == 0 and tch == 0:
                dump(1, w6p[:, 0:512])

            wavetok = [None, None]
            for qw in range(4):
                par = qw % 2
                fts = list(range(qw * 4, (qw + 1) * 4))
                chunks = []
                for kc in range(16):
                    ch_ = ws.chunk(WsuB, kc * 128, qw * 512, 512,
                                   row1=((kc + 1) * 128 if kc % 2 == 0 else None))
                    ch_["lcols"] = lambda fl, ft: fl * 128
                    ch_["rhs"] = xsl[:, kc * TCH: kc * TCH + TCH]
                    if kc == 0:
                        ch_["toks"] = ch_["toks"] + xtoks
                    chunks.append(ch_)

                def ev_su(fl, ft, ps):
                    P.op("scalar", lambda e, o=hsh[:, ft * TCH:(ft + 1) * TCH], i=ps,
                         bcol=bpk[:, bp.col("bsu", ft):bp.col("bsu", ft) + 1]:
                         e.activation(out=o, in_=i, func=AF.Identity, bias=bcol, scale=1.0), inc=ev)

                pre = [wavetok[par]] if wavetok[par] else ([tok(ev), tok(vv)] if qw == 0 else [])
                gemm(P, f"su{j}{tch}{qw}", chunks=chunks, ftiles=fts, N=TCH,
                     banks=banks[par * 4:(par + 1) * 4], evac=ev_su, pre_toks=pre)
                wavetok[par] = tok(ev)
            evt_su = tok(ev)
            if j == 0 and tch == 0:
                P.wait_tok("gpsimd", [evt_su])
                dump(2, hsh[:, 0:512])
            for s_ in range(S1):
                P.wait("scalar", vv)
                for qw in range(2):
                    par = qw % 2
                    fts = list(range(qw * 4, (qw + 1) * 4))
                    chunks = []
                    for kc in range(8):
                        ch_ = ws.chunk(WsaB, kc * 128, qw * 512, 512,
                                       row1=((kc + 1) * 128 if kc % 2 == 0 else None))
                        ch_["lcols"] = lambda fl, ft: fl * 128
                        ch_["rhs"] = hsh[:, (s_ * 8 + kc) * TCH: (s_ * 8 + kc) * TCH + TCH]
                        if kc == 0:
                            ch_["toks"] = ch_["toks"] + [evt_su]
                        chunks.append(ch_)

                    def ev_sa(fl, ft, ps):
                        P.op("scalar", lambda e, o=zb[:, ft * TCH:(ft + 1) * TCH], i=ps,
                             bcol=bpk[:, bp.col("bsa", ft):bp.col("bsa", ft) + 1]:
                             e.activation(out=o, in_=i, func=AF.Silu, bias=bcol, scale=1.0),
                             inc=ev)

                    pre = [wavetok[par]] if wavetok[par] else [tok(ev), tok(vv)]
                    gemm(P, f"sa{j}{tch}{s_}{qw}", chunks=chunks, ftiles=fts, N=TCH,
                         banks=banks[par * 4:(par + 1) * 4], evac=ev_sa, pre_toks=pre)
                    wavetok[par] = tok(ev)
                P.wait_tok("vector", [tok(ev)])
                for ft in range(8):
                    gf = s_ * 8 + ft
                    P.op("vector", lambda e, a=zb[:, ft * TCH:(ft + 1) * TCH],
                         b2=hsh[:, gf * TCH: gf * TCH + TCH]:
                         e.tensor_mul(b2, a, b2), inc=vv)
            evt_hsh = tok(vv)
            if j == 0 and tch == 0:
                dump(3, hsh[:, 0:512])

            mult_done = [vv.val, vv.val]
            for n in range(NEXP1):
                par = n % 2
                po = par * 8 * TCH
                def ev_up(fl, ft, ps, n=n, po=po, par=par):
                    P.wait("scalar", vv, mult_done[par])
                    P.op("scalar", lambda e, o=hn[:, po + ft * TCH: po + (ft + 1) * TCH], i=ps,
                         bcol=bpk[:, bp.col(f"bup{n}", ft):bp.col(f"bup{n}", ft) + 1]:
                         e.activation(out=o, in_=i, func=AF.Identity, bias=bcol, scale=1.0), inc=ev)

                for qw in range(2):
                    bpar = qw % 2
                    chunks = []
                    for kc in range(16):
                        ch_ = ws.chunk(WupB, n * D + kc * 128, qw * 512, 512,
                                       row1=(n * D + (kc + 1) * 128 if kc % 2 == 0 else None))
                        ch_["lcols"] = lambda fl, ft: fl * 128
                        ch_["rhs"] = xsl[:, kc * TCH: kc * TCH + TCH]
                        if kc == 0 and qw == 0:
                            ch_["toks"] = ch_["toks"] + xtoks
                        chunks.append(ch_)
                    pre = [wavetok[bpar]] if wavetok[bpar] else [tok(ev)]
                    gemm(P, f"up{j}{tch}{n}{qw}", chunks=chunks,
                         ftiles=list(range(qw * 4, (qw + 1) * 4)), N=TCH,
                         banks=banks[bpar * 4:(bpar + 1) * 4], evac=ev_up, pre_toks=pre)
                    wavetok[bpar] = tok(ev)
                evt_up = tok(ev)
                def ev_sw(fl, ft, ps, n=n, po=po, par=par):
                    P.wait("scalar", vv, mult_done[par])
                    P.op("scalar", lambda e, o=zb[:, po + ft * TCH: po + (ft + 1) * TCH], i=ps,
                         bcol=bpk[:, bp.col(f"bsw{n}", ft):bp.col(f"bsw{n}", ft) + 1]:
                         e.activation(out=o, in_=i, func=AF.Silu, bias=bcol, scale=1.0), inc=ev)

                for qw in range(2):
                    bpar = qw % 2
                    chunks = []
                    for kc in range(8):
                        ch_ = ws.chunk(WswB, n * E1 + kc * 128, qw * 512, 512,
                                       row1=(n * E1 + (kc + 1) * 128 if kc % 2 == 0 else None))
                        ch_["lcols"] = lambda fl, ft: fl * 128
                        ch_["rhs"] = hn[:, po + kc * TCH: po + kc * TCH + TCH]
                        if kc == 0 and qw == 0:
                            ch_["toks"] = ch_["toks"] + [evt_up]
                        chunks.append(ch_)
                    pre = [wavetok[bpar]] if wavetok[bpar] else [tok(ev)]
                    gemm(P, f"sw{j}{tch}{n}{qw}", chunks=chunks,
                         ftiles=list(range(qw * 4, (qw + 1) * 4)), N=TCH,
                         banks=banks[bpar * 4:(bpar + 1) * 4], evac=ev_sw, pre_toks=pre)
                    wavetok[bpar] = tok(ev)
                evt_sw = tok(ev)
                P.wait_tok("tensor", [evt_gates, tok(ev)])
                P.op("tensor", lambda e, n=n: e.matmul(banks[0][:, 0:TCH],
                                                       sb["selbc_sb"][:, n * 128:(n + 1) * 128],
                                                       w6p[:, 0:TCH], start=True, stop=True),
                     inc=P.ctr("bc"))
                P.wait("scalar", P.ctr("bc"))
                P.op("scalar", lambda e, par=par: e.activation(
                     out=wbsb[:, par * TCH:(par + 1) * TCH], in_=banks[0][:, 0:TCH],
                     func=AF.Copy, bias=0.0, scale=1.0), inc=ev)
                evt_wb = tok(ev)
                wavetok[0] = evt_wb
                P.wait_tok("vector", [evt_sw, evt_wb])
                for ft in range(8):
                    gf = n * 8 + ft
                    P.op("vector", lambda e, a=zb[:, po + ft * TCH: po + (ft + 1) * TCH],
                         b2=hn[:, po + ft * TCH: po + (ft + 1) * TCH]:
                         e.tensor_mul(a, a, b2))
                    P.op("vector", lambda e, a=zb[:, po + ft * TCH: po + (ft + 1) * TCH],
                         o=HH[:, gf * TCH: gf * TCH + TCH], par=par:
                         e.tensor_mul(o, a, wbsb[:, par * TCH:(par + 1) * TCH]), inc=vv)
                mult_done[par] = vv.val
                if j == 0 and tch == 0 and n == 0:
                    dump(4, hn[:, 0:512])
                    dump(5, zb[:, 0:512])
                    dump(6, wbsb[:, 0:512])
                    dump(7, HH[:, 0:512])
            evt_hh = tok(vv)

            lnd = P.ctr("ln")
            x2tok0 = tok(P.ctr("x2c"))
            for qw in range(4):
                par = qw % 2
                fts = list(range(qw * 4, (qw + 1) * 4))
                chunks = []
                for n in range(NEXP1):
                    for e_ in range(8):
                        ch_ = ws.chunk(WdnB, n * E1 + e_ * 128, qw * 512, 512,
                                   row1=(n * E1 + (e_ + 1) * 128 if e_ % 2 == 0 else None))
                        ch_["lcols"] = lambda fl, ft: fl * 128
                        ch_["rhs"] = HH[:, (n * 8 + e_) * TCH:(n * 8 + e_) * TCH + TCH]
                        if n == 0 and e_ == 0 and qw == 0:
                            ch_["toks"] = ch_["toks"] + [evt_hh]
                        chunks.append(ch_)
                for kc in range(16):
                    ch_ = ws.chunk(WsdB, kc * 128, qw * 512, 512,
                                   row1=((kc + 1) * 128 if kc % 2 == 0 else None))
                    ch_["lcols"] = lambda fl, ft: fl * 128
                    ch_["rhs"] = hsh[:, kc * TCH: kc * TCH + TCH]
                    if kc == 0 and qw == 0:
                        ch_["toks"] = ch_["toks"] + [evt_hsh]
                    chunks.append(ch_)
                chunks.append({"lhsT": (lambda fl, ft: bdnp[:, ft * 128: ft * 128 + 128]),
                               "rhs": w6p[:, 0:TCH],
                               "toks": ([evt_gates] if qw == 0 else [])})

                def ev_dn(fl, ft, ps, j=j, c0=c0):
                    par = (ft % 2) * TCH
                    P.wait("scalar", x2c, max(0, x2c.val - 1))
                    P.op("scalar", lambda e, o=tmpf[:, par:par + TCH], i=ps,
                         bcol=bpk[:, bp.col("bsd", ft):bp.col("bsd", ft) + 1]:
                         e.activation(out=o, in_=i, func=AF.Identity, bias=bcol, scale=1.0), inc=ev)
                    buf, toks = x2_stage(ft, c0)
                    P.wait_tok("vector", [tok(ev)] + toks)
                    if j == 0:
                        P.op("vector", lambda e, s2=buf, o=rbuf[:, ft * TCH:(ft + 1) * TCH]:
                             e.tensor_add(o, tmpf[:, par:par + TCH], s2[:, 0:TCH]), inc=x2c)
                        P.op("vector", lambda e, o=xsl[:, ft * TCH:(ft + 1) * TCH],
                             i=rbuf[:, ft * TCH:(ft + 1) * TCH]:
                             e.tensor_copy(out=o, in_=i), inc=vv)
                        P.wait("gpsimd", vv)
                        emit_dma(P, "gpsimd", xtT[ft * 128:(ft + 1) * 128, c0:c0 + TCH],
                                 xsl[:, ft * TCH:(ft + 1) * TCH], st)
                    else:
                        P.op("vector", lambda e, s2=buf, o=rbuf[:, ft * TCH:(ft + 1) * TCH]:
                             e.scalar_tensor_tensor(out=o, in0=s2[:, 0:TCH], scalar=2.0,
                                                    in1=tmpf[:, par:par + TCH],
                                                    op0=ALU.mult, op1=ALU.add), inc=x2c)

                pre = ([wavetok[par]] if wavetok[par] else [tok(ev)]) + [x2tok0]
                gemm(P, f"dn{j}{tch}{qw}", chunks=chunks, ftiles=fts, N=TCH,
                     banks=banks[par * 4:(par + 1) * 4], evac=ev_dn, pre_toks=pre)
                wavetok[par] = tok(ev)
            if j == 0 and tch == 0:
                dump(8, xsl[:, 0:512])
                dump(9, xsl[:, 15 * TCH: 15 * TCH + 512])

            if j == 1:
                layer_norm(P, f"p3ln{tch}", rbuf=rbuf, rT=TCH, t0=0, N=TCH,
                           out_fn=lambda ft: rbuf[:, ft * TCH:(ft + 1) * TCH],
                           out_dtype_note="f32", banks=banks[5:8], sb=sb,
                           sq_scratch=sqs, rows=rows, done_ctr=lnd,
                           r_toks=[tok(x2c)],
                           eps_col=bpk[:, bp.col("ln_eps"):bp.col("ln_eps") + 1])
                P.wait("gpsimd", lnd)
                for ft in range(16):
                    emit_dma(P, "gpsimd", outT[ft * 128:(ft + 1) * 128, c0:c0 + TCH],
                             rbuf[:, ft * TCH:(ft + 1) * TCH], st)
            P.barrier()

    P.wait("sync", st)
    P.wait("gpsimd", st)



P3_WEIGHTS = ["Wup2d", "Wsw2d", "Wdn2d", "Wsu", "Wsa", "Wsd", "Wg0", "Wg1", "bdn_pad"]
P3_CONSTS = ["bg0", "bg1", "ident_f", "ones_col_f", "ones_col_bf", "ones_row_f",
             "ones_row_bf", "selbc"]


def build_launch_p3(bp, x2_kind="ExternalInput", dbg=False):
    nc = bass.Bass("TRN2", target_bir_lowering=False)
    ctx = ExitStack()
    P = Prog(nc, ctx)
    with ctx:
        io = {}
        if dbg:
            io["dbg"] = nc.dram_tensor("dbg", [128, 16 * 512], BF16, kind="ExternalOutput").ap()
        io["x2T"] = nc.dram_tensor("x2T", [D, OWN], BF16, kind=x2_kind).ap()
        io["outT"] = nc.dram_tensor("outT", [D, OWN], F32, kind="ExternalOutput").ap()
        io["xtT"] = nc.dram_tensor("xtT", [D, OWN], BF16,
                                   kind="ExternalOutput" if dbg else "Internal").ap()
        wsrc = {}
        for nm, shape in [("Wup2d", [NEXP1 * D, E1]), ("Wsw2d", [NEXP1 * E1, E1]),
                          ("Wdn2d", [NEXP1 * E1, D]), ("Wsu", [D, D]), ("Wsa", [E1, E1]),
                          ("Wsd", [D, D]), ("Wg0", [D, 8]), ("Wg1", [D, 8]),
                          ("bdn_pad", [128, D])]:
            wsrc[nm] = nc.dram_tensor(nm, list(shape), F32, kind="ExternalInput").ap()
        io["WupB"] = nc.dram_tensor("WupB", [NEXP1 * D, E1], BF16, kind="Internal").ap()
        io["WswB"] = nc.dram_tensor("WswB", [NEXP1 * E1, E1], BF16, kind="Internal").ap()
        io["WdnB"] = nc.dram_tensor("WdnB", [NEXP1 * E1, D], BF16, kind="Internal").ap()
        io["WsuB"] = nc.dram_tensor("WsuB", [D, D], BF16, kind="Internal").ap()
        io["WsaB"] = nc.dram_tensor("WsaB", [E1, E1], BF16, kind="Internal").ap()
        io["WsdB"] = nc.dram_tensor("WsdB", [D, D], BF16, kind="Internal").ap()

        sb = load_consts(P, ctx, nc, P3_CONSTS, bp.array_cols)
        banks = [ctx.enter_context(nc.psum_tensor(f"bank{i}", [128, 512], F32)) for i in range(8)]

        sb["bdn_pad_bf"] = ctx.enter_context(nc.sbuf_tensor("bdnp_sb", [128, D], BF16))
        sb["Wg0_bf"] = ctx.enter_context(nc.sbuf_tensor("wg0_sb", [128, 16 * 8], BF16))
        sb["Wg1_bf"] = ctx.enter_context(nc.sbuf_tensor("wg1_sb", [128, 16 * 8], BF16))
        with ExitStack() as wctx:
            smstage = wctx.enter_context(nc.sbuf_tensor("smw_stage", [128, 2048], F32))
            sm = P.ctr("smw")
            smc = P.ctr("smc")
            emit_dma(P, "sync", smstage[:, 0:D], wsrc["bdn_pad"][0:128, 0:D], sm)
            P.wait("vector", sm)
            P.op("vector", lambda e, s2=smstage: e.tensor_copy(out=sb["bdn_pad_bf"][:],
                 in_=s2[:, 0:D]), inc=smc)
            for nm, key in [("Wg0", "Wg0_bf"), ("Wg1", "Wg1_bf")]:
                P.wait("sync", smc)
                for kc in range(16):
                    emit_dma(P, "sync", smstage[:, kc * 8:(kc + 1) * 8],
                             wsrc[nm][kc * 128:(kc + 1) * 128, 0:8], sm)
                P.wait("vector", sm)
                P.op("vector", lambda e, d=sb[key]: e.tensor_copy(out=d[:], in_=smstage[:, 0:128]),
                     inc=smc)

            for e_ in ("tensor", "scalar", "gpsimd"):
                P.wait(e_, smc)
            wcd = P.ctr("wconv_done")
            jobs = [(wsrc["Wup2d"], io["WupB"], NEXP1 * D, E1),
                    (wsrc["Wsw2d"], io["WswB"], NEXP1 * E1, E1),
                    (wsrc["Wdn2d"], io["WdnB"], NEXP1 * E1, D),
                    (wsrc["Wsu"], io["WsuB"], D, D),
                    (wsrc["Wsa"], io["WsaB"], E1, E1),
                    (wsrc["Wsd"], io["WsdB"], D, D)]
            emit_wconv(P, wctx, jobs, wcd)
            P.wait("sync", wcd)

        ws = WStream(P, ctx)
        emit_p3(P, ctx, io, sb, banks, bp, ws)
        P.run()
    return nc




def emit_p2(P, ctx, io, sb, banks, bp, ws):
    nc = P.nc
    dbg = io.get("dbg")
    dbg_ctr = P.ctr("dbg") if dbg is not None else None
    x1T, x2T = io["x1T"], io["x2T"]
    wqB, wkB, wvB, woB = io["wqB"], io["wkB"], io["wvB"], io["woB"]
    bpk = sb["bpack_sb"]

    kT = [ctx.enter_context(nc.sbuf_tensor(f"p2_kT{h}", [128, T1], BF16)) for h in range(HK1)]
    vN = [ctx.enter_context(nc.sbuf_tensor(f"p2_vN{h}", [128, T1], BF16)) for h in range(HK1)]
    xs = ctx.enter_context(nc.sbuf_tensor("p2_xs", [128, 16 * 512], BF16))
    C1d = nc.dram_tensor("C1", [128, T1], F32, kind="ExternalInput").ap()
    S1d = nc.dram_tensor("S1", [128, T1], F32, kind="ExternalInput").ap()
    Cbuf = ctx.enter_context(nc.sbuf_tensor("p2_cb", [128, 512], F32))
    Sbuf = ctx.enter_context(nc.sbuf_tensor("p2_sb2", [128, 512], F32))
    sqln = ctx.enter_context(nc.sbuf_tensor("p2_sqln", [128, 1024], BF16))
    qbf = ctx.enter_context(nc.sbuf_tensor("p2_qbf", [128, 16 * 512], BF16))
    ro_qs = ctx.enter_context(nc.sbuf_tensor("p2_qs", [128, 512], F32))
    ro_t = ctx.enter_context(nc.sbuf_tensor("rt", [128, 512], F32))
    sqs = ctx.enter_context(nc.sbuf_tensor("p2_sq", [128, 512], F32))
    rows = ctx.enter_context(nc.sbuf_tensor("p2_rows", [1, 3 * 512], F32))
    smat = ctx.enter_context(nc.sbuf_tensor("p2_smat", [128, 512], F32))
    cmb = ctx.enter_context(nc.sbuf_tensor("p2_cmb", [128, 1152], F32))
    msc = ctx.enter_context(nc.sbuf_tensor("p2_msc", [128, 2 * 1152], F32))
    pT = ctx.enter_context(nc.sbuf_tensor("p2_pT", [128, 2 * 9 * 128], BF16))
    att = ctx.enter_context(nc.sbuf_tensor("p2_att", [128, 16 * 512], BF16))
    rbuf = ctx.enter_context(nc.sbuf_tensor("p2_rbuf", [128, 16 * 512], BF16))
    kvald = nc.dram_tensor("kval1b", [1, T1], BF16, kind="ExternalInput").ap()
    kval = ctx.enter_context(nc.sbuf_tensor("p2_kval", [1, T1], BF16))
    ncol = ctx.enter_context(nc.sbuf_tensor("p2_ncol", [128, 4], F32))

    xl = ActLoader(P, ctx, "xlb", nslots=2, width=512, dtype=BF16)
    rl = ActLoader(P, ctx, "rlb", nslots=2, width=512, dtype=BF16)
    ev = P.ctr("ev")
    vv = P.ctr("vv")
    st = P.ctr("st")
    kld = P.ctr("p2_kld")
    emit_dma(P, "sync", kval[:], kvald, kld)
    for e_ in ("tensor", "vector", "scalar", "gpsimd"):
        P.wait(e_, kld)

    P.op("gpsimd", lambda e: e.memset(att[:], 0.0))
    csld = [P.ctr("csld0"), P.ctr("csld1")]

    def load_cs(c0):
        P.wait("gpsimd", vv)
        emit_dma(P, "gpsimd", Cbuf[:], C1d[:, c0:c0 + 512], csld[0])
        emit_dma(P, "gpsimd", Sbuf[:], S1d[:, c0:c0 + 512], csld[1])
        toks = [(csld[0], csld[0].val), (csld[1], csld[1].val)]
        P.wait_tok("vector", toks)
        P.wait_tok("scalar", toks)

    for c in range(6):
        c0 = c * 512
        load_cs(c0)
        P.wait("vector", ws.use)
        xt_toks = load_cast_slice(P, xl, x1T, c0, xs, 512, 512)
        for which, wB, dst in (("k", wkB, None), ("v", wvB, None)):
            chunks = []
            for kc in range(16):
                ch_ = ws.chunk(wB, kc * 128, 0, 512,
                               row1=((kc + 1) * 128 if kc % 2 == 0 else None))
                ch_["lcols"] = lambda fl, ft: fl * 128
                ch_["rhs"] = xs[:, kc * 512: kc * 512 + 512]
                if kc == 0:
                    ch_["toks"] = ch_["toks"] + xt_toks
                chunks.append(ch_)

            if which == "k":
                def ev_k2(fl, ft, ps, c0=c0):
                    dst_ap = kT[ft][:, c0:c0 + 512]
                    rope_norm(P, sb, banks, src=ps, Cs=Cbuf[:, 0:512],
                              Ss=Sbuf[:, 0:512], qs=ro_qs, tmp=ro_t, sq=sqs,
                              rows=rows, scalemat=smat, out_bf=dst_ap,
                              wrow=sb["gkrow_sb"], eps_col=None, extra_scale=1.0,
                              half=64, N=512, statbank=banks[5], bcbank=banks[6],
                              vv=vv, ev=ev)

                gemm(P, f"p2k{c}", chunks=chunks, ftiles=list(range(4)), N=512,
                     banks=banks[0:4], evac=ev_k2, evac_eng="vector",
                     pre_toks=[tok(ev), tok(vv)])
            else:
                def ev_v(fl, ft, ps, c0=c0):
                    vtmp = ro_t
                    P.op("scalar", lambda e, o=vtmp[:, 0:512], i=ps:
                         e.activation(out=o, in_=i, func=AF.Identity, bias=0.0, scale=1.0),
                         inc=ev)
                    P.wait_tok("tensor", [tok(ev)])
                    for i4 in range(4):
                        P.op("tensor", lambda e, i4=i4: e.transpose(
                            banks[5][:, i4 * 128:(i4 + 1) * 128],
                            vtmp[:, i4 * 128:(i4 + 1) * 128], sb["ident_f_sb"][:]),
                            inc=(P.ctr("tp") if i4 == 3 else None))
                    P.wait("scalar", P.ctr("tp"))
                    P.op("scalar", lambda e, o=vN[ft][:, c0:c0 + 512], i=banks[5][:, 0:512]:
                         e.activation(out=o, in_=i, func=AF.Identity, bias=0.0, scale=1.0),
                         inc=ev)

                gemm(P, f"p2v{c}", chunks=chunks, ftiles=list(range(4)), N=512,
                     banks=banks[0:4], evac=ev_v, evac_eng="scalar",
                     pre_toks=[tok(ev), tok(vv)])
    P.barrier()

    for sr in range(P2_LIM["sr"]):
        q0 = HALO1 + sr * 512
        load_cs(q0)
        P.wait("vector", ws.use)
        P.wait("vector", P.ctr("sc"))
        P.wait("vector", st)
        P.wait("scalar", ws.use)
        xt_toks = load_cast_slice(P, xl, x1T, q0, xs, 512, 512)
        for qtr in range(4):
            fts = [qtr * 4 + i for i in range(4)]
            chunks = []
            for kc in range(16):
                ch_ = ws.chunk(wqB, kc * 128, qtr * 512, 512,
                               row1=((kc + 1) * 128 if kc % 2 == 0 else None))
                ch_["lcols"] = lambda fl, ft: fl * 128
                ch_["rhs"] = xs[:, kc * 512: kc * 512 + 512]
                if kc == 0:
                    ch_["toks"] = ch_["toks"] + xt_toks
                chunks.append(ch_)

            def ev_q(fl, ft, ps, q0=q0):
                dst = qbf[:, ft * 512:(ft + 1) * 512]
                rope_norm(P, sb, banks, src=ps, Cs=Cbuf[:, 0:512],
                          Ss=Sbuf[:, 0:512], qs=ro_qs, tmp=ro_t, sq=sqs,
                          rows=rows, scalemat=smat, out_bf=dst,
                          wrow=sb["gqrow_sb"], eps_col=None,
                          extra_scale=1.0 / np.sqrt(HD1), half=64, N=512,
                          statbank=banks[5], bcbank=banks[6], vv=vv, ev=ev)

            gemm(P, f"p2q{sr}{qtr}", chunks=chunks, ftiles=fts, N=512,
                 banks=banks[0:4], evac=ev_q, evac_eng="vector",
                 pre_toks=[tok(ev), tok(vv)])
        qtoks = [tok(vv)]

        pend = [None]
        mask_tok = [[tok(vv)]]

        def emit_tail(pd):
            if pd is None:
                return
            h, qt, g, pp = pd["h"], pd["qt"], pd["g"], pd["pp"]
            kh = h // 4
            P.wait_tok("tensor", pd["pn_tok"])
            for i9 in range(9):
                bnk = banks[3 + (i9 % 2)]
                if i9 >= 2:
                    P.wait("tensor", ev, ev.val - 1)
                P.op("tensor", lambda e, i9=i9, bnk=bnk, pp=pp: e.transpose(
                    bnk[:, 0:128], msc[:, pp * 1152 + i9 * 128: pp * 1152 + (i9 + 1) * 128],
                    sb["ident_f_sb"][:]), inc=P.ctr("tp2"))
                P.wait("scalar", P.ctr("tp2"))
                P.op("scalar", lambda e, i9=i9, bnk=bnk, pp=pp: e.activation(
                    out=pT[:, pp * 1152 + i9 * 128: pp * 1152 + (i9 + 1) * 128],
                    in_=bnk[:, 0:128], func=AF.Identity, bias=0.0, scale=1.0), inc=ev)
            P.wait_tok("tensor", [tok(ev)])
            ktile0 = g // 128
            for i9 in range(9):
                P.op("tensor", lambda e, i9=i9, kh=kh, ktile0=ktile0, pp=pp: e.matmul(
                    banks[5][:, 0:128],
                    vN[kh][:, (ktile0 + i9) * 128: (ktile0 + i9 + 1) * 128],
                    pT[:, pp * 1152 + i9 * 128: pp * 1152 + (i9 + 1) * 128],
                    start=(i9 == 0), stop=(i9 == 8)),
                    inc=(P.ctr("om") if i9 == 8 else None))
            P.wait("scalar", P.ctr("om"))
            P.op("scalar", lambda e, h=h, qt=qt: e.activation(
                out=att[:, h * 512 + qt * 128: h * 512 + qt * 128 + 128],
                in_=banks[5][:, 0:128], func=AF.Identity, bias=0.0, scale=1.0), inc=ev)

        it = [0]
        mask_tok[0] = [tok(vv)]
        for qt in range(P2_LIM["qt"]):
            g = sr * 512 + qt * 128
            segs = [(0, 512), (512, 1024), (1024, 1152)]
            P.wait_tok("tensor", [tok(vv), tok(ev)])
            for si, (a, b2) in enumerate(segs):
                if si > 0:
                    P.wait("tensor", vv)
                P.op("tensor", lambda e, si=si, a=a, b2=b2, g=g: e.matmul(
                    banks[6][:, 0:b2 - a],
                    sb["ones_row_bf_sb"][0:1, :], kval[0:1, g + a:g + b2],
                    start=True, stop=True), inc=P.ctr("bc"))
                P.wait("vector", P.ctr("bc"))
                P.op("vector", lambda e, si=si, a=a, b2=b2: e.tensor_add(
                    cmb[:, a:b2], sb["genmask1_sb"][:, a:b2], banks[6][:, 0:b2 - a]), inc=vv)
            for h in range(P2_LIM["h"]):
                kh = h // 4
                pp = it[0] % 2
                P.wait_tok("tensor", mask_tok[0])
                for si, (a, b2) in enumerate(segs):
                    P.op("tensor", lambda e, si=si, a=a, b2=b2, h=h, kh=kh, qt=qt, g=g:
                        e.matmul(
                            banks[si][:, 0:b2 - a],
                            qbf[:, h * 512 + qt * 128: h * 512 + qt * 128 + 128],
                            kT[kh][:, g + a: g + b2], start=True, stop=True),
                        inc=(P.ctr("sc") if si == 2 else None))
                P.wait("vector", P.ctr("sc"))
                for si, (a, b2) in enumerate(segs):
                    P.op("vector", lambda e, si=si, a=a, b2=b2, pp=pp: e.tensor_add(
                        msc[:, pp * 1152 + a: pp * 1152 + b2], banks[si][:, 0:b2 - a],
                        cmb[:, a:b2]), inc=(vv if si == 2 else None))
                mask_tok[0] = [tok(vv)]
                P.op("vector", lambda e, pp=pp: e.reduce_max(ncol[:, pp * 2: pp * 2 + 1],
                     msc[:, pp * 1152:(pp + 1) * 1152], axis=AX.X, negate=True))
                P.wait("scalar", P.ctr("vector_ser"))
                P.op("scalar", lambda e, pp=pp: e.activation(
                     out=msc[:, pp * 1152:(pp + 1) * 1152],
                     in_=msc[:, pp * 1152:(pp + 1) * 1152], func=AF.Exp,
                     bias=ncol[:, pp * 2: pp * 2 + 1], scale=1.0,
                     accum_out=ncol[:, pp * 2 + 1: pp * 2 + 2]), inc=ev)
                P.wait("vector", ev)
                P.op("vector", lambda e, pp=pp: e.reciprocal(
                     ncol[:, pp * 2 + 1: pp * 2 + 2], ncol[:, pp * 2 + 1: pp * 2 + 2]))
                P.op("vector", lambda e, pp=pp: e.tensor_scalar(
                     out=msc[:, pp * 1152:(pp + 1) * 1152],
                     in0=msc[:, pp * 1152:(pp + 1) * 1152],
                     scalar1=ncol[:, pp * 2 + 1: pp * 2 + 2], scalar2=None,
                     op0=ALU.mult), inc=vv)
                pn_tok = [tok(vv)]
                emit_tail(pend[0])
                pend[0] = {"h": h, "qt": qt, "g": g, "pp": pp, "pn_tok": pn_tok}
                it[0] += 1
        emit_tail(pend[0])
        pend[0] = None

        lnd = P.ctr("ln")
        wavetok = [None, None]
        pre0 = [tok(ev), tok(vv), tok(rl.cons)]
        for qw in range(4):
            par = qw % 2
            fts = list(range(qw * 4, (qw + 1) * 4))
            chunks = []
            for kc in range(16):
                ch_ = ws.chunk(woB, kc * 128, qw * 512, 512,
                               row1=((kc + 1) * 128 if kc % 2 == 0 else None))
                ch_["lcols"] = lambda fl, ft: fl * 128
                ch_["rhs"] = att[:, kc * 512: kc * 512 + 512]
                if kc == 0 and qw == 0:
                    ch_["toks"] = ch_["toks"] + [tok(ev)]
                chunks.append(ch_)

            def ev_o(fl, ft, ps, sr=sr):
                buf, toks = rl.load(x1T, ft * 128, HALO1 + sr * 512, 512)
                P.wait_tok("vector", [tok(ev)] + toks)
                P.op("vector", lambda e, b2=buf, o=rbuf[:, ft * 512:(ft + 1) * 512], i=ps:
                     e.tensor_add(o, i, b2[:, 0:512]), inc=rl.cons)

            pre = [wavetok[par]] if wavetok[par] else pre0
            gemm(P, f"p2o{sr}{qw}", chunks=chunks, ftiles=fts, N=512,
                 banks=banks[par * 4:(par + 1) * 4], evac=ev_o, evac_eng="vector",
                 pre_toks=pre)
            wavetok[par] = tok(rl.cons)
        layer_norm(P, f"p2ln{sr}", rbuf=rbuf, rT=512, t0=0, N=512,
                   out_fn=lambda ft: rbuf[:, ft * 512:(ft + 1) * 512],
                   out_dtype_note="bf16", banks=banks[5:8], sb=sb,
                   sq_scratch=sqln, rows=rows, done_ctr=lnd, ones_key="ones_col_bf_sb",
                   r_toks=[tok(rl.cons)],
                   eps_col=bpk[:, bp.col("ln_eps"):bp.col("ln_eps") + 1])
        P.wait("gpsimd", lnd)
        P.wait("vector", st)
        P.op("vector", lambda e: e.nop(), inc=vv)
        P.wait("gpsimd", vv)
        for ft in range(16):
            emit_dma(P, "gpsimd", x2T[ft * 128:(ft + 1) * 128, sr * 512:(sr + 1) * 512],
                     rbuf[:, ft * 512:(ft + 1) * 512], st)
        P.barrier()

    P.wait("sync", st)
    P.wait("gpsimd", st)


P2_LIM = {"sr": 4, "qt": 4, "h": 16}

P2_CONSTS = ["ident_f", "ones_col_f", "ones_col_bf", "ones_row_f", "ones_row_bf",
             "genmask1", "gqrow", "gkrow"]


def build_launch_p2(bp, x1_kind="ExternalInput", dbg=False):
    nc = bass.Bass("TRN2", target_bir_lowering=False)
    ctx = ExitStack()
    P = Prog(nc, ctx)
    with ctx:
        io = {}
        if dbg:
            io["dbg"] = nc.dram_tensor("dbg", [2048, 512], BF16, kind="ExternalOutput").ap()
        io["x1T"] = nc.dram_tensor("x1T", [D, T1], BF16, kind=x1_kind).ap()
        io["x2T"] = nc.dram_tensor("x2T", [D, OWN], BF16, kind="ExternalOutput").ap()
        wsrc = {}
        for nm, shape in [("wqT", [D, D]), ("wkT", [D, 512]), ("wvT", [D, 512]),
                          ("woT", [D, D])]:
            wsrc[nm] = nc.dram_tensor(nm, list(shape), F32, kind="ExternalInput").ap()
        io["wqB"] = nc.dram_tensor("wqB", [D, D], BF16, kind="Internal").ap()
        io["wkB"] = nc.dram_tensor("wkB", [D, 512], BF16, kind="Internal").ap()
        io["wvB"] = nc.dram_tensor("wvB", [D, 512], BF16, kind="Internal").ap()
        io["woB"] = nc.dram_tensor("woB", [D, D], BF16, kind="Internal").ap()

        sb = load_consts(P, ctx, nc, P2_CONSTS, bp.array_cols)
        banks = [ctx.enter_context(nc.psum_tensor(f"bank{i}", [128, 512], F32)) for i in range(8)]

        with ExitStack() as wctx:
            wcd = P.ctr("wconv_done")
            jobs = [(wsrc["wqT"], io["wqB"], D, 1024), (wsrc["wkT"], io["wkB"], D, 512),
                    (wsrc["wvT"], io["wvB"], D, 512), (wsrc["woT"], io["woB"], D, 1024)]
            jobs = [(wsrc["wqT"], io["wqB"], D, 2048), (wsrc["wkT"], io["wkB"], D, 512),
                    (wsrc["wvT"], io["wvB"], D, 512), (wsrc["woT"], io["woB"], D, 2048)]
            emit_wconv(P, wctx, jobs, wcd)
            P.wait("sync", wcd)

        ws = WStream(P, ctx)
        emit_p2(P, ctx, io, sb, banks, bp, ws)
        P.run()
    return nc




P1_LIM = {"sr": NSR, "h": H0}

A_SMALL = [
    ("kvc_w", D, 128), ("kvc_wz", D, 128), ("idx_pack", D, 128),
    ("k_proj", D, 128), ("v_proj", D, 128), ("w_dq", D, 64), ("w_iuq", 64, 256),
    ("w_w", D, 4), ("W_routerT", D, 8), ("A_allT", D, 64), ("B_allT_pad", 128, D),
]


def emit_p0p1(P, ctx, io, sb, banks, bp, ws, wsm):
    nc = P.nc
    hT_full, hT_loc, x1T = io["hT_full"], io["hT_loc"], io["x1T"]
    qcB, finB, wbB, gwB, qupB = io["qcB"], io["finB"], io["wbB"], io["gwB"], io["qupB"]
    bpk = sb["bpack_sb"]
    rms_col = bpk[:, bp.col("rms_eps"):bp.col("rms_eps") + 1]
    ln_col = bpk[:, bp.col("ln_eps"):bp.col("ln_eps") + 1]

    swkT = ctx.enter_context(nc.sbuf_tensor("p0_swkT", [128, TLOCP], BF16))
    vN0 = ctx.enter_context(nc.sbuf_tensor("p0_vN", [128, TLOCP], BF16))
    ckvnT = ctx.enter_context(nc.sbuf_tensor("p0_ckvnT", [128, NBL], BF16))
    ckvnN = ctx.enter_context(nc.sbuf_tensor("p0_ckvnN", [128, NBL], BF16))
    kcompT = ctx.enter_context(nc.sbuf_tensor("p0_kcompT", [128, NBL], BF16))
    hbf = ctx.enter_context(nc.sbuf_tensor("a_hbf", [128, 16 * 512], BF16))
    zbuf = ctx.enter_context(nc.sbuf_tensor("a_z", [128, 512], F32))
    ebuf = ctx.enter_context(nc.sbuf_tensor("a_e", [128, 512], F32))
    wbuf = ctx.enter_context(nc.sbuf_tensor("a_w", [128, 512], F32))
    segb = ctx.enter_context(nc.sbuf_tensor("a_seg", [128, 64], F32))

    qs0 = ctx.enter_context(nc.sbuf_tensor("a_qs0", [128, 512], F32))
    rot0 = ctx.enter_context(nc.sbuf_tensor("a_rot0", [128, 512], F32))
    sq0 = ctx.enter_context(nc.sbuf_tensor("a_sq0", [128, 512], F32))
    smat0 = ctx.enter_context(nc.sbuf_tensor("a_smat0", [128, 512], F32))
    rows = ctx.enter_context(nc.sbuf_tensor("a_rows", [1, 3 * 512], F32))
    Cb = ctx.enter_context(nc.sbuf_tensor("a_cb", [128, 512], F32))
    Sb = ctx.enter_context(nc.sbuf_tensor("a_sb", [128, 512], F32))
    C0d = nc.dram_tensor("C0", [128, TLOCP], F32, kind="ExternalInput").ap()
    S0d = nc.dram_tensor("S0", [128, TLOCP], F32, kind="ExternalInput").ap()
    kval0d = nc.dram_tensor("kval0b", [1, TLOCP], BF16, kind="ExternalInput").ap()
    kval0 = ctx.enter_context(nc.sbuf_tensor("a_kval0", [1, 384], BF16))
    kvld = P.ctr("kvld")
    tposd = nc.dram_tensor("tpos_cols", [128, NSR * 4], F32, kind="ExternalInput").ap()
    tpos = ctx.enter_context(nc.sbuf_tensor("a_tpos", [128, NSR * 4], F32))

    qcT = ctx.enter_context(nc.sbuf_tensor("a_qcT", [128, 4 * 512], BF16))
    dqT = ctx.enter_context(nc.sbuf_tensor("a_dqT", [128, 512], BF16))
    qiT = ctx.enter_context(nc.sbuf_tensor("a_qiT", [128, 4 * 512], BF16))
    wih = ctx.enter_context(nc.sbuf_tensor("wih", [128, 16], F32))
    isc = zbuf
    wkb = ctx.enter_context(nc.sbuf_tensor("a_wkb", [128, 512], F32))
    mbuf = ctx.enter_context(nc.sbuf_tensor("a_mbuf", [128, 512], F32))
    m8 = ctx.enter_context(nc.sbuf_tensor("a_m8", [128, 8], F32))
    selb = ctx.enter_context(nc.sbuf_tensor("a_selb", [128, 4 * 512], BF16))
    avb = ctx.enter_context(nc.sbuf_tensor("a_avb", [128, 4], F32))
    cmb0 = ctx.enter_context(nc.sbuf_tensor("a_cmb0", [128, 4 * 384], BF16))
    qbf = ctx.enter_context(nc.sbuf_tensor("a_qbf", [128, 512], BF16))
    msc0 = ctx.enter_context(nc.sbuf_tensor("a_msc0p", [128, 2 * 512], F32))
    pT0 = ctx.enter_context(nc.sbuf_tensor("a_pT0", [128, 2 * 512], BF16))
    msc1 = ctx.enter_context(nc.sbuf_tensor("a_msc1", [128, 2 * 384], F32))
    pT1 = ctx.enter_context(nc.sbuf_tensor("a_pT1", [128, 2 * 384], BF16))
    ncol = ctx.enter_context(nc.sbuf_tensor("a_ncol", [128, 8], F32))
    att0 = ctx.enter_context(nc.sbuf_tensor("a_att0", [128, 16 * 512], BF16))
    grp = ctx.enter_context(nc.sbuf_tensor("a_grp", [128, 16 * 512], BF16))
    rbf = ctx.enter_context(nc.sbuf_tensor("a_rbf", [128, 16 * 512], BF16))
    sqln = ctx.enter_context(nc.sbuf_tensor("a_sqln", [128, 2 * 512], BF16))
    glog = ctx.enter_context(nc.sbuf_tensor("a_glog", [128, 4 * 8], F32))
    gsc = ctx.enter_context(nc.sbuf_tensor("a_gsc", [128, 8], F32))
    gcol = ctx.enter_context(nc.sbuf_tensor("a_gcol", [128, 4], F32))
    hallf = ctx.enter_context(nc.sbuf_tensor("a_hallf", [128, 64], F32))
    hprT = ctx.enter_context(nc.sbuf_tensor("a_hprT", [128, 512], BF16))
    tmpf = ctx.enter_context(nc.sbuf_tensor("a_tmpf", [128, 2 * 512], F32))

    ckvf = msc0
    kcof = tmpf
    if wsm is not None:
        att0f = att0[:].bitcast(F32)
        rbff = rbf[:].bitcast(F32)
        wsm = WconvPump(P, wsm,
                        [att0f[:, i * 1024:(i + 1) * 1024] for i in range(4)]
                        + [rbff[:, i * 1024:(i + 1) * 1024] for i in range(4)],
                        [grp[:, i * 1024:(i + 1) * 1024] for i in range(8)])
    xl = ActLoader(P, ctx, "xl", nslots=2, width=512)
    rl = ActLoader(P, ctx, "rl", nslots=2, width=512)
    ev = P.ctr("ev")
    vv = P.ctr("vv")
    st = P.ctr("st")
    cld = P.ctr("cld")

    P.op("gpsimd", lambda e: e.memset(qs0[:], 0.0))
    P.op("gpsimd", lambda e: e.memset(hprT[:], 0.0))
    emit_dma(P, "sync", tpos[:], tposd, cld)
    for e_ in ("tensor", "vector", "scalar", "gpsimd"):
        P.wait(e_, cld)

    csld = [P.ctr("csld0"), P.ctr("csld1")]

    def load_cs0(c0):
        P.wait("gpsimd", vv)
        emit_dma(P, "gpsimd", Cb[:], C0d[:, c0:c0 + 512], csld[0])
        emit_dma(P, "gpsimd", Sb[:], S0d[:, c0:c0 + 512], csld[1])
        toks = [(csld[0], csld[0].val), (csld[1], csld[1].val)]
        P.wait_tok("vector", toks)
        P.wait_tok("scalar", toks)

    def rmsnorm_from_psum(ps, wrow, out_ap, N, extra=np.sqrt(128.0), pre=()):
        P.wait("scalar", P.ctr("rn"))
        P.wait("vector", P.ctr("rn"))
        P.wait_tok("scalar", list(pre))
        P.wait_tok("vector", list(pre))
        P.op("scalar", lambda e: e.activation(out=sq0[:, 0:N], in_=ps, func=AF.Square,
                                              bias=0.0, scale=1.0), inc=ev)
        P.wait_tok("tensor", [tok(ev)])
        P.op("tensor", lambda e: e.matmul(banks[6][0:1, 0:N], sb["ones_col_f_sb"][:, 0:1],
                                          sq0[:, 0:N], start=True, stop=True), inc=P.ctr("rn"))
        P.wait("scalar", P.ctr("rn"))
        P.op("scalar", lambda e: e.activation(out=rows[0:1, 0:N], in_=banks[6][0:1, 0:N],
                                              func=AF.Sqrt, bias=rms_col[0:1, 0:1],
                                              scale=1.0), inc=ev)
        P.wait("vector", ev)
        P.op("vector", lambda e: e.reciprocal(rows[0:1, 0:N], rows[0:1, 0:N]))
        if extra != 1.0:
            P.op("vector", lambda e: e.tensor_scalar_mul(rows[0:1, 0:N], rows[0:1, 0:N],
                                                         float(extra)))
        P.op("vector", lambda e: e.nop(), inc=vv)
        P.wait_tok("tensor", [tok(vv)])
        P.op("tensor", lambda e: e.matmul(banks[7][:, 0:N], wrow[0:1, :], rows[0:1, 0:N],
                                          start=True, stop=True), inc=P.ctr("rn"))
        P.wait("scalar", P.ctr("rn"))
        P.op("scalar", lambda e: e.activation(out=smat0[:, 0:N], in_=banks[7][:, 0:N],
                                              func=AF.Identity, bias=0.0, scale=1.0), inc=ev)
        P.wait("vector", ev)
        P.op("vector", lambda e, o=out_ap: e.tensor_mul(o, ps, smat0[:, 0:N]), inc=vv)

    for c in range(16):
        c0 = c * 512
        if wsm is not None:
            wsm.pump(23)
        P.wait_tok("vector", [tok(P.ctr("gmm"))])
        toks = load_cast_slice(P, xl, hT_full, c0, hbf, 512, 512)
        chunks = []
        for kc in range(16):
            chunks.append({"lhsT": (lambda fl, ft, kc=kc:
                           [sb["kvc_w_bf"], sb["kvc_wz_bf"], sb["idx_pack_bf"]][ft]
                           [:, kc * 128:(kc + 1) * 128]),
                           "rhs": hbf[:, kc * 512: kc * 512 + 512],
                           "toks": toks if kc == 0 else ()})
        cm = gemm(P, f"p0a{c}", chunks=chunks, ftiles=[0, 1, 2], N=512,
                  banks=banks[0:3], evac=None, pre_toks=[tok(ev), tok(vv)])
        P.wait_tok("vector", [cm])
        P.op("vector", lambda e: e.tensor_add(zbuf[:], banks[1][:, 0:512],
                                              sb["kvc_bias_rep_sb"][:]), inc=vv)
        P.wait("scalar", vv)
        P.op("scalar", lambda e: e.activation(out=ebuf[:], in_=zbuf[:], func=AF.Exp,
                                              bias=0.0, scale=1.0), inc=ev)
        P.wait("vector", ev)
        P.op("vector", lambda e: e.tensor_mul(wbuf[:], ebuf[:], banks[0][:, 0:512]))
        P.op("vector", lambda e: e.reduce_sum(segb[:, 0:32],
                                              wbuf[:].rearrange("p (a b) -> p a b", b=16),
                                              axis=AX.X))
        P.op("vector", lambda e: e.reduce_sum(segb[:, 32:64],
                                              ebuf[:].rearrange("p (a b) -> p a b", b=16),
                                              axis=AX.X))
        P.op("vector", lambda e: e.reciprocal(segb[:, 32:64], segb[:, 32:64]))
        P.op("vector", lambda e, c=c: e.tensor_mul(ckvf[:, c * 32:(c + 1) * 32],
                                                   segb[:, 0:32], segb[:, 32:64]))
        P.op("vector", lambda e: e.tensor_add(zbuf[0:64, :], banks[2][64:128, 0:512],
                                              sb["idx_bias_rep_sb"][64:128, :]), inc=vv)
        P.wait("scalar", vv)
        P.op("scalar", lambda e: e.activation(out=ebuf[0:64, :], in_=zbuf[0:64, :],
                                              func=AF.Exp, bias=0.0, scale=1.0), inc=ev)
        P.wait("vector", ev)
        P.op("vector", lambda e: e.tensor_mul(wbuf[0:64, :], ebuf[0:64, :],
                                              banks[2][0:64, 0:512]))
        P.op("vector", lambda e: e.reduce_sum(segb[0:64, 0:32],
                                              wbuf[0:64, :].rearrange("p (a b) -> p a b", b=16),
                                              axis=AX.X))
        P.op("vector", lambda e: e.reduce_sum(segb[0:64, 32:64],
                                              ebuf[0:64, :].rearrange("p (a b) -> p a b", b=16),
                                              axis=AX.X))
        P.op("vector", lambda e: e.reciprocal(segb[0:64, 32:64], segb[0:64, 32:64]))
        P.op("vector", lambda e, c=c: e.tensor_mul(kcof[0:64, c * 32:(c + 1) * 32],
                                                   segb[0:64, 0:32], segb[0:64, 32:64]),
             inc=vv)
    rmsnorm_from_psum(ckvf[:, 0:NBL], sb["knrow_sb"], zbuf[:, 0:NBL], NBL, pre=[tok(vv)])
    P.op("vector", lambda e: e.tensor_copy(out=ckvnT[:], in_=zbuf[:, 0:NBL]), inc=vv)
    P.wait_tok("tensor", [tok(vv)])
    for i4 in range(4):
        P.op("tensor", lambda e, i4=i4: e.transpose(banks[0][:, i4 * 128:(i4 + 1) * 128],
                                                    zbuf[:, i4 * 128:(i4 + 1) * 128],
                                                    sb["ident_f_sb"][:]),
             inc=(P.ctr("tp") if i4 == 3 else None))
    P.wait("scalar", P.ctr("tp"))
    P.op("scalar", lambda e: e.activation(out=ckvnN[:, 0:512], in_=banks[0][:, 0:512],
                                          func=AF.Identity, bias=0.0, scale=1.0), inc=ev)
    P.op("vector", lambda e: e.tensor_copy(out=kcompT[0:64, :], in_=kcof[0:64, 0:NBL]), inc=vv)

    for c in range(7):
        c0 = c * 512
        if wsm is not None:
            wsm.pump(22)
        load_cs0(c0)
        P.wait_tok("vector", [tok(P.ctr("gmm"))])
        toks = load_cast_slice(P, xl, hT_loc, c0, hbf, 512, 512)
        chunks = []
        for kc in range(16):
            chunks.append({"lhsT": (lambda fl, ft, kc=kc:
                           [sb["k_proj_bf"], sb["v_proj_bf"]][ft][:, kc * 128:(kc + 1) * 128]),
                           "rhs": hbf[:, kc * 512: kc * 512 + 512],
                           "toks": toks if kc == 0 else ()})
        cm = gemm(P, f"p0b{c}", chunks=chunks, ftiles=[0, 1], N=512,
                  banks=banks[0:2], evac=None, pre_toks=[tok(ev), tok(vv)])
        P.wait_tok("vector", [cm])
        rmsnorm_from_psum(banks[0][:, 0:512], sb["knrow_sb"], rot0[:, 0:512], 512, pre=[cm])
        P.wait("scalar", vv)
        P.op("scalar", lambda e: e.activation(out=qs0[0:32, 0:512], in_=rot0[32:64, 0:512],
                                              func=AF.Identity, bias=0.0, scale=1.0))
        P.op("scalar", lambda e: e.activation(out=qs0[32:64, 0:512], in_=rot0[0:32, 0:512],
                                              func=AF.Identity, bias=0.0, scale=1.0), inc=ev)
        P.wait("vector", ev)
        P.op("vector", lambda e: e.tensor_mul(qs0[0:64, 0:512], qs0[0:64, 0:512],
                                              Sb[0:64, 0:512]))
        P.op("vector", lambda e: e.tensor_mul(zbuf[:, 0:512], rot0[:, 0:512], Cb[:, 0:512]))
        P.op("vector", lambda e, c0=c0: e.tensor_add(swkT[:, c0:c0 + 512], zbuf[:, 0:512],
                                                     qs0[:, 0:512]), inc=vv)
        P.op("scalar", lambda e: e.activation(out=wbuf[:], in_=banks[1][:, 0:512],
                                              func=AF.Identity, bias=0.0, scale=1.0), inc=ev)
        P.wait_tok("tensor", [tok(ev)])
        for i4 in range(4):
            P.op("tensor", lambda e, i4=i4: e.transpose(banks[2][:, i4 * 128:(i4 + 1) * 128],
                                                        wbuf[:, i4 * 128:(i4 + 1) * 128],
                                                        sb["ident_f_sb"][:]),
                 inc=(P.ctr("tp") if i4 == 3 else None))
        P.wait("scalar", P.ctr("tp"))
        P.op("scalar", lambda e, c0=c0: e.activation(out=vN0[:, c0:c0 + 512],
                                                     in_=banks[2][:, 0:512],
                                                     func=AF.Identity, bias=0.0, scale=1.0),
             inc=ev)
    P.barrier()

    if wsm is not None:
        wsm.finish()

    for r in range(P1_LIM["sr"]):
        g1 = r * 512
        hcol = HALO0 + g1
        load_cs0(hcol)
        P.wait("vector", ws.use)
        P.wait("vector", st)
        toks = load_cast_slice(P, xl, hT_loc, hcol, hbf, 512, 512)

        chunks = []
        for kc in range(16):
            ch_ = ws.chunk(qcB, kc * 128, 0, 512,
                           row1=((kc + 1) * 128 if kc % 2 == 0 else None))
            ch_["lcols"] = lambda fl, ft: fl * 128
            ch_["rhs"] = hbf[:, kc * 512: kc * 512 + 512]
            if kc == 0:
                ch_["toks"] = ch_["toks"] + toks
            chunks.append(ch_)

        def ev_qc(fl, ft, ps):
            P.op("scalar", lambda e, o=qcT[:, ft * 512:(ft + 1) * 512], i=ps:
                 e.activation(out=o, in_=i, func=AF.Identity, bias=0.0, scale=1.0), inc=ev)

        gemm(P, f"aqc{r}", chunks=chunks, ftiles=[0, 1, 2, 3], N=512,
             banks=banks[0:4], evac=ev_qc, pre_toks=[tok(ev), tok(vv)])

        P.wait_tok("tensor", [tok(ev), tok(vv)])
        for kc in range(16):
            P.op("tensor", lambda e, kc=kc: e.matmul(banks[4][0:64, 0:512],
                 sb["w_dq_bf"][:, kc * 64:(kc + 1) * 64],
                 hbf[:, kc * 512: kc * 512 + 512],
                 start=(kc == 0), stop=(kc == 15)),
                 inc=(P.ctr("gmm") if kc == 15 else None))
        P.wait_tok("scalar", [tok(P.ctr("gmm"))])
        P.op("scalar", lambda e: e.activation(out=dqT[0:64, 0:512], in_=banks[4][0:64, 0:512],
                                              func=AF.Identity, bias=0.0, scale=1.0), inc=ev)
        P.wait_tok("tensor", [tok(ev)])
        for ih in range(4):
            P.wait("tensor", ev)
            P.op("tensor", lambda e, ih=ih: e.matmul(banks[5][0:64, 0:512],
                 sb["w_iuq_bf"][0:64, ih * 64:(ih + 1) * 64], dqT[0:64, 0:512],
                 start=True, stop=True), inc=P.ctr("qi"))
            P.wait("scalar", P.ctr("qi"))
            P.op("scalar", lambda e, ih=ih: e.activation(out=qiT[0:64, ih * 512:(ih + 1) * 512],
                 in_=banks[5][0:64, 0:512], func=AF.Identity, bias=0.0, scale=1.0), inc=ev)

        for tt in range(4):
            P.wait("tensor", ev)
            for kc in range(16):
                P.op("tensor", lambda e, tt=tt, kc=kc: e.matmul(banks[6][:, 0:4],
                     hbf[:, kc * 512 + tt * 128: kc * 512 + tt * 128 + 128],
                     sb["w_w_bf"][:, kc * 4:(kc + 1) * 4],
                     start=(kc == 0), stop=(kc == 15)),
                     inc=(P.ctr("wih") if kc == 15 else None))
            P.wait("scalar", P.ctr("wih"))
            P.op("scalar", lambda e, tt=tt: e.activation(out=wih[:, tt * 4:(tt + 1) * 4],
                 in_=banks[6][:, 0:4], func=AF.Identity, bias=0.0, scale=1.0), inc=ev)

        for qt in range(4):
            tcol = tpos[:, r * 4 + qt: r * 4 + qt + 1]
            P.wait_tok("tensor", [tok(ev), tok(vv)])
            for ih in range(4):
                P.wait("tensor", ev)
                P.op("tensor", lambda e, ih=ih, qt=qt: e.matmul(banks[7][:, 0:512],
                     qiT[0:64, ih * 512 + qt * 128: ih * 512 + qt * 128 + 128],
                     kcompT[0:64, 0:NBL], start=True, stop=True), inc=P.ctr("hs"))
                P.wait("scalar", P.ctr("hs"))
                P.wait("scalar", vv)
                P.op("scalar", lambda e: e.activation(out=ebuf[:], in_=banks[7][:, 0:512],
                     func=AF.Relu, bias=0.0, scale=1.0), inc=ev)
                P.wait("vector", ev)
                if ih == 0:
                    P.op("vector", lambda e, ih=ih, qt=qt: e.tensor_scalar(
                        out=isc[:], in0=ebuf[:], scalar1=wih[:, qt * 4 + ih: qt * 4 + ih + 1],
                        scalar2=None, op0=ALU.mult), inc=vv)
                else:
                    P.op("vector", lambda e, ih=ih, qt=qt: e.scalar_tensor_tensor(
                        out=isc[:], in0=ebuf[:], scalar=wih[:, qt * 4 + ih: qt * 4 + ih + 1],
                        in1=isc[:], op0=ALU.mult, op1=ALU.add), inc=vv)
            P.op("vector", lambda e, tcol=tcol: e.tensor_scalar(out=mbuf[:], in0=sb["bn_row_sb"][:],
                 scalar1=tcol, scalar2=None, op0=ALU.is_lt))
            P.op("vector", lambda e: e.tensor_scalar(out=wkb[:], in0=mbuf[:], scalar1=1.0,
                 scalar2=1.0e30, op0=ALU.subtract, op1=ALU.mult))
            P.op("vector", lambda e: e.tensor_add(isc[:], isc[:], wkb[:]))
            for rnd in range(8):
                srcb = isc if rnd == 0 else wkb
                P.op("vector", lambda e, srcb=srcb: e.max(out=m8[:], in_=srcb[:]))
                P.op("vector", lambda e, srcb=srcb: e.match_replace(out=wkb[:], in_to_replace=m8[:],
                     in_values=srcb[:], imm_value=SEL_IMM))
            P.op("vector", lambda e: e.tensor_single_scalar(out=ebuf[:], in_=wkb[:],
                 scalar=-1.5e30, op=ALU.is_lt))
            P.op("vector", lambda e: e.tensor_mul(ebuf[:], ebuf[:], mbuf[:]))
            P.op("vector", lambda e, qt=qt: e.tensor_scalar(out=selb[:, qt * 512:(qt + 1) * 512],
                 in0=ebuf[:], scalar1=1.0, scalar2=1.0e30, op0=ALU.subtract, op1=ALU.mult))
            P.op("vector", lambda e, qt=qt, tcol=tcol: e.tensor_single_scalar(
                 out=avb[:, qt:qt + 1], in_=tcol, scalar=16.0, op=ALU.is_ge), inc=vv)
            Q = g1 + qt * 128
            P.wait("gpsimd", P.ctr("bc"))
            emit_dma(P, "gpsimd", kval0[0:1, 0:384], kval0d[0:1, Q + 256: Q + 640], kvld)
            P.wait_tok("tensor", [tok(vv), (kvld, kvld.val)])
            P.op("tensor", lambda e: e.matmul(banks[6][:, 0:384],
                 sb["ones_row_bf_sb"][0:1, :], kval0[0:1, 0:384],
                 start=True, stop=True), inc=P.ctr("bc"))
            P.wait("vector", P.ctr("bc"))
            P.op("vector", lambda e, qt=qt: e.tensor_add(cmb0[:, qt * 384:(qt + 1) * 384],
                 sb["genmask0_sb"][:], banks[6][:, 0:384]), inc=vv)

        pend = [None]
        mask_tok = [[tok(vv)]]

        def emit_tail(pd):
            if pd is None:
                return
            h, qt, Q, pp = pd["h"], pd["qt"], pd["Q"], pd["pp"]
            P.wait_tok("tensor", pd["pn_tok"])
            for i4 in range(4):
                bnk = banks[2 + (i4 % 2)]
                if i4 >= 2:
                    P.wait("tensor", ev, ev.val - 1)
                P.op("tensor", lambda e, i4=i4, bnk=bnk, pp=pp: e.transpose(bnk[:, 0:128],
                     msc0[:, pp * 512 + i4 * 128: pp * 512 + (i4 + 1) * 128],
                     sb["ident_f_sb"][:]), inc=P.ctr("tp2"))
                P.wait("scalar", P.ctr("tp2"))
                P.op("scalar", lambda e, i4=i4, bnk=bnk, pp=pp: e.activation(
                     out=pT0[:, pp * 512 + i4 * 128: pp * 512 + (i4 + 1) * 128],
                     in_=bnk[:, 0:128], func=AF.Identity, bias=0.0, scale=1.0), inc=ev)
            for i3 in range(3):
                bnk = banks[2 + (i3 % 2)]
                P.wait("tensor", ev, ev.val - 1)
                P.op("tensor", lambda e, i3=i3, bnk=bnk, pp=pp: e.transpose(bnk[:, 0:128],
                     msc1[:, pp * 384 + i3 * 128: pp * 384 + (i3 + 1) * 128],
                     sb["ident_f_sb"][:]), inc=P.ctr("tp2"))
                P.wait("scalar", P.ctr("tp2"))
                P.op("scalar", lambda e, i3=i3, bnk=bnk, pp=pp: e.activation(
                     out=pT1[:, pp * 384 + i3 * 128: pp * 384 + (i3 + 1) * 128],
                     in_=bnk[:, 0:128], func=AF.Identity, bias=0.0, scale=1.0), inc=ev)
            P.wait_tok("tensor", [tok(ev), tok(vv)])
            for i4 in range(4):
                P.op("tensor", lambda e, i4=i4, pp=pp: e.matmul(banks[4][:, 0:128],
                     ckvnN[:, i4 * 128:(i4 + 1) * 128],
                     pT0[:, pp * 512 + i4 * 128: pp * 512 + (i4 + 1) * 128],
                     start=(i4 == 0), stop=False))
            for i3 in range(3):
                kt = (Q + 256) // 128 + i3
                P.op("tensor", lambda e, i3=i3, kt=kt, pp=pp: e.matmul(banks[4][:, 0:128],
                     vN0[:, kt * 128:(kt + 1) * 128],
                     pT1[:, pp * 384 + i3 * 128: pp * 384 + (i3 + 1) * 128],
                     start=False, stop=(i3 == 2)),
                     inc=(P.ctr("om") if i3 == 2 else None))
            P.wait("scalar", P.ctr("om"))
            P.op("scalar", lambda e, h=h, qt=qt: e.activation(
                 out=att0[:, h * 512 + qt * 128: h * 512 + qt * 128 + 128],
                 in_=banks[4][:, 0:128], func=AF.Identity, bias=0.0, scale=1.0), inc=ev)

        it = [0]
        for h in range(P1_LIM["h"]):
            chunks = []
            for kc in range(4):
                ch_ = ws.chunk(qupB, kc * 128, h * 128, 128,
                               row1=((kc + 1) * 128 if kc % 2 == 0 else None))
                ch_["lcols"] = lambda fl, ft: 0
                ch_["rhs"] = qcT[:, kc * 512: kc * 512 + 512]
                chunks.append(ch_)
            cm = gemm(P, f"aq{r}{h}", chunks=chunks, ftiles=[0], N=512, banks=[banks[6]],
                      evac=None, pre_toks=[tok(ev), tok(vv)])
            P.wait_tok("vector", [cm])
            P.wait("vector", P.ctr("sc"))
            rope_norm(P, sb, banks, src=banks[6][:, 0:512], Cs=Cb[:, 0:512], Ss=Sb[:, 0:512],
                      qs=qs0, tmp=rot0, sq=sq0, rows=rows, scalemat=smat0,
                      out_bf=qbf[:, 0:512], wrow=sb["qnrow_sb"], eps_col=rms_col,
                      extra_scale=1.0, half=32, N=512, statbank=banks[5], bcbank=banks[7],
                      vv=vv, ev=ev)
            mask_tok[0] = mask_tok[0] + [tok(vv)]
            for qt in range(4):
                Q = g1 + qt * 128
                pp = it[0] % 2
                P.wait_tok("tensor", mask_tok[0])
                P.op("tensor", lambda e, qt=qt: e.matmul(banks[1][:, 0:512],
                     qbf[:, qt * 128:(qt + 1) * 128], ckvnT[:, 0:NBL],
                     start=True, stop=True))
                P.op("tensor", lambda e, Q=Q, qt=qt: e.matmul(banks[5][:, 0:384],
                     qbf[:, qt * 128:(qt + 1) * 128], swkT[:, Q + 256: Q + 640],
                     start=True, stop=True), inc=P.ctr("sc"))
                P.wait("vector", P.ctr("sc"))
                P.op("vector", lambda e, qt=qt, pp=pp: e.tensor_add(
                     msc0[:, pp * 512:(pp + 1) * 512], banks[1][:, 0:512],
                     selb[:, qt * 512:(qt + 1) * 512]))
                P.op("vector", lambda e, pp=pp: e.reduce_max(ncol[:, pp * 4: pp * 4 + 1],
                     msc0[:, pp * 512:(pp + 1) * 512], axis=AX.X, negate=True))
                P.op("vector", lambda e, qt=qt, pp=pp: e.tensor_add(
                     msc1[:, pp * 384:(pp + 1) * 384], banks[5][:, 0:384],
                     cmb0[:, qt * 384:(qt + 1) * 384]))
                P.op("vector", lambda e, pp=pp: e.reduce_max(ncol[:, pp * 4 + 1: pp * 4 + 2],
                     msc1[:, pp * 384:(pp + 1) * 384], axis=AX.X, negate=True), inc=vv)
                mask_tok[0] = [tok(vv)]
                P.wait("scalar", vv)
                P.op("scalar", lambda e, pp=pp: e.activation(
                     out=msc0[:, pp * 512:(pp + 1) * 512],
                     in_=msc0[:, pp * 512:(pp + 1) * 512], func=AF.Exp,
                     bias=ncol[:, pp * 4: pp * 4 + 1], scale=1.0,
                     accum_out=ncol[:, pp * 4 + 2: pp * 4 + 3]), inc=ev)
                P.op("scalar", lambda e, pp=pp: e.activation(
                     out=msc1[:, pp * 384:(pp + 1) * 384],
                     in_=msc1[:, pp * 384:(pp + 1) * 384], func=AF.Exp,
                     bias=ncol[:, pp * 4 + 1: pp * 4 + 2], scale=1.0,
                     accum_out=ncol[:, pp * 4 + 3: pp * 4 + 4]), inc=ev)
                P.wait("vector", ev)
                P.op("vector", lambda e, pp=pp: e.reciprocal(
                     ncol[:, pp * 4 + 2: pp * 4 + 3], ncol[:, pp * 4 + 2: pp * 4 + 3]))
                P.op("vector", lambda e, qt=qt, pp=pp: e.tensor_mul(
                     ncol[:, pp * 4 + 2: pp * 4 + 3], ncol[:, pp * 4 + 2: pp * 4 + 3],
                     avb[:, qt:qt + 1]))
                P.op("vector", lambda e, pp=pp: e.reciprocal(
                     ncol[:, pp * 4 + 3: pp * 4 + 4], ncol[:, pp * 4 + 3: pp * 4 + 4]))
                P.op("vector", lambda e, pp=pp: e.tensor_scalar(
                     out=msc0[:, pp * 512:(pp + 1) * 512],
                     in0=msc0[:, pp * 512:(pp + 1) * 512],
                     scalar1=ncol[:, pp * 4 + 2: pp * 4 + 3], scalar2=None, op0=ALU.mult))
                P.op("vector", lambda e, pp=pp: e.tensor_scalar(
                     out=msc1[:, pp * 384:(pp + 1) * 384],
                     in0=msc1[:, pp * 384:(pp + 1) * 384],
                     scalar1=ncol[:, pp * 4 + 3: pp * 4 + 4], scalar2=None, op0=ALU.mult),
                     inc=vv)
                pn_tok = [tok(vv)]
                emit_tail(pend[0])
                pend[0] = {"h": h, "qt": qt, "Q": Q, "pp": pp, "pn_tok": pn_tok}
                it[0] += 1
        emit_tail(pend[0])
        pend[0] = None

        P.wait_tok("tensor", [tok(vv), tok(ev)])
        for g in range(4):
            chunks = []
            for kc in range(4):
                ch_ = ws.chunk(gwB, (g * 4 + kc) * 128, 0, 512,
                               row1=((g * 4 + kc + 1) * 128 if kc % 2 == 0 else None))
                ch_["lcols"] = lambda fl, ft: fl * 128
                ch_["rhs"] = att0[:, (g * 4 + kc) * 512: (g * 4 + kc) * 512 + 512]
                chunks.append(ch_)

            def ev_grp(fl, ft, ps, g=g):
                P.op("scalar", lambda e, o=grp[:, (g * 4 + ft) * 512:(g * 4 + ft + 1) * 512],
                     i=ps: e.activation(out=o, in_=i, func=AF.Identity, bias=0.0, scale=1.0),
                     inc=ev)

            gemm(P, f"agrp{r}{g}", chunks=chunks, ftiles=[0, 1, 2, 3], N=512,
                 banks=banks[0:4], evac=ev_grp, pre_toks=[tok(ev), tok(vv)])

        wavetok = [None, None]
        pre0 = [tok(ev), tok(vv), tok(rl.cons)]
        for qw in range(4):
            par = qw % 2
            fts = list(range(qw * 4, (qw + 1) * 4))
            chunks = []
            for kc in range(16):
                ch_ = ws.chunk(finB, kc * 128, qw * 512, 512,
                               row1=((kc + 1) * 128 if kc % 2 == 0 else None))
                ch_["lcols"] = lambda fl, ft: fl * 128
                ch_["rhs"] = grp[:, kc * 512: kc * 512 + 512]
                if kc == 0 and qw == 0:
                    ch_["toks"] = ch_["toks"] + [tok(ev)]
                chunks.append(ch_)

            def ev_fin(fl, ft, ps, hcol=hcol):
                buf, toks = rl.load(hT_loc, ft * 128, hcol, 512)
                P.wait_tok("vector", toks + [tok(ev)])
                P.op("vector", lambda e, b2=buf, o=rbf[:, ft * 512:(ft + 1) * 512], i=ps:
                     e.tensor_add(o, i, b2[:, 0:512]), inc=rl.cons)

            pre = [wavetok[par]] if wavetok[par] else pre0
            gemm(P, f"afin{r}{qw}", chunks=chunks, ftiles=fts, N=512,
                 banks=banks[par * 4:(par + 1) * 4], evac=ev_fin, evac_eng="vector",
                 pre_toks=pre)
            wavetok[par] = tok(rl.cons)
        lnd = P.ctr("ln")
        layer_norm(P, f"aln{r}", rbuf=rbf, rT=512, t0=0, N=512,
                   out_fn=lambda ft: rbf[:, ft * 512:(ft + 1) * 512],
                   out_dtype_note="bf16", banks=banks[5:8], sb=sb,
                   sq_scratch=sqln, rows=rows, done_ctr=lnd,
                   r_toks=[tok(rl.cons)], eps_col=ln_col, ones_key="ones_col_bf_sb")

        P.wait_tok("tensor", [tok(lnd)])
        for tt in range(4):
            P.wait("vector", P.ctr("rt"))
            P.wait("tensor", vv)
            for kc in range(16):
                P.op("tensor", lambda e, tt=tt, kc=kc: e.matmul(banks[0][:, 0:8],
                     rbf[:, kc * 512 + tt * 128: kc * 512 + tt * 128 + 128],
                     sb["W_routerT_bf"][:, kc * 8:(kc + 1) * 8],
                     start=(kc == 0), stop=(kc == 15)),
                     inc=(P.ctr("rt") if kc == 15 else None))
            P.wait("vector", P.ctr("rt"))
            P.op("vector", lambda e, tt=tt: e.tensor_copy(out=glog[:, tt * 8:(tt + 1) * 8],
                 in_=banks[0][:, 0:8]), inc=vv)
        for tt in range(4):
            gl = glog[:, tt * 8:(tt + 1) * 8]
            P.op("vector", lambda e, gl=gl: e.reduce_max(gcol[:, 0:1], gl, axis=AX.X))
            P.op("vector", lambda e, gl=gl: e.tensor_scalar(out=gsc[:], in0=gl,
                 scalar1=gcol[:, 0:1], scalar2=None, op0=ALU.is_equal))
            P.op("vector", lambda e, gl=gl: e.scalar_tensor_tensor(out=gsc[:], in0=gsc[:],
                 scalar=-1.0e30, in1=gl, op0=ALU.mult, op1=ALU.add))
            P.op("vector", lambda e: e.reduce_max(gcol[:, 1:2], gsc[:], axis=AX.X))
            P.op("vector", lambda e, gl=gl: e.tensor_scalar(out=gsc[:], in0=gl,
                 scalar1=gcol[:, 1:2], scalar2=None, op0=ALU.is_ge))
            P.op("vector", lambda e, gl=gl: e.reduce_max(gcol[:, 2:3], gl, axis=AX.X,
                 negate=True), inc=vv)
            P.wait("scalar", vv)
            P.op("scalar", lambda e, gl=gl: e.activation(out=gl, in_=gl, func=AF.Exp,
                 bias=gcol[:, 2:3], scale=1.0), inc=ev)
            P.wait("vector", ev)
            P.op("vector", lambda e, gl=gl: e.tensor_mul(gl, gl, gsc[:]))
            P.op("vector", lambda e, gl=gl: e.reduce_sum(gcol[:, 3:4], gl, axis=AX.X))
            P.op("vector", lambda e: e.reciprocal(gcol[:, 3:4], gcol[:, 3:4]))
            P.op("vector", lambda e: e.tensor_scalar_mul(gcol[:, 3:4], gcol[:, 3:4],
                 float(SCALE0)))
            P.op("vector", lambda e, gl=gl: e.tensor_scalar(out=gl, in0=gl,
                 scalar1=gcol[:, 3:4], scalar2=None, op0=ALU.mult), inc=vv)
            P.wait_tok("tensor", [tok(vv)])
            P.wait("tensor", vv)
            for kc in range(16):
                P.op("tensor", lambda e, tt=tt, kc=kc: e.matmul(banks[1][:, 0:64],
                     rbf[:, kc * 512 + tt * 128: kc * 512 + tt * 128 + 128],
                     sb["A_allT_bf"][:, kc * 64:(kc + 1) * 64],
                     start=(kc == 0), stop=(kc == 15)),
                     inc=(P.ctr("ha") if kc == 15 else None))
            P.wait("vector", P.ctr("ha"))
            for e0 in range(8):
                P.op("vector", lambda e, e0=e0, gl=gl: e.tensor_scalar(
                     out=hallf[:, e0 * 8:(e0 + 1) * 8], in0=banks[1][:, e0 * 8:(e0 + 1) * 8],
                     scalar1=gl[:, e0:e0 + 1], scalar2=None, op0=ALU.mult),
                     inc=(vv if e0 == 7 else None))
            P.wait_tok("tensor", [tok(vv)])
            P.op("tensor", lambda e: e.transpose(banks[2][0:64, 0:128], hallf[:, 0:64],
                 sb["ident_f_sb"][:]), inc=P.ctr("tp3"))
            P.wait("scalar", P.ctr("tp3"))
            P.op("scalar", lambda e, tt=tt: e.activation(out=hprT[0:64, tt * 128:(tt + 1) * 128],
                 in_=banks[2][0:64, 0:128], func=AF.Identity, bias=0.0, scale=1.0), inc=ev)
        wavetok = [None, None]
        pre0 = [tok(ev), tok(vv)]
        for qw in range(4):
            par = qw % 2
            fts = list(range(qw * 4, (qw + 1) * 4))
            chunks = []
            for kc in range(16):
                ch_ = ws.chunk(wbB, kc * 128, qw * 512, 512,
                               row1=((kc + 1) * 128 if kc % 2 == 0 else None))
                ch_["lcols"] = lambda fl, ft: fl * 128
                ch_["rhs"] = rbf[:, kc * 512: kc * 512 + 512]
                if kc == 0 and qw == 0:
                    ch_["toks"] = ch_["toks"] + [tok(ev), tok(vv)]
                chunks.append(ch_)
            chunks.append({"lhsT": (lambda fl, ft, qw=qw:
                           sb["B_allT_pad_bf"][:, qw * 512 + fl * 128: qw * 512 + fl * 128 + 128]),
                           "rhs": hprT[:, 0:512],
                           "toks": ([tok(ev)] if qw == 0 else [])})

            def ev_wb(fl, ft, ps):
                bcol = bpk[:, bp.col("b_base", ft):bp.col("b_base", ft) + 1]
                P.wait("scalar", vv, max(0, vv.val - 1))
                P.op("scalar", lambda e, i=ps, bcol=bcol, ft=ft: e.activation(
                     out=tmpf[:, (ft % 2) * 512:(ft % 2) * 512 + 512], in_=i,
                     func=AF.Identity, bias=bcol, scale=1.0), inc=ev)
                P.wait_tok("vector", [tok(ev)])
                P.op("vector", lambda e, ft=ft: e.tensor_add(
                     att0[:, ft * 512:(ft + 1) * 512],
                     tmpf[:, (ft % 2) * 512:(ft % 2) * 512 + 512],
                     rbf[:, ft * 512:(ft + 1) * 512]), inc=vv)

            pre = [wavetok[par]] if wavetok[par] else pre0
            gemm(P, f"awb{r}{qw}", chunks=chunks, ftiles=fts, N=512,
                 banks=banks[par * 4:(par + 1) * 4], evac=ev_wb, pre_toks=pre)
            wavetok[par] = tok(ev)
        layer_norm(P, f"aln2{r}", rbuf=att0, rT=512, t0=0, N=512,
                   out_fn=lambda ft: att0[:, ft * 512:(ft + 1) * 512],
                   out_dtype_note="bf16", banks=banks[5:8], sb=sb,
                   sq_scratch=sqln, rows=rows, done_ctr=lnd,
                   r_toks=[tok(vv)], eps_col=ln_col, ones_key="ones_col_bf_sb")
        P.wait("gpsimd", lnd)
        P.wait("vector", st)
        P.op("vector", lambda e: e.nop(), inc=vv)
        P.wait("gpsimd", vv)
        for ft in range(16):
            emit_dma(P, "gpsimd", x1T[ft * 128:(ft + 1) * 128, g1:g1 + 512],
                     att0[:, ft * 512:(ft + 1) * 512], st)
        P.barrier()

    P.wait("sync", st)
    P.wait("gpsimd", st)


A_CONSTS = ["ident_f", "ones_col_f", "ones_col_bf", "ones_row_f", "ones_row_bf",
            "bn_row", "genmask0", "kvc_bias_rep", "idx_bias_rep", "qnrow", "knrow"]


def build_launch_A(bp):
    nc = bass.Bass("TRN2", target_bir_lowering=False)
    ctx = ExitStack()
    P = Prog(nc, ctx)
    with ctx:
        io = {}
        io["hT_full"] = nc.dram_tensor("hT_full", [D, T], F32, kind="ExternalInput").ap()
        io["hT_loc"] = nc.dram_tensor("hT_loc", [D, TLOCP], F32, kind="ExternalInput").ap()
        io["x1T"] = nc.dram_tensor("x1T", [D, T1], BF16, kind="ExternalOutput").ap()
        wsrc = {}
        for nm, shape in [("w_qc", [D, QC]), ("w_qup", [QC, D]), ("group_w", [G0 * DG, DG]),
                          ("final_w", [D, D]), ("W_baseT", [D, D])]:
            wsrc[nm] = nc.dram_tensor(nm, list(shape), F32, kind="ExternalInput").ap()
        smallsrc = {}
        for nm, R, Cc in A_SMALL:
            smallsrc[nm] = nc.dram_tensor(nm, [R, Cc], F32, kind="ExternalInput").ap()
        io["qcB"] = nc.dram_tensor("qcB", [D, QC], BF16, kind="Internal").ap()
        io["qupB"] = nc.dram_tensor("qupB", [QC, D], BF16, kind="Internal").ap()
        io["gwB"] = nc.dram_tensor("gwB", [G0 * DG, DG], BF16, kind="Internal").ap()
        io["finB"] = nc.dram_tensor("finB", [D, D], BF16, kind="Internal").ap()
        io["wbB"] = nc.dram_tensor("wbB", [D, D], BF16, kind="Internal").ap()

        sb = load_consts(P, ctx, nc, A_CONSTS, bp.array_cols)
        banks = [ctx.enter_context(nc.psum_tensor(f"bank{i}", [128, 512], F32)) for i in range(8)]

        for nm, R, Cc in A_SMALL:
            nt = max(1, R // 128)
            sb[nm + "_bf"] = ctx.enter_context(nc.sbuf_tensor(nm + "_bf", [128, nt * Cc], BF16))
        with ExitStack() as wctx:
            smstage = wctx.enter_context(nc.sbuf_tensor("a_smst", [128, 2048], F32))
            sm = P.ctr("smw")
            smc = P.ctr("smc")
            for nm, R, Cc in A_SMALL:
                nt = max(1, R // 128)
                t = sb[nm + "_bf"]
                for i in range(nt):
                    rr = min(128, R - i * 128)
                    P.wait("sync", smc)
                    emit_dma(P, "sync", smstage[0:rr, 0:Cc],
                             smallsrc[nm][i * 128:i * 128 + rr, 0:Cc], sm)
                    P.wait("vector", sm)
                    P.op("vector", lambda e, t=t, i=i, Cc=Cc, rr=rr: e.tensor_copy(
                         out=t[0:rr, i * Cc:(i + 1) * Cc], in_=smstage[0:rr, 0:Cc]), inc=smc)
            for e_ in ("tensor", "scalar", "gpsimd"):
                P.wait(e_, smc)
            wcd = P.ctr("wconv_done")
            jobs = [(wsrc["w_qc"], io["qcB"], D, QC), (wsrc["w_qup"], io["qupB"], QC, D),
                    (wsrc["group_w"], io["gwB"], G0 * DG, DG),
                    (wsrc["final_w"], io["finB"], D, D), (wsrc["W_baseT"], io["wbB"], D, D)]
            emit_wconv(P, wctx, jobs, wcd)
            P.wait("sync", wcd)

        ws = WStream(P, ctx)
        emit_p0p1(P, ctx, io, sb, banks, bp, ws, None)
        P.run()
    return nc




FUSED_CONSTS = sorted(set(A_CONSTS + ["ones_row_f", "ones_row_bf", "ident_f",
                                      "ones_col_f", "ones_col_bf"]))
FUSED_P2_EXTRA = ["genmask1", "gqrow", "gkrow"]
FUSED_P3_EXTRA = ["selbc", "bg0", "bg1"]


def build_fused(bp):
    nc = bass.Bass("TRN2", target_bir_lowering=False)
    ctx = ExitStack()
    P = Prog(nc, ctx)
    with ctx:
        io = {}
        io["hT_full"] = nc.dram_tensor("hT_full", [D, T], F32, kind="ExternalInput").ap()
        io["hT_loc"] = nc.dram_tensor("hT_loc", [D, TLOCP], F32, kind="ExternalInput").ap()
        io["x1T"] = nc.dram_tensor("x1T", [D, T1], BF16, kind="Internal").ap()
        io["x2T"] = nc.dram_tensor("x2T", [D, OWN], BF16, kind="Internal").ap()
        io["xtT"] = nc.dram_tensor("xtT", [D, OWN], BF16, kind="Internal").ap()
        io["outT"] = nc.dram_tensor("outT", [D, OWN], F32, kind="ExternalOutput").ap()
        wsrc = {}
        for nm, shape in [("w_qc", [D, QC]), ("w_qup", [QC, D]), ("group_w", [G0 * DG, DG]),
                          ("final_w", [D, D]), ("W_baseT", [D, D]),
                          ("wqT", [D, D]), ("wkT", [D, 512]), ("wvT", [D, 512]), ("woT", [D, D]),
                          ("Wup2d", [NEXP1 * D, E1]), ("Wsw2d", [NEXP1 * E1, E1]),
                          ("Wdn2d", [NEXP1 * E1, D]), ("Wsu", [D, D]), ("Wsa", [E1, E1]),
                          ("Wsd", [D, D]), ("Wg0", [D, 8]), ("Wg1", [D, 8]),
                          ("bdn_pad", [128, D])]:
            wsrc[nm] = nc.dram_tensor(nm, list(shape), F32, kind="ExternalInput").ap()
        for nm, shape in [("qcB", [D, QC]), ("qupB", [QC, D]), ("gwB", [G0 * DG, DG]),
                          ("finB", [D, D]), ("wbB", [D, D]),
                          ("wqB", [D, D]), ("wkB", [D, 512]), ("wvB", [D, 512]), ("woB", [D, D]),
                          ("WupB", [NEXP1 * D, E1]), ("WswB", [NEXP1 * E1, E1]),
                          ("WdnB", [NEXP1 * E1, D]), ("WsuB", [D, D]), ("WsaB", [E1, E1]),
                          ("WsdB", [D, D])]:
            io[nm] = nc.dram_tensor(nm, list(shape), BF16, kind="Internal").ap()
        smallsrc = {nm: wsrc.get(nm) for nm in []}
        a_smallsrc = {}
        for nm, R, Cc in A_SMALL:
            a_smallsrc[nm] = nc.dram_tensor(nm, [R, Cc], F32, kind="ExternalInput").ap()

        sb = load_consts(P, ctx, nc, FUSED_CONSTS, bp.array_cols)
        banks = [ctx.enter_context(nc.psum_tensor(f"bank{i}", [128, 512], F32)) for i in range(8)]

        with ExitStack() as actx:
            for nm, R, Cc in A_SMALL:
                nt = max(1, R // 128)
                sb[nm + "_bf"] = actx.enter_context(
                    nc.sbuf_tensor(nm + "_bf", [128, nt * Cc], BF16))
            with ExitStack() as wctx:
                smstage = wctx.enter_context(nc.sbuf_tensor("a_smst", [128, 2048], F32))
                sm = P.ctr("smw")
                smc = P.ctr("smc")
                for nm, R, Cc in A_SMALL:
                    nt = max(1, R // 128)
                    t = sb[nm + "_bf"]
                    for i in range(nt):
                        rr = min(128, R - i * 128)
                        P.wait("sync", smc)
                        emit_dma(P, "sync", smstage[0:rr, 0:Cc],
                                 a_smallsrc[nm][i * 128:i * 128 + rr, 0:Cc], sm)
                        P.wait("vector", sm)
                        P.op("vector", lambda e, t=t, i=i, Cc=Cc, rr=rr, s2=smstage:
                             e.tensor_copy(out=t[0:rr, i * Cc:(i + 1) * Cc],
                                           in_=s2[0:rr, 0:Cc]), inc=smc)
                for e_ in ("tensor", "scalar", "gpsimd"):
                    P.wait(e_, smc)
                P.wait("sync", smc)

            jobs = [(wsrc["w_qc"], io["qcB"], D, QC), (wsrc["w_qup"], io["qupB"], QC, D),
                    (wsrc["group_w"], io["gwB"], G0 * DG, DG),
                    (wsrc["final_w"], io["finB"], D, D), (wsrc["W_baseT"], io["wbB"], D, D),
                    (wsrc["wqT"], io["wqB"], D, 2048), (wsrc["wkT"], io["wkB"], D, 512),
                    (wsrc["wvT"], io["wvB"], D, 512), (wsrc["woT"], io["woB"], D, 2048),
                    (wsrc["Wup2d"], io["WupB"], NEXP1 * D, E1),
                    (wsrc["Wsw2d"], io["WswB"], NEXP1 * E1, E1),
                    (wsrc["Wdn2d"], io["WdnB"], NEXP1 * E1, D),
                    (wsrc["Wsu"], io["WsuB"], D, D), (wsrc["Wsa"], io["WsaB"], E1, E1),
                    (wsrc["Wsd"], io["WsdB"], D, D)]
            ws = WStream(P, actx, name="slabA")
            emit_p0p1(P, actx, io, sb, banks, bp, ws, jobs)
            P.barrier()

        with ExitStack() as bctx:
            sb.update(load_consts(P, bctx, nc, FUSED_P2_EXTRA, bp.array_cols,
                                  with_bpack=False))
            ws2 = WStream(P, bctx, name="slabB")
            emit_p2(P, bctx, io, sb, banks, bp, ws2)
            P.barrier()

        with ExitStack() as cctx:
            sb.update(load_consts(P, cctx, nc, FUSED_P3_EXTRA, bp.array_cols,
                                  with_bpack=False))
            sb["bdn_pad_bf"] = cctx.enter_context(nc.sbuf_tensor("bdnp_sb", [128, D], BF16))
            sb["Wg0_bf"] = cctx.enter_context(nc.sbuf_tensor("wg0_sb", [128, 16 * 8], BF16))
            sb["Wg1_bf"] = cctx.enter_context(nc.sbuf_tensor("wg1_sb", [128, 16 * 8], BF16))
            with ExitStack() as wctx2:
                smstage = wctx2.enter_context(nc.sbuf_tensor("c_smst", [128, 2048], F32))
                sm = P.ctr("smw")
                smc = P.ctr("smc")
                P.op("gpsimd", lambda e: e.memset(smstage[:], 0.0), inc=smc)
                P.wait("sync", smc)
                emit_dma(P, "sync", smstage[:, 0:D], wsrc["bdn_pad"][0:128, 0:D], sm)
                P.wait("vector", sm)
                P.op("vector", lambda e, s2=smstage: e.tensor_copy(out=sb["bdn_pad_bf"][:],
                     in_=s2[:, 0:D]), inc=smc)
                for nm, key in [("Wg0", "Wg0_bf"), ("Wg1", "Wg1_bf")]:
                    P.wait("sync", smc)
                    for kc in range(16):
                        emit_dma(P, "sync", smstage[:, kc * 8:(kc + 1) * 8],
                                 wsrc[nm][kc * 128:(kc + 1) * 128, 0:8], sm)
                    P.wait("vector", sm)
                    P.op("vector", lambda e, d=sb[key], s2=smstage: e.tensor_copy(
                         out=d[:], in_=s2[:, 0:128]), inc=smc)
                for e_ in ("tensor", "scalar", "gpsimd"):
                    P.wait(e_, smc)
                P.wait("sync", smc)
            ws3 = WStream(P, cctx, name="slabC")
            emit_p3(P, cctx, io, sb, banks, bp, ws3)
        P.run()
    return nc


FUSED_INPUT_NAMES = None


def kernel_fused(per_core, bp):
    nc = build_fused(bp)
    names = ["hT_full", "hT_loc", "w_qc", "w_qup", "group_w", "final_w", "W_baseT",
             "wqT", "wkT", "wvT", "woT", "Wup2d", "Wsw2d", "Wdn2d", "Wsu", "Wsa", "Wsd",
             "Wg0", "Wg1", "bdn_pad",
             "kvc_w", "kvc_wz", "idx_pack", "k_proj", "v_proj", "w_dq", "w_iuq", "w_w",
             "W_routerT", "A_allT", "B_allT_pad", "bpack",
             "C0", "S0", "kval0b", "tpos_cols", "C1", "S1", "kval1b"] + \
            list(FUSED_CONSTS) + FUSED_P2_EXTRA + FUSED_P3_EXTRA
    in_maps = [{k: per_core[c][k] for k in names} for c in range(NCORES)]
    res = run_bass_kernel_spmd(nc, in_maps, core_ids=list(range(NCORES)))
    out = np.zeros((B, T, D), np.float32)
    for c in range(NCORES):
        b, ch = c // CHUNKS, c % CHUNKS
        s = ch * OWN
        out[b, s:s + OWN, :] = np.asarray(res.results[c]["outT"], np.float32).T
    return out


A_INPUT_NAMES = ["hT_full", "hT_loc", "w_qc", "w_qup", "group_w", "final_w", "W_baseT",
                 "kvc_w", "kvc_wz", "idx_pack", "k_proj", "v_proj", "w_dq", "w_iuq", "w_w",
                 "W_routerT", "A_allT", "B_allT_pad",
                 "ident_f", "ones_col_f", "ones_col_bf", "ones_row_f", "ones_row_bf",
                 "bn_row", "genmask0", "kvc_bias_rep", "idx_bias_rep", "qnrow", "knrow",
                 "bpack", "C0", "S0", "kval0b", "tpos_cols"]
P2_INPUT_NAMES = ["wqT", "wkT", "wvT", "woT", "kval1b", "C1", "S1",
                  "ident_f", "ones_col_f", "ones_row_f", "ones_row_bf", "genmask1",
                  "gqrow", "gkrow", "bpack"]
P3_INPUT_NAMES = ["Wup2d", "Wsw2d", "Wdn2d", "Wsu", "Wsa", "Wsd", "Wg0", "Wg1", "bdn_pad",
                  "bg0", "bg1", "ident_f", "ones_col_f", "ones_col_bf", "ones_row_f",
                  "ones_row_bf", "selbc", "bpack"]


def kernel(**inputs):
    per_core, bp = host_prep(inputs)
    return kernel_fused(per_core, bp)


def kernel_multilaunch(**inputs):
    per_core, bp = host_prep(inputs)

    ncA = build_launch_A(bp)
    resA = run_bass_kernel_spmd(
        ncA, [{k: per_core[c][k] for k in A_INPUT_NAMES} for c in range(NCORES)],
        core_ids=list(range(NCORES)))

    ncB = build_launch_p2(bp)
    in_b = []
    for c in range(NCORES):
        m = {k: per_core[c][k] for k in P2_INPUT_NAMES}
        m["x1T"] = np.asarray(resA.results[c]["x1T"])
        in_b.append(m)
    resB = run_bass_kernel_spmd(ncB, in_b, core_ids=list(range(NCORES)))

    ncC = build_launch_p3(bp)
    in_c = []
    for c in range(NCORES):
        m = {k: per_core[c][k] for k in P3_INPUT_NAMES}
        m["x2T"] = np.asarray(resB.results[c]["x2T"], dtype=np.float32)
        in_c.append(m)
    resC = run_bass_kernel_spmd(ncC, in_c, core_ids=list(range(NCORES)))

    out = np.zeros((B, T, D), np.float32)
    for c in range(NCORES):
        b, ch = c // CHUNKS, c % CHUNKS
        s = ch * OWN
        out[b, s:s + OWN, :] = np.asarray(resC.results[c]["outT"], np.float32).T
    return out


# version 13
# speedup vs baseline: 1.0629x; 1.0629x over previous
import numpy as np
import ml_dtypes
from contextlib import ExitStack

import concourse.bass as bass
import concourse.mybir as mybir
from concourse.bass_utils import run_bass_kernel_spmd

F32 = mybir.dt.float32
BF16 = mybir.dt.bfloat16
AF = mybir.ActivationFunctionType
ALU = mybir.AluOpType
AX = mybir.AxisListType
BF = ml_dtypes.bfloat16

B = 2
T = 8192
D = 2048
NCORES = 8
CHUNKS = 4
OWN = T // CHUNKS
HALO1 = 1024
T1 = OWN + HALO1
HALO0 = 512
TLOCP = T1 + HALO0
NEG = -1.0e30
SEL_IMM = -2.0e30
SR = 512
NSR = T1 // SR
H0, HD0 = 16, 128
QC = 512
ROPE0 = 64
M0 = 16
NBL = T // M0
TOPK = 64
NIH, CI = 4, 64
G0, DG = 4, 512
WIN0 = 256
EPS0 = 1e-6
H1, HK1, HD1 = 16, 4, 128
WIN1 = 1024
EPS1 = 1e-6
E0, R0 = 8, 8
SCALE0 = 16.0 / 8.0
NE1, NEXP1, E1, S1, C1 = 8, 6, 1024, 2, 2
LN_EPS = 1e-5
TCH = 512
NTCH = OWN // TCH


class Ctr:
    __slots__ = ("sem", "val", "name")

    def __init__(self, sem, name):
        self.sem = sem
        self.val = 0
        self.name = name


def tok(ctr):
    return (ctr, ctr.val)


class Prog:
    ENGINES = ("sync", "tensor", "vector", "scalar", "gpsimd")
    SELF_EDGE = ("vector", "scalar", "gpsimd")

    def __init__(self, nc, ctx):
        self.nc = nc
        self.ctx = ctx
        self.q = {e: [] for e in self.ENGINES}
        self._ctrs = {}
        self._last = {e: None for e in self.ENGINES}

    def ctr(self, name):
        if name not in self._ctrs:
            sem = self.ctx.enter_context(self.nc.semaphore(name))
            self._ctrs[name] = Ctr(sem, name)
        return self._ctrs[name]

    def op(self, eng, fn, inc=None, k=1, dma=False, track=True):
        if eng in self.SELF_EDGE and not dma:
            prev = self._last[eng]
            if prev is not None:
                self.q[eng].append(("wait", prev[0], prev[1]))
            if inc is None and track:
                inc = self.ctr(eng + "_ser")
                k = 1
        if inc is not None:
            inc.val += k
            self.q[eng].append(("op", fn, inc, k))
            if eng in self.SELF_EDGE and not dma and track:
                self._last[eng] = (inc, inc.val)
        else:
            self.q[eng].append(("op", fn, None, 0))

    def wait(self, eng, ctr, val=None):
        self.q[eng].append(("wait", ctr, ctr.val if val is None else val))

    def wait_tok(self, eng, toks):
        for (ctr, val) in toks:
            if val > 0:
                self.q[eng].append(("wait", ctr, val))

    def barrier(self, engines=("tensor", "vector", "scalar", "gpsimd")):
        bar = self.ctr("bar")
        for e in engines:
            self.op(e, lambda eng: eng.nop(), inc=bar, track=False)
        tgt = bar.val
        for e in engines:
            self.q[e].append(("wait", bar, tgt))

    def check(self):
        ptr = {e: 0 for e in self.ENGINES}
        cur = {name: 0 for name in self._ctrs}
        progress = True
        while progress:
            progress = False
            for e in self.ENGINES:
                while ptr[e] < len(self.q[e]):
                    item = self.q[e][ptr[e]]
                    if item[0] == "wait":
                        if cur[item[1].name] < item[2]:
                            break
                    else:
                        _, fn, inc, k = item
                        if inc is not None:
                            cur[inc.name] += k
                    ptr[e] += 1
                    progress = True
        stuck = {e: (ptr[e], len(self.q[e])) for e in self.ENGINES if ptr[e] < len(self.q[e])}
        if stuck:
            detail = []
            for e, (p, n) in stuck.items():
                item = self.q[e][p]
                if item[0] == "wait":
                    detail.append(f"{e}@{p}/{n} waits {item[1].name}>={item[2]} (cur {cur[item[1].name]})")
                else:
                    detail.append(f"{e}@{p}/{n} op-blocked?")
            raise RuntimeError("deadlock in program build: " + "; ".join(detail))

    def run(self):
        self.check()
        with self.nc.Block() as block:
            for eng_name in self.ENGINES:
                items = self.q[eng_name]
                if not items:
                    continue
                dec = getattr(block, eng_name)

                def body(eng, items=items):
                    for item in items:
                        if item[0] == "wait":
                            eng.wait_ge(item[1].sem, item[2])
                        else:
                            _, fn, inc, k = item
                            inst = fn(eng)
                            if inc is not None:
                                inst.then_inc(inc.sem, k)

                dec(body)


def emit_dma(P, eng, dst, src, inc):
    P.op(eng, lambda e, d=dst, s=src: e.dma_start(out=d, in_=s), inc=inc, k=16, dma=True)


class WStream:
    RING = 4

    RING_U = 4

    def __init__(self, P, ctx, name="slab", ring=None):
        self.P = P
        self.RING = WStream.RING_U
        self.bufs = [ctx.enter_context(P.nc.sbuf_tensor(f"{name}{i}", [128, 2048], BF16))
                     for i in range(self.RING)]
        self.lds = [P.ctr(f"sl_ld{i}") for i in range(self.RING)]
        self.use = P.ctr("sl_use")
        self.seq = self.use.val
        self.pair = self.seq // 2
        self._half = None

    def chunk(self, dram2d, row0, col0, ncols, row1=None):
        j = self.seq
        self.seq += 1
        if self._half is not None:
            buf, tk, off = self._half
            self._half = None
            return {"stream": self, "buf": buf, "toks": [tk], "loff": off}
        pj = self.pair
        self.pair += 1
        slot = pj % self.RING
        buf = self.bufs[slot]
        if pj >= self.RING:
            self.P.wait("sync", self.use, 2 * (pj - self.RING) + 2)
        if row1 is not None:
            dst = buf[:, 0:2 * ncols].rearrange("p (a n) -> p a n", a=2)
            import concourse.bass as _bass
            t = dram2d.tensor
            rs = dram2d.ap[0][0]
            base = dram2d.offset + row0 * rs + col0
            sap = _bass.AP(t, base, [[rs, 128], [(row1 - row0) * rs, 2], [1, ncols]])
            emit_dma(self.P, "sync", dst, sap, self.lds[slot])
            tk = (self.lds[slot], self.lds[slot].val)
            self._half = (buf, tk, ncols)
            return {"stream": self, "buf": buf, "toks": [tk], "loff": 0}
        emit_dma(self.P, "sync", buf[:, 0:ncols],
                 dram2d[row0:row0 + 128, col0:col0 + ncols], self.lds[slot])
        self.seq += 1
        self.use.val += 0
        self._single_pad = True
        return {"stream": self, "buf": buf, "toks": [(self.lds[slot], self.lds[slot].val)],
                "loff": 0, "pad_use": True}


def gemm(P, name, *, chunks, ftiles, N, banks, evac, pre_toks=(), evac_eng="scalar"):
    mm = P.ctr("gmm") if chunks and ("stream" not in chunks[-1]) else None
    P.wait_tok("tensor", list(pre_toks))
    nk = len(chunks)
    comp_tok = None
    for kci, ch in enumerate(chunks):
        P.wait_tok("tensor", ch.get("toks", ()))
        rhs = ch["rhs"]
        strm = ch.get("stream")
        for fl, ft in enumerate(ftiles):
            last_of_chunk = fl == len(ftiles) - 1
            if "buf" in ch:
                c0 = ch["lcols"](fl, ft) + ch.get("loff", 0)
                lhsT = ch["buf"][:, c0:c0 + 128]
            else:
                lhsT = ch["lhsT"](fl, ft)
            inc = None
            k = 1
            if last_of_chunk and strm is not None:
                inc = strm.use
                if ch.get("pad_use"):
                    k = 2
            elif last_of_chunk and kci == nk - 1 and mm is not None:
                inc = mm
            P.op("tensor",
                 lambda e, o=banks[fl][:, 0:N], l=lhsT, r=rhs,
                        s=(kci == 0), p=(kci == nk - 1): e.matmul(o, l, r, start=s, stop=p),
                 inc=inc, k=k)
        if kci == nk - 1:
            comp_tok = tok(strm.use) if strm is not None else tok(mm)
    if evac is not None:
        P.wait_tok(evac_eng, [comp_tok])
        for fl, ft in enumerate(ftiles):
            evac(fl, ft, banks[fl][:, 0:N])
    return comp_tok


def layer_norm(P, name, *, rbuf, rT, t0, N, out_fn, out_dtype_note, banks, sb,
               sq_scratch, rows, done_ctr, r_toks, nf=16, eps_col=None,
               ones_key="ones_col_f_sb"):
    mm = P.ctr("ln_mm")
    sq = P.ctr("ln_sq")
    vv = P.ctr("ln_vv")
    onesf = sb[ones_key]
    P.wait("scalar", mm)
    P.wait("vector", mm)
    P.wait_tok("tensor", r_toks)
    for ft in range(nf):
        P.op("tensor", lambda e, o=banks[0][0:1, 0:N], l=onesf[:, 0:1],
             r=rbuf[:, ft * rT + t0: ft * rT + t0 + N], s=(ft == 0), p=(ft == nf - 1):
             e.matmul(o, l, r, start=s, stop=p), inc=(mm if ft == nf - 1 else None))
    P.wait_tok("scalar", r_toks)
    for ft in range(nf):
        par = (ft % 2) * 512
        if ft >= 2:
            P.wait("scalar", mm, mm.val - 1)
        P.op("scalar", lambda e, o=sq_scratch[:, par:par + N],
             i=rbuf[:, ft * rT + t0: ft * rT + t0 + N]:
             e.activation(out=o, in_=i, func=AF.Square, bias=0.0, scale=1.0), inc=sq)
        P.wait("tensor", sq)
        P.op("tensor", lambda e, o=banks[1][0:1, 0:N], l=onesf[:, 0:1],
             r=sq_scratch[:, par:par + N], s=(ft == 0), p=(ft == nf - 1):
             e.matmul(o, l, r, start=s, stop=p), inc=mm)
    m_row = rows[0:1, 0:N]
    ex2_row = rows[0:1, 512:512 + N]
    m2_row = rows[0:1, 1024:1024 + N]
    var_row = rows[0:1, 512:512 + N]
    rstd_row = rows[0:1, 1024:1024 + N]
    mr_row = rows[0:1, 512:512 + N]
    P.wait("vector", mm)
    P.op("vector", lambda e: e.tensor_scalar_mul(m_row, banks[0][0:1, 0:N], 1.0 / D))
    P.op("vector", lambda e: e.tensor_scalar_mul(ex2_row, banks[1][0:1, 0:N], 1.0 / D))
    P.op("vector", lambda e: e.tensor_mul(m2_row, m_row, m_row))
    P.op("vector", lambda e: e.tensor_sub(var_row, ex2_row, m2_row), inc=vv)
    P.wait("scalar", vv)
    P.op("scalar", lambda e: e.activation(out=var_row, in_=var_row, func=AF.Sqrt,
                                          bias=eps_col[0:1, 0:1], scale=1.0), inc=sq)
    P.wait("vector", sq)
    P.op("vector", lambda e: e.reciprocal(rstd_row, var_row))
    P.op("vector", lambda e: e.tensor_mul(mr_row, m_row, rstd_row), inc=vv)
    P.wait("tensor", vv)
    P.op("tensor", lambda e: e.matmul(banks[2][:, 0:N], sb["ones_row_f_sb"][0:1, :],
                                      rstd_row, start=True, stop=True))
    P.op("tensor", lambda e: e.matmul(banks[1][:, 0:N], sb["ones_row_f_sb"][0:1, :],
                                      mr_row, start=True, stop=True), inc=mm)
    P.wait("vector", mm)
    for ft in range(nf):
        dst = out_fn(ft)
        src = rbuf[:, ft * rT + t0: ft * rT + t0 + N]
        P.op("vector", lambda e, s=src: e.tensor_mul(s, s, banks[2][:, 0:N]))
        P.op("vector", lambda e, s=src, d=dst: e.tensor_sub(d, s, banks[1][:, 0:N]),
             inc=done_ctr)



class ActLoader:

    def __init__(self, P, ctx, name, nslots=3, width=512, eng="gpsimd", dtype=F32):
        self.P = P
        self.eng = eng
        self.bufs = [ctx.enter_context(P.nc.sbuf_tensor(f"{name}{i}", [128, width], dtype))
                     for i in range(nslots)]
        self.lds = [P.ctr(f"{name}_ld{i}") for i in range(nslots)]
        self.cons = P.ctr(name + "_cons")
        self.n = nslots
        self.seq = self.cons.val

    def load(self, dram2d, r0, c0, ncols):
        j = self.seq
        self.seq += 1
        sl = j % self.n
        if j >= self.n:
            self.P.wait(self.eng, self.cons, j - self.n + 1)
        emit_dma(self.P, self.eng, self.bufs[sl][:, 0:ncols],
                 dram2d[r0:r0 + 128, c0:c0 + ncols], self.lds[sl])
        return self.bufs[sl], [(self.lds[sl], self.lds[sl].val)]


def load_cast_slice(P, loader, dram2d, c0, dest, destT, ncols, nf=16):
    for ft in range(nf):
        buf, toks = loader.load(dram2d, ft * 128, c0, ncols)
        P.wait_tok("vector", toks)
        P.op("vector", lambda e, o=dest[:, ft * destT:ft * destT + ncols], i=buf[:, 0:ncols]:
             e.tensor_copy(out=o, in_=i), inc=loader.cons)
    return [tok(loader.cons)]


def rope_norm(P, sb, banks, *, src, Cs, Ss, qs, tmp, sq, rows, scalemat, out_bf,
              wrow, eps_col, extra_scale, half, N, statbank, bcbank, vv, ev,
              lookback=0):
    rn0 = P.ctr("rn_mm")
    P.wait("vector", rn0, max(0, rn0.val - lookback))
    P.wait("scalar", rn0, max(0, rn0.val - lookback))
    P.op("vector", lambda e: e.tensor_mul(qs[0:half, 0:N], src[half:2 * half, 0:N],
                                          Ss[0:half, 0:N]))
    P.op("vector", lambda e: e.tensor_mul(qs[half:2 * half, 0:N], src[0:half, 0:N],
                                          Ss[half:2 * half, 0:N]))
    P.op("vector", lambda e: e.tensor_mul(tmp[:, 0:N], src[:, 0:N], Cs[:, 0:N]))
    P.op("vector", lambda e: e.tensor_add(tmp[:, 0:N], tmp[:, 0:N], qs[:, 0:N]), inc=vv)
    P.wait("scalar", vv)
    P.op("scalar", lambda e: e.activation(out=sq[:, 0:N], in_=tmp[:, 0:N], func=AF.Square,
                                          bias=0.0, scale=1.0), inc=ev)
    P.wait_tok("tensor", [tok(ev)])
    P.op("tensor", lambda e: e.matmul(statbank[0:1, 0:N], sb["ones_col_f_sb"][:, 0:1],
                                      sq[:, 0:N], start=True, stop=True), inc=P.ctr("rn_mm"))
    rn = P.ctr("rn_mm")
    P.wait("scalar", rn)
    if eps_col is not None:
        P.op("scalar", lambda e: e.activation(out=rows[0:1, 0:N], in_=statbank[0:1, 0:N],
                                              func=AF.Sqrt, bias=eps_col[0:1, 0:1],
                                              scale=1.0), inc=ev)
        P.wait("vector", ev)
        P.op("vector", lambda e: e.reciprocal(rows[0:1, 0:N], rows[0:1, 0:N]))
    else:
        P.op("scalar", lambda e: e.activation(out=rows[0:1, 0:N], in_=statbank[0:1, 0:N],
                                              func=AF.Sqrt, bias=0.0, scale=1.0), inc=ev)
        P.wait("vector", ev)
        P.op("vector", lambda e: e.tensor_scalar_add(rows[0:1, 0:N], rows[0:1, 0:N], EPS1))
        P.op("vector", lambda e: e.reciprocal(rows[0:1, 0:N], rows[0:1, 0:N]))
    if extra_scale != 1.0:
        P.op("vector", lambda e: e.tensor_scalar_mul(rows[0:1, 0:N], rows[0:1, 0:N],
                                                     float(extra_scale)))
    P.op("vector", lambda e: e.nop(), inc=vv)
    P.wait_tok("tensor", [tok(vv)])
    P.op("tensor", lambda e: e.matmul(bcbank[:, 0:N], wrow[0:1, :], rows[0:1, 0:N],
                                      start=True, stop=True), inc=rn)
    P.wait("scalar", rn)
    P.op("scalar", lambda e: e.activation(out=scalemat[:, 0:N], in_=bcbank[:, 0:N],
                                          func=AF.Identity, bias=0.0, scale=1.0), inc=ev)
    P.wait("vector", ev)
    P.op("vector", lambda e: e.tensor_mul(out_bf[:, 0:N], tmp[:, 0:N], scalemat[:, 0:N]),
         inc=vv)



def _bf(x):
    return np.ascontiguousarray(np.asarray(x, dtype=np.float32).astype(BF))


def _f32(x):
    return np.ascontiguousarray(np.asarray(x, dtype=np.float32))


class BiasPack:
    def __init__(self):
        self.cols = []
        self.index = {}

    def add(self, name, vec):
        vec = np.asarray(vec, dtype=np.float32).reshape(-1)
        assert vec.size % 128 == 0
        k = vec.size // 128
        self.index[name] = (len(self.cols), k)
        for i in range(k):
            self.cols.append(vec[i * 128:(i + 1) * 128])

    def array(self):
        return np.ascontiguousarray(np.stack(self.cols, axis=1).astype(np.float32))

    @property
    def array_cols(self):
        return len(self.cols)

    def col(self, name, i=0):
        base, k = self.index[name]
        assert i < k
        return base + i


def host_prep(inputs):
    h = np.asarray(inputs["L0_attn_h"], dtype=np.float32)

    inv0 = 1.0 / (10000.0 ** (np.arange(0, ROPE0, 2, dtype=np.float32) / ROPE0))
    inv1 = 1.0 / (10000.0 ** (np.arange(0, HD1, 2, dtype=np.float32) / HD1))

    bp = BiasPack()
    bp.add("ln_eps", np.full(128, LN_EPS, np.float32))
    bp.add("rms_eps", np.full(128, 128.0 * EPS0, np.float32))
    bp.add("b_base", inputs["L0_ffn_b_base"])
    bp.add("q_norm_w", inputs["L0_attn_q_norm_w"])
    bp.add("k_norm_w", inputs["L0_attn_k_norm_w"])
    bp.add("gamma_q", inputs["L1_attn_gamma_q"])
    bp.add("gamma_k", inputs["L1_attn_gamma_k"])
    bp.add("bsu", inputs["L1_ffn_bsu"])
    bp.add("bsa", inputs["L1_ffn_bsa"])
    bp.add("bsd", inputs["L1_ffn_bsd"])
    for n in range(NEXP1):
        bp.add(f"bup{n}", np.asarray(inputs["L1_ffn_bup"])[n])
        bp.add(f"bsw{n}", np.asarray(inputs["L1_ffn_bsw"])[n])
    bpack = bp.array()

    W = {}
    W["w_qc"] = _f32(inputs["L0_attn_w_qc"])
    W["w_qup"] = _f32(inputs["L0_attn_w_qup"])
    W["kvc_w"] = _f32(inputs["L0_attn_kvc_w"])
    W["kvc_wz"] = _f32(inputs["L0_attn_kvc_wz"])
    W["idx_pack"] = _f32(np.concatenate([np.asarray(inputs["L0_attn_idx_c_w"]),
                                         np.asarray(inputs["L0_attn_idx_c_wz"])], axis=1))
    W["k_proj"] = _f32(inputs["L0_attn_k_proj_w"])
    W["v_proj"] = _f32(inputs["L0_attn_v_proj_w"])
    W["w_dq"] = _f32(inputs["L0_attn_w_dq"])
    W["w_iuq"] = _f32(inputs["L0_attn_w_iuq"])
    W["w_w"] = _f32(inputs["L0_attn_w_w"])
    W["group_w"] = _f32(np.asarray(inputs["L0_attn_group_w"]).reshape(G0 * DG, DG))
    W["final_w"] = _f32(inputs["L0_attn_final_w"])
    W["W_baseT"] = _f32(np.asarray(inputs["L0_ffn_W_base"]).T)
    W["W_routerT"] = _f32(np.asarray(inputs["L0_ffn_W_router"]).T)
    A_w = np.asarray(inputs["L0_ffn_A_w"])
    W["A_allT"] = _f32(np.transpose(A_w, (2, 0, 1)).reshape(D, E0 * R0))
    B_w = np.asarray(inputs["L0_ffn_B_w"])
    ballt = np.transpose(B_w, (0, 2, 1)).reshape(E0 * R0, D)
    W["B_allT_pad"] = _f32(np.concatenate([ballt, np.zeros((128 - E0 * R0, D), np.float32)], axis=0))
    W["wqT"] = _f32(np.asarray(inputs["L1_attn_wq"]).T)
    W["wkT"] = _f32(np.asarray(inputs["L1_attn_wk"]).T)
    W["wvT"] = _f32(np.asarray(inputs["L1_attn_wv"]).T)
    W["woT"] = _f32(np.asarray(inputs["L1_attn_wo"]).T)
    W["Wg0"] = _f32(np.asarray(inputs["L1_ffn_Wg"])[0])
    W["Wg1"] = _f32(np.asarray(inputs["L1_ffn_Wg"])[1])
    W["Wup2d"] = _f32(np.asarray(inputs["L1_ffn_Wup"]).reshape(NEXP1 * D, E1))
    W["Wsw2d"] = _f32(np.asarray(inputs["L1_ffn_Wsw"]).reshape(NEXP1 * E1, E1))
    W["Wdn2d"] = _f32(np.asarray(inputs["L1_ffn_Wdn"]).reshape(NEXP1 * E1, D))
    W["Wsu"] = _f32(inputs["L1_ffn_Wsu"])
    W["Wsa"] = _f32(inputs["L1_ffn_Wsa"])
    W["Wsd"] = _f32(inputs["L1_ffn_Wsd"])
    bdn = np.asarray(inputs["L1_ffn_bdn"], dtype=np.float32)
    W["bdn_pad"] = _f32(np.concatenate([bdn, np.zeros((128 - NEXP1, D), np.float32)], axis=0))
    bg = np.asarray(inputs["L1_ffn_bg"], dtype=np.float32)
    bg0 = np.ascontiguousarray(np.tile(bg[0][None, :], (128, 1)).astype(np.float32))
    bg1 = np.ascontiguousarray(np.tile(bg[1][None, :], (128, 1)).astype(np.float32))

    kvcb = np.asarray(inputs["L0_attn_kvc_bias"], dtype=np.float32)
    kvc_bias_rep = np.ascontiguousarray(np.tile(kvcb.T, (1, 32)).astype(np.float32))
    idxb = np.asarray(inputs["L0_attn_idx_c_bias"], dtype=np.float32)
    idx_bias_rep = np.zeros((128, 512), np.float32)
    idx_bias_rep[64:128, :] = np.tile(idxb.T, (1, 32))

    ident_bf = _bf(np.eye(128, dtype=np.float32))
    ident_f = _f32(np.eye(128, dtype=np.float32))
    ones_col_f = _f32(np.ones((128, 1), np.float32))
    ones_col_bf = _bf(np.ones((128, 1), np.float32))
    ones_row_f = _f32(np.ones((1, 128), np.float32))
    ones_row_bf = _bf(np.ones((1, 128), np.float32))
    bn_row = np.ascontiguousarray(np.tile((np.arange(NBL, dtype=np.float32) * M0 + (M0 - 1))[None, :],
                                          (128, 1)))
    gqrow = _f32(np.asarray(inputs["L1_attn_gamma_q"]).reshape(1, 128))
    gkrow = _f32(np.asarray(inputs["L1_attn_gamma_k"]).reshape(1, 128))
    qnrow = _f32(np.asarray(inputs["L0_attn_q_norm_w"]).reshape(1, 128))
    knrow = _f32(np.asarray(inputs["L0_attn_k_norm_w"]).reshape(1, 128))
    selbc = np.zeros((128, 8 * 128), np.float32)
    for n_ in range(8):
        selbc[n_, n_ * 128:(n_ + 1) * 128] = 1.0
    selbc = _bf(selbc)
    i0 = np.arange(128)[:, None]
    j0 = np.arange(384)[None, :]
    genmask0 = _bf(np.where((j0 > i0) & (j0 <= i0 + WIN0), 0.0, NEG))
    j1 = np.arange(1152)[None, :]
    genmask1 = _bf(np.where((j1 >= i0) & (j1 <= i0 + WIN1), 0.0, NEG))

    per_core = []
    for core in range(NCORES):
        b = core // CHUNKS
        ch = core % CHUNKS
        s = ch * OWN
        e = s + OWN
        hT = np.ascontiguousarray(h[b].T)

        lo = s - (HALO1 + HALO0)
        hT_loc = np.zeros((D, TLOCP), np.float32)
        a0 = max(lo, 0)
        hT_loc[:, a0 - lo:TLOCP] = hT[:, a0:e]
        hT_loc = np.ascontiguousarray(hT_loc)

        pos_t1 = np.arange(s - HALO1, e, dtype=np.float32)
        pos_t1[pos_t1 < 0] = -1.0
        tpos_cols = np.ascontiguousarray(pos_t1.reshape(NSR * 4, 128).T)

        posk0 = np.arange(lo, e, dtype=np.float32)
        kval0 = _f32(np.where(posk0 < 0, NEG, 0.0)[None, :])
        posk1 = np.arange(s - HALO1, e, dtype=np.float32)
        kval1 = _f32(np.where(posk1 < 0, NEG, 0.0)[None, :])

        p0 = np.maximum(np.arange(lo, e, dtype=np.float32), 0.0)
        ang0 = p0[None, :] * inv0[:, None]
        cos0 = np.cos(ang0)
        sin0 = np.sin(ang0)
        C0 = np.ones((128, TLOCP), np.float32)
        C0[0:32, :] = cos0
        C0[32:64, :] = cos0
        S0 = np.zeros((128, TLOCP), np.float32)
        S0[0:32, :] = -sin0
        S0[32:64, :] = sin0
        p1 = np.maximum(np.arange(s - HALO1, e, dtype=np.float32), 0.0)
        ang1 = p1[None, :] * inv1[:, None]
        cos1 = np.cos(ang1)
        sin1 = np.sin(ang1)
        C1f = np.empty((128, T1), np.float32)
        C1f[0:64, :] = cos1
        C1f[64:128, :] = cos1
        S1f = np.empty((128, T1), np.float32)
        S1f[0:64, :] = -sin1
        S1f[64:128, :] = sin1

        d = {
            "hT_full": hT, "hT_loc": hT_loc,
            "bpack": bpack, "bg0": bg0, "bg1": bg1,
            "kvc_bias_rep": kvc_bias_rep, "idx_bias_rep": idx_bias_rep,
            "ident_bf": ident_bf, "ident_f": ident_f,
            "ones_col_f": ones_col_f, "ones_col_bf": ones_col_bf,
            "ones_row_f": ones_row_f, "ones_row_bf": ones_row_bf,
            "bn_row": bn_row, "genmask0": genmask0, "genmask1": genmask1, "selbc": selbc,
            "gqrow": gqrow, "gkrow": gkrow, "qnrow": qnrow, "knrow": knrow,
            "tpos_cols": tpos_cols, "kval0": kval0, "kval1": kval1,
            "kval1b": _bf(kval1), "kval0b": _bf(kval0),
            "C0": _f32(C0), "S0": _f32(S0), "C1": _f32(C1f), "S1": _f32(S1f),
        }
        d.update(W)
        per_core.append(d)
    return per_core, bp



CONST_SPECS = [
    ("bg0", [128, 8], F32), ("bg1", [128, 8], F32),
    ("ident_bf", [128, 128], BF16), ("ident_f", [128, 128], F32),
    ("ones_col_f", [128, 1], F32), ("ones_col_bf", [128, 1], BF16),
    ("ones_row_f", [1, 128], F32), ("ones_row_bf", [1, 128], BF16),
    ("bn_row", [128, 512], F32), ("genmask0", [128, 384], BF16),
    ("genmask1", [128, 1152], BF16), ("tpos_cols", [128, NSR * 4], F32),
    ("kval0", [1, TLOCP], F32), ("kval1", [1, T1], F32),
    ("C0", [128, TLOCP], F32), ("S0", [128, TLOCP], F32),
    ("C1", [128, T1], F32), ("S1", [128, T1], F32),
    ("kvc_bias_rep", [128, 512], F32), ("idx_bias_rep", [128, 512], F32),
    ("selbc", [128, 8 * 128], BF16),
    ("gqrow", [1, 128], F32), ("gkrow", [1, 128], F32),
    ("qnrow", [1, 128], F32), ("knrow", [1, 128], F32),
]


def load_consts(P, ctx, nc, names, bp_cols, with_bpack=True):
    sb = {}
    ld = P.ctr("const_ld")
    specs = [s for s in CONST_SPECS if s[0] in names]
    if with_bpack:
        specs.append(("bpack", [128, bp_cols], F32))
    for nm, shape, dt in specs:
        dr = nc.dram_tensor(nm, list(shape), dt, kind="ExternalInput").ap()
        t = ctx.enter_context(nc.sbuf_tensor(nm + "_sb", list(shape), dt))
        emit_dma(P, "sync", t[:], dr, ld)
        sb[nm + "_sb"] = t
    for e in ("tensor", "vector", "scalar", "gpsimd"):
        P.wait(e, ld)
    return sb


def emit_wconv(P, ctx, jobs, done_ctr):
    nstage = 4
    stg = [ctx.enter_context(P.nc.sbuf_tensor(f"wcs{i}", [128, 2048], F32)) for i in range(nstage)]
    bst = [ctx.enter_context(P.nc.sbuf_tensor(f"wcb{i}", [128, 2048], BF16)) for i in range(nstage)]
    lds = [P.ctr(f"wconv_ld{i}") for i in range(nstage)]
    sts = [P.ctr(f"wconv_st{i}") for i in range(nstage)]
    cast = P.ctr("wconv_cast")
    i = 0
    for (src, dst, Rr, Cc) in jobs:
        for r0 in range(0, Rr, 128):
            sl = i % nstage
            s_ = stg[sl]
            b_ = bst[sl]
            if i >= nstage:
                P.wait("gpsimd", cast, i - nstage + 1)
            emit_dma(P, "gpsimd", s_[:, 0:Cc], src[r0:r0 + 128, 0:Cc], lds[sl])
            P.wait("vector", lds[sl])
            if i >= nstage:
                P.wait("vector", sts[sl])
            P.op("vector", lambda e, o=b_[:, 0:Cc], ii=s_[:, 0:Cc]: e.tensor_copy(out=o, in_=ii),
                 inc=cast)
            P.wait("sync", cast)
            P.op("sync", lambda e, d=dst[r0:r0 + 128, 0:Cc], s2=b_[:, 0:Cc]:
                 e.dma_start(out=d, in_=s2), inc=sts[sl], k=16, dma=True)
            done_ctr.val += 0
            i += 1
    for sl in range(nstage):
        P.wait("gpsimd", sts[sl])
        P.wait("sync", sts[sl])
    P.op("gpsimd", lambda e: e.nop(), inc=done_ctr)
    return i


P3_NUCLEAR = [False]


def emit_p3(P, ctx, io, sb, banks, bp, ws):
    nc = P.nc

    def nbar():
        if P3_NUCLEAR[0]:
            P.barrier()
    dbg = io.get("dbg")
    dbg_ctr = P.ctr("dbg") if dbg is not None else None

    def dump(idx, ap, cols=512):
        if dbg is None:
            return
        P.barrier()
        emit_dma(P, "gpsimd", dbg[:, idx * 512: idx * 512 + cols], ap, dbg_ctr)
        P.wait("gpsimd", dbg_ctr)
        P.barrier()

    x2T, xtT, outT = io["x2T"], io["xtT"], io["outT"]
    WupB, WswB, WdnB = io["WupB"], io["WswB"], io["WdnB"]
    WsuB, WsaB, WsdB = io["WsuB"], io["WsaB"], io["WsdB"]
    bpk = sb["bpack_sb"]

    xsl = ctx.enter_context(nc.sbuf_tensor("p3_xsl", [128, 16 * TCH], BF16))
    hn = ctx.enter_context(nc.sbuf_tensor("p3_hn", [128, 2 * 8 * TCH], BF16))
    zb = ctx.enter_context(nc.sbuf_tensor("p3_zb", [128, 2 * 8 * TCH], BF16))
    HH = ctx.enter_context(nc.sbuf_tensor("p3_hh", [128, NEXP1 * 8 * TCH], BF16))
    hsh = ctx.enter_context(nc.sbuf_tensor("p3_hsh", [128, 16 * TCH], BF16))
    w6p = ctx.enter_context(nc.sbuf_tensor("p3_w6p", [128, TCH], BF16))
    wbsb = ctx.enter_context(nc.sbuf_tensor("p3_wbsb", [128, 2 * TCH], BF16))
    gt = ctx.enter_context(nc.sbuf_tensor("p3_gt", [128, 8], F32))
    gsc = ctx.enter_context(nc.sbuf_tensor("p3_gsc", [128, 8], F32))
    gcol = ctx.enter_context(nc.sbuf_tensor("p3_gcol", [128, 4], F32))
    rbuf = ctx.enter_context(nc.sbuf_tensor("p3_rbuf", [128, 16 * TCH], F32))
    tmpf = ctx.enter_context(nc.sbuf_tensor("p3_tmpf", [128, 2 * TCH], F32))
    sqs = ctx.enter_context(nc.sbuf_tensor("p3_sqs", [128, 2 * 512], F32))
    rows = ctx.enter_context(nc.sbuf_tensor("p3_rows", [1, 3 * 512], F32))
    x2s = [ctx.enter_context(nc.sbuf_tensor(f"p3_x2s{i}", [128, TCH], BF16)) for i in range(3)]
    wg = [sb["Wg0_bf"], sb["Wg1_bf"]]
    bdnp = sb["bdn_pad_bf"]

    xld = P.ctr("xld")
    x2lds = [P.ctr(f"p3_x2ld{i}") for i in range(3)]
    cst = P.ctr("cst")
    ev = P.ctr("ev")
    vv = P.ctr("vv")
    st = P.ctr("st")
    x2c = P.ctr("x2c")
    x2seq = [x2c.val]

    P.op("gpsimd", lambda e: e.memset(w6p[:], 0.0), inc=P.ctr("init"))
    P.wait("scalar", P.ctr("init"))

    def x2_stage(ft, c0):
        j = x2seq[0]
        x2seq[0] += 1
        sl = j % 3
        buf = x2s[sl]
        if j >= 3:
            P.wait("gpsimd", x2c, j - 2)
        emit_dma(P, "gpsimd", buf[:, 0:TCH], x2T[ft * 128:(ft + 1) * 128, c0:c0 + TCH], x2lds[sl])
        return buf, [(x2lds[sl], x2lds[sl].val)]

    def load_xslice(j, c0):
        if j == 0:
            P.wait("vector", st)
            for ft in range(16):
                buf, toks = x2_stage(ft, c0)
                P.wait_tok("vector", toks)
                P.op("vector", lambda e, o=xsl[:, ft * TCH:(ft + 1) * TCH], i=buf[:, 0:TCH]:
                     e.tensor_copy(out=o, in_=i), inc=x2c)
            P.op("vector", lambda e: e.nop(), inc=cst)
        else:
            P.wait("gpsimd", st)
            P.wait("vector", st)
            for ft in range(16):
                emit_dma(P, "gpsimd", xsl[:, ft * TCH:(ft + 1) * TCH],
                         xtT[ft * 128:(ft + 1) * 128, c0:c0 + TCH], xld)
        return [(xld, xld.val), (cst, cst.val)]

    for j in range(C1):
        for tch in range(NTCH):
            c0 = tch * TCH
            xtoks = load_xslice(j, c0)
            P.barrier()
            if j == 0 and tch == 0:
                dump(0, xsl[:, 0:512])

            for tt in range(4):
                chunks = [{"lhsT": (lambda fl, ft, kc=kc, tt=tt:
                                    xsl[:, kc * TCH + tt * 128: kc * TCH + tt * 128 + 128]),
                           "rhs": wg[j][:, kc * 8:(kc + 1) * 8],
                           "toks": xtoks if kc == 0 else ()} for kc in range(16)]
                mmt = gemm(P, f"g{j}{tch}{tt}", chunks=chunks, ftiles=[0], N=8,
                           banks=[banks[0]], evac=None, pre_toks=[tok(ev), tok(vv)])
                bgt = sb["bg0_sb"] if j == 0 else sb["bg1_sb"]
                P.wait_tok("vector", [mmt])
                P.op("vector", lambda e, b2=bgt: e.tensor_add(gt[:, 0:8], banks[0][:, 0:8], b2[:, 0:8]))
                P.op("vector", lambda e: e.tensor_reduce(gcol[:, 0:1], gt[:, 0:8], AX.X, ALU.min))
                P.op("vector", lambda e: e.tensor_scalar(out=gsc[:, 0:8], in0=gt[:, 0:8],
                                                         scalar1=gcol[:, 0:1], scalar2=None,
                                                         op0=ALU.is_equal))
                P.op("vector", lambda e: e.scalar_tensor_tensor(out=gsc[:, 0:8], in0=gsc[:, 0:8],
                                                                scalar=1.0e30, in1=gt[:, 0:8],
                                                                op0=ALU.mult, op1=ALU.add))
                P.op("vector", lambda e: e.tensor_reduce(gcol[:, 1:2], gsc[:, 0:8], AX.X, ALU.min))
                P.op("vector", lambda e: e.tensor_scalar(out=gsc[:, 0:8], in0=gt[:, 0:8],
                                                         scalar1=gcol[:, 1:2], scalar2=None,
                                                         op0=ALU.is_gt))
                P.op("vector", lambda e: e.reduce_max(gcol[:, 2:3], gt[:, 0:8], axis=AX.X,
                                                      negate=True), inc=vv)
                P.wait("scalar", vv)
                P.op("scalar", lambda e: e.activation(out=gt[:, 0:8], in_=gt[:, 0:8], func=AF.Exp,
                                                      bias=gcol[:, 2:3], scale=1.0), inc=ev)
                P.wait("vector", ev)
                P.op("vector", lambda e: e.tensor_mul(gt[:, 0:8], gt[:, 0:8], gsc[:, 0:8]))
                P.op("vector", lambda e: e.reduce_sum(gcol[:, 3:4], gt[:, 0:8], axis=AX.X))
                P.op("vector", lambda e: e.reciprocal(gcol[:, 3:4], gcol[:, 3:4]))
                P.op("vector", lambda e: e.tensor_scalar_mul(gt[:, 0:8], gt[:, 0:8], gcol[:, 3:4]),
                     inc=vv)
                P.wait_tok("tensor", [tok(vv), tok(ev)])
                P.op("tensor", lambda e: e.transpose(banks[1][0:8, 0:128], gt[:, 0:8],
                                                     sb["ident_f_sb"][:]), inc=P.ctr("tp"))
                P.wait("scalar", P.ctr("tp"))
                P.op("scalar", lambda e, tt=tt: e.activation(out=w6p[0:6, tt * 128:(tt + 1) * 128],
                                                             in_=banks[1][0:6, 0:128],
                                                             func=AF.Copy, bias=0.0, scale=1.0),
                     inc=ev)
            evt_gates = tok(ev)
            if j == 0 and tch == 0:
                dump(1, w6p[:, 0:512])

            wavetok = [None, None]
            for qw in range(4):
                par = qw % 2
                fts = list(range(qw * 4, (qw + 1) * 4))
                chunks = []
                for kc in range(16):
                    ch_ = ws.chunk(WsuB, kc * 128, qw * 512, 512,
                                   row1=((kc + 1) * 128 if kc % 2 == 0 else None))
                    ch_["lcols"] = lambda fl, ft: fl * 128
                    ch_["rhs"] = xsl[:, kc * TCH: kc * TCH + TCH]
                    if kc == 0:
                        ch_["toks"] = ch_["toks"] + xtoks
                    chunks.append(ch_)

                def ev_su(fl, ft, ps):
                    P.op("scalar", lambda e, o=hsh[:, ft * TCH:(ft + 1) * TCH], i=ps,
                         bcol=bpk[:, bp.col("bsu", ft):bp.col("bsu", ft) + 1]:
                         e.activation(out=o, in_=i, func=AF.Identity, bias=bcol, scale=1.0), inc=ev)

                pre = [wavetok[par]] if wavetok[par] else ([tok(ev), tok(vv)] if qw == 0 else [])
                gemm(P, f"su{j}{tch}{qw}", chunks=chunks, ftiles=fts, N=TCH,
                     banks=banks[par * 4:(par + 1) * 4], evac=ev_su, pre_toks=pre)
                wavetok[par] = tok(ev)
            evt_su = tok(ev)
            if j == 0 and tch == 0:
                P.wait_tok("gpsimd", [evt_su])
                dump(2, hsh[:, 0:512])
            for s_ in range(S1):
                P.wait("scalar", vv)
                for qw in range(2):
                    par = qw % 2
                    fts = list(range(qw * 4, (qw + 1) * 4))
                    chunks = []
                    for kc in range(8):
                        ch_ = ws.chunk(WsaB, kc * 128, qw * 512, 512,
                                       row1=((kc + 1) * 128 if kc % 2 == 0 else None))
                        ch_["lcols"] = lambda fl, ft: fl * 128
                        ch_["rhs"] = hsh[:, (s_ * 8 + kc) * TCH: (s_ * 8 + kc) * TCH + TCH]
                        if kc == 0:
                            ch_["toks"] = ch_["toks"] + [evt_su]
                        chunks.append(ch_)

                    def ev_sa(fl, ft, ps):
                        P.op("scalar", lambda e, o=zb[:, ft * TCH:(ft + 1) * TCH], i=ps,
                             bcol=bpk[:, bp.col("bsa", ft):bp.col("bsa", ft) + 1]:
                             e.activation(out=o, in_=i, func=AF.Silu, bias=bcol, scale=1.0),
                             inc=ev)

                    pre = [wavetok[par]] if wavetok[par] else [tok(ev), tok(vv)]
                    gemm(P, f"sa{j}{tch}{s_}{qw}", chunks=chunks, ftiles=fts, N=TCH,
                         banks=banks[par * 4:(par + 1) * 4], evac=ev_sa, pre_toks=pre)
                    wavetok[par] = tok(ev)
                P.wait_tok("vector", [tok(ev)])
                for ft in range(8):
                    gf = s_ * 8 + ft
                    P.op("vector", lambda e, a=zb[:, ft * TCH:(ft + 1) * TCH],
                         b2=hsh[:, gf * TCH: gf * TCH + TCH]:
                         e.tensor_mul(b2, a, b2), inc=vv)
            evt_hsh = tok(vv)
            if j == 0 and tch == 0:
                dump(3, hsh[:, 0:512])

            mult_done = [vv.val, vv.val]
            for n in range(NEXP1):
                par = n % 2
                po = par * 8 * TCH
                def ev_up(fl, ft, ps, n=n, po=po, par=par):
                    P.wait("scalar", vv, mult_done[par])
                    P.op("scalar", lambda e, o=hn[:, po + ft * TCH: po + (ft + 1) * TCH], i=ps,
                         bcol=bpk[:, bp.col(f"bup{n}", ft):bp.col(f"bup{n}", ft) + 1]:
                         e.activation(out=o, in_=i, func=AF.Identity, bias=bcol, scale=1.0), inc=ev)

                for qw in range(2):
                    bpar = qw % 2
                    chunks = []
                    for kc in range(16):
                        ch_ = ws.chunk(WupB, n * D + kc * 128, qw * 512, 512,
                                       row1=(n * D + (kc + 1) * 128 if kc % 2 == 0 else None))
                        ch_["lcols"] = lambda fl, ft: fl * 128
                        ch_["rhs"] = xsl[:, kc * TCH: kc * TCH + TCH]
                        if kc == 0 and qw == 0:
                            ch_["toks"] = ch_["toks"] + xtoks
                        chunks.append(ch_)
                    pre = [wavetok[bpar]] if wavetok[bpar] else [tok(ev)]
                    gemm(P, f"up{j}{tch}{n}{qw}", chunks=chunks,
                         ftiles=list(range(qw * 4, (qw + 1) * 4)), N=TCH,
                         banks=banks[bpar * 4:(bpar + 1) * 4], evac=ev_up, pre_toks=pre)
                    wavetok[bpar] = tok(ev)
                evt_up = tok(ev)
                def ev_sw(fl, ft, ps, n=n, po=po, par=par):
                    P.wait("scalar", vv, mult_done[par])
                    P.op("scalar", lambda e, o=zb[:, po + ft * TCH: po + (ft + 1) * TCH], i=ps,
                         bcol=bpk[:, bp.col(f"bsw{n}", ft):bp.col(f"bsw{n}", ft) + 1]:
                         e.activation(out=o, in_=i, func=AF.Silu, bias=bcol, scale=1.0), inc=ev)

                for qw in range(2):
                    bpar = qw % 2
                    chunks = []
                    for kc in range(8):
                        ch_ = ws.chunk(WswB, n * E1 + kc * 128, qw * 512, 512,
                                       row1=(n * E1 + (kc + 1) * 128 if kc % 2 == 0 else None))
                        ch_["lcols"] = lambda fl, ft: fl * 128
                        ch_["rhs"] = hn[:, po + kc * TCH: po + kc * TCH + TCH]
                        if kc == 0 and qw == 0:
                            ch_["toks"] = ch_["toks"] + [evt_up]
                        chunks.append(ch_)
                    pre = [wavetok[bpar]] if wavetok[bpar] else [tok(ev)]
                    gemm(P, f"sw{j}{tch}{n}{qw}", chunks=chunks,
                         ftiles=list(range(qw * 4, (qw + 1) * 4)), N=TCH,
                         banks=banks[bpar * 4:(bpar + 1) * 4], evac=ev_sw, pre_toks=pre)
                    wavetok[bpar] = tok(ev)
                evt_sw = tok(ev)
                P.wait_tok("tensor", [evt_gates, tok(ev)])
                P.op("tensor", lambda e, n=n: e.matmul(banks[0][:, 0:TCH],
                                                       sb["selbc_sb"][:, n * 128:(n + 1) * 128],
                                                       w6p[:, 0:TCH], start=True, stop=True),
                     inc=P.ctr("bc"))
                P.wait("scalar", P.ctr("bc"))
                P.op("scalar", lambda e, par=par: e.activation(
                     out=wbsb[:, par * TCH:(par + 1) * TCH], in_=banks[0][:, 0:TCH],
                     func=AF.Copy, bias=0.0, scale=1.0), inc=ev)
                evt_wb = tok(ev)
                wavetok[0] = evt_wb
                P.wait_tok("vector", [evt_sw, evt_wb])
                for ft in range(8):
                    gf = n * 8 + ft
                    P.op("vector", lambda e, a=zb[:, po + ft * TCH: po + (ft + 1) * TCH],
                         b2=hn[:, po + ft * TCH: po + (ft + 1) * TCH]:
                         e.tensor_mul(a, a, b2))
                    P.op("vector", lambda e, a=zb[:, po + ft * TCH: po + (ft + 1) * TCH],
                         o=HH[:, gf * TCH: gf * TCH + TCH], par=par:
                         e.tensor_mul(o, a, wbsb[:, par * TCH:(par + 1) * TCH]), inc=vv)
                mult_done[par] = vv.val
                if j == 0 and tch == 0 and n == 0:
                    dump(4, hn[:, 0:512])
                    dump(5, zb[:, 0:512])
                    dump(6, wbsb[:, 0:512])
                    dump(7, HH[:, 0:512])
            evt_hh = tok(vv)

            lnd = P.ctr("ln")
            x2tok0 = tok(P.ctr("x2c"))
            for qw in range(4):
                par = qw % 2
                fts = list(range(qw * 4, (qw + 1) * 4))
                chunks = []
                for n in range(NEXP1):
                    for e_ in range(8):
                        ch_ = ws.chunk(WdnB, n * E1 + e_ * 128, qw * 512, 512,
                                   row1=(n * E1 + (e_ + 1) * 128 if e_ % 2 == 0 else None))
                        ch_["lcols"] = lambda fl, ft: fl * 128
                        ch_["rhs"] = HH[:, (n * 8 + e_) * TCH:(n * 8 + e_) * TCH + TCH]
                        if n == 0 and e_ == 0 and qw == 0:
                            ch_["toks"] = ch_["toks"] + [evt_hh]
                        chunks.append(ch_)
                for kc in range(16):
                    ch_ = ws.chunk(WsdB, kc * 128, qw * 512, 512,
                                   row1=((kc + 1) * 128 if kc % 2 == 0 else None))
                    ch_["lcols"] = lambda fl, ft: fl * 128
                    ch_["rhs"] = hsh[:, kc * TCH: kc * TCH + TCH]
                    if kc == 0 and qw == 0:
                        ch_["toks"] = ch_["toks"] + [evt_hsh]
                    chunks.append(ch_)
                chunks.append({"lhsT": (lambda fl, ft: bdnp[:, ft * 128: ft * 128 + 128]),
                               "rhs": w6p[:, 0:TCH],
                               "toks": ([evt_gates] if qw == 0 else [])})

                def ev_dn(fl, ft, ps, j=j, c0=c0):
                    par = (ft % 2) * TCH
                    P.wait("scalar", x2c, max(0, x2c.val - 1))
                    P.op("scalar", lambda e, o=tmpf[:, par:par + TCH], i=ps,
                         bcol=bpk[:, bp.col("bsd", ft):bp.col("bsd", ft) + 1]:
                         e.activation(out=o, in_=i, func=AF.Identity, bias=bcol, scale=1.0), inc=ev)
                    buf, toks = x2_stage(ft, c0)
                    P.wait_tok("vector", [tok(ev)] + toks)
                    if j == 0:
                        P.op("vector", lambda e, s2=buf, o=rbuf[:, ft * TCH:(ft + 1) * TCH]:
                             e.tensor_add(o, tmpf[:, par:par + TCH], s2[:, 0:TCH]), inc=x2c)
                        P.op("vector", lambda e, o=xsl[:, ft * TCH:(ft + 1) * TCH],
                             i=rbuf[:, ft * TCH:(ft + 1) * TCH]:
                             e.tensor_copy(out=o, in_=i), inc=vv)
                        P.wait("gpsimd", vv)
                        emit_dma(P, "gpsimd", xtT[ft * 128:(ft + 1) * 128, c0:c0 + TCH],
                                 xsl[:, ft * TCH:(ft + 1) * TCH], st)
                    else:
                        P.op("vector", lambda e, s2=buf, o=rbuf[:, ft * TCH:(ft + 1) * TCH]:
                             e.scalar_tensor_tensor(out=o, in0=s2[:, 0:TCH], scalar=2.0,
                                                    in1=tmpf[:, par:par + TCH],
                                                    op0=ALU.mult, op1=ALU.add), inc=x2c)

                pre = ([wavetok[par]] if wavetok[par] else [tok(ev)]) + [x2tok0]
                gemm(P, f"dn{j}{tch}{qw}", chunks=chunks, ftiles=fts, N=TCH,
                     banks=banks[par * 4:(par + 1) * 4], evac=ev_dn, pre_toks=pre)
                wavetok[par] = tok(ev)
            if j == 0 and tch == 0:
                dump(8, xsl[:, 0:512])
                dump(9, xsl[:, 15 * TCH: 15 * TCH + 512])

            if j == 1:
                layer_norm(P, f"p3ln{tch}", rbuf=rbuf, rT=TCH, t0=0, N=TCH,
                           out_fn=lambda ft: rbuf[:, ft * TCH:(ft + 1) * TCH],
                           out_dtype_note="f32", banks=banks[5:8], sb=sb,
                           sq_scratch=sqs, rows=rows, done_ctr=lnd,
                           r_toks=[tok(x2c)],
                           eps_col=bpk[:, bp.col("ln_eps"):bp.col("ln_eps") + 1])
                P.wait("gpsimd", lnd)
                for ft in range(16):
                    emit_dma(P, "gpsimd", outT[ft * 128:(ft + 1) * 128, c0:c0 + TCH],
                             rbuf[:, ft * TCH:(ft + 1) * TCH], st)
            P.barrier()

    P.wait("sync", st)
    P.wait("gpsimd", st)



P3_WEIGHTS = ["Wup2d", "Wsw2d", "Wdn2d", "Wsu", "Wsa", "Wsd", "Wg0", "Wg1", "bdn_pad"]
P3_CONSTS = ["bg0", "bg1", "ident_f", "ones_col_f", "ones_col_bf", "ones_row_f",
             "ones_row_bf", "selbc"]


def build_launch_p3(bp, x2_kind="ExternalInput", dbg=False):
    nc = bass.Bass("TRN2", target_bir_lowering=False)
    ctx = ExitStack()
    P = Prog(nc, ctx)
    with ctx:
        io = {}
        if dbg:
            io["dbg"] = nc.dram_tensor("dbg", [128, 16 * 512], BF16, kind="ExternalOutput").ap()
        io["x2T"] = nc.dram_tensor("x2T", [D, OWN], BF16, kind=x2_kind).ap()
        io["outT"] = nc.dram_tensor("outT", [D, OWN], F32, kind="ExternalOutput").ap()
        io["xtT"] = nc.dram_tensor("xtT", [D, OWN], BF16,
                                   kind="ExternalOutput" if dbg else "Internal").ap()
        wsrc = {}
        for nm, shape in [("Wup2d", [NEXP1 * D, E1]), ("Wsw2d", [NEXP1 * E1, E1]),
                          ("Wdn2d", [NEXP1 * E1, D]), ("Wsu", [D, D]), ("Wsa", [E1, E1]),
                          ("Wsd", [D, D]), ("Wg0", [D, 8]), ("Wg1", [D, 8]),
                          ("bdn_pad", [128, D])]:
            wsrc[nm] = nc.dram_tensor(nm, list(shape), F32, kind="ExternalInput").ap()
        io["WupB"] = nc.dram_tensor("WupB", [NEXP1 * D, E1], BF16, kind="Internal").ap()
        io["WswB"] = nc.dram_tensor("WswB", [NEXP1 * E1, E1], BF16, kind="Internal").ap()
        io["WdnB"] = nc.dram_tensor("WdnB", [NEXP1 * E1, D], BF16, kind="Internal").ap()
        io["WsuB"] = nc.dram_tensor("WsuB", [D, D], BF16, kind="Internal").ap()
        io["WsaB"] = nc.dram_tensor("WsaB", [E1, E1], BF16, kind="Internal").ap()
        io["WsdB"] = nc.dram_tensor("WsdB", [D, D], BF16, kind="Internal").ap()

        sb = load_consts(P, ctx, nc, P3_CONSTS, bp.array_cols)
        banks = [ctx.enter_context(nc.psum_tensor(f"bank{i}", [128, 512], F32)) for i in range(8)]

        sb["bdn_pad_bf"] = ctx.enter_context(nc.sbuf_tensor("bdnp_sb", [128, D], BF16))
        sb["Wg0_bf"] = ctx.enter_context(nc.sbuf_tensor("wg0_sb", [128, 16 * 8], BF16))
        sb["Wg1_bf"] = ctx.enter_context(nc.sbuf_tensor("wg1_sb", [128, 16 * 8], BF16))
        with ExitStack() as wctx:
            smstage = wctx.enter_context(nc.sbuf_tensor("smw_stage", [128, 2048], F32))
            sm = P.ctr("smw")
            smc = P.ctr("smc")
            emit_dma(P, "sync", smstage[:, 0:D], wsrc["bdn_pad"][0:128, 0:D], sm)
            P.wait("vector", sm)
            P.op("vector", lambda e, s2=smstage: e.tensor_copy(out=sb["bdn_pad_bf"][:],
                 in_=s2[:, 0:D]), inc=smc)
            for nm, key in [("Wg0", "Wg0_bf"), ("Wg1", "Wg1_bf")]:
                P.wait("sync", smc)
                for kc in range(16):
                    emit_dma(P, "sync", smstage[:, kc * 8:(kc + 1) * 8],
                             wsrc[nm][kc * 128:(kc + 1) * 128, 0:8], sm)
                P.wait("vector", sm)
                P.op("vector", lambda e, d=sb[key]: e.tensor_copy(out=d[:], in_=smstage[:, 0:128]),
                     inc=smc)

            for e_ in ("tensor", "scalar", "gpsimd"):
                P.wait(e_, smc)
            wcd = P.ctr("wconv_done")
            jobs = [(wsrc["Wup2d"], io["WupB"], NEXP1 * D, E1),
                    (wsrc["Wsw2d"], io["WswB"], NEXP1 * E1, E1),
                    (wsrc["Wdn2d"], io["WdnB"], NEXP1 * E1, D),
                    (wsrc["Wsu"], io["WsuB"], D, D),
                    (wsrc["Wsa"], io["WsaB"], E1, E1),
                    (wsrc["Wsd"], io["WsdB"], D, D)]
            emit_wconv(P, wctx, jobs, wcd)
            P.wait("sync", wcd)

        ws = WStream(P, ctx)
        emit_p3(P, ctx, io, sb, banks, bp, ws)
        P.run()
    return nc




def emit_p2(P, ctx, io, sb, banks, bp, ws):
    nc = P.nc
    dbg = io.get("dbg")
    dbg_ctr = P.ctr("dbg") if dbg is not None else None
    x1T, x2T = io["x1T"], io["x2T"]
    wqB, wkB, wvB, woB = io["wqB"], io["wkB"], io["wvB"], io["woB"]
    bpk = sb["bpack_sb"]

    kT = [ctx.enter_context(nc.sbuf_tensor(f"p2_kT{h}", [128, T1], BF16)) for h in range(HK1)]
    vN = [ctx.enter_context(nc.sbuf_tensor(f"p2_vN{h}", [128, T1], BF16)) for h in range(HK1)]
    xs = ctx.enter_context(nc.sbuf_tensor("p2_xs", [128, 16 * 512], BF16))
    C1d = nc.dram_tensor("C1", [128, T1], F32, kind="ExternalInput").ap()
    S1d = nc.dram_tensor("S1", [128, T1], F32, kind="ExternalInput").ap()
    Cbuf = ctx.enter_context(nc.sbuf_tensor("p2_cb", [128, 512], F32))
    Sbuf = ctx.enter_context(nc.sbuf_tensor("p2_sb2", [128, 512], F32))
    sqln = ctx.enter_context(nc.sbuf_tensor("p2_sqln", [128, 1024], BF16))
    qbf = ctx.enter_context(nc.sbuf_tensor("p2_qbf", [128, 16 * 512], BF16))
    ro_qs = ctx.enter_context(nc.sbuf_tensor("p2_qs", [128, 512], F32))
    ro_t = ctx.enter_context(nc.sbuf_tensor("rt", [128, 512], F32))
    sqs = ctx.enter_context(nc.sbuf_tensor("p2_sq", [128, 512], F32))
    rows = ctx.enter_context(nc.sbuf_tensor("p2_rows", [1, 3 * 512], F32))
    smat = ctx.enter_context(nc.sbuf_tensor("p2_smat", [128, 512], F32))
    ro_qs2 = ctx.enter_context(nc.sbuf_tensor("p2_qs2", [128, 512], F32))
    ro_t2 = ctx.enter_context(nc.sbuf_tensor("rt2", [128, 512], F32))
    sqs2 = ctx.enter_context(nc.sbuf_tensor("p2_sq2", [128, 512], F32))
    rows2 = ctx.enter_context(nc.sbuf_tensor("p2_rows2", [1, 512], F32))
    smat2 = ctx.enter_context(nc.sbuf_tensor("p2_smat2", [128, 512], F32))
    cmb = ctx.enter_context(nc.sbuf_tensor("p2_cmb", [128, 1152], F32))
    msc = ctx.enter_context(nc.sbuf_tensor("p2_msc", [128, 2 * 1152], F32))
    pT = ctx.enter_context(nc.sbuf_tensor("p2_pT", [128, 2 * 9 * 128], BF16))
    att = ctx.enter_context(nc.sbuf_tensor("p2_att", [128, 16 * 512], BF16))
    rbuf = ctx.enter_context(nc.sbuf_tensor("p2_rbuf", [128, 16 * 512], BF16))
    kvald = nc.dram_tensor("kval1b", [1, T1], BF16, kind="ExternalInput").ap()
    kval = ctx.enter_context(nc.sbuf_tensor("p2_kval", [1, T1], BF16))
    ncol = ctx.enter_context(nc.sbuf_tensor("p2_ncol", [128, 4], F32))

    xl = ActLoader(P, ctx, "xlb", nslots=2, width=512, dtype=BF16)
    rl = ActLoader(P, ctx, "rlb", nslots=2, width=512, dtype=BF16)
    ev = P.ctr("ev")
    vv = P.ctr("vv")
    st = P.ctr("st")
    kld = P.ctr("p2_kld")
    emit_dma(P, "sync", kval[:], kvald, kld)
    for e_ in ("tensor", "vector", "scalar", "gpsimd"):
        P.wait(e_, kld)

    P.op("gpsimd", lambda e: e.memset(att[:], 0.0))
    csld = [P.ctr("csld0"), P.ctr("csld1")]

    def load_cs(c0):
        P.wait("gpsimd", vv)
        emit_dma(P, "gpsimd", Cbuf[:], C1d[:, c0:c0 + 512], csld[0])
        emit_dma(P, "gpsimd", Sbuf[:], S1d[:, c0:c0 + 512], csld[1])
        toks = [(csld[0], csld[0].val), (csld[1], csld[1].val)]
        P.wait_tok("vector", toks)
        P.wait_tok("scalar", toks)

    for c in range(6):
        c0 = c * 512
        load_cs(c0)
        P.wait("vector", ws.use)
        xt_toks = load_cast_slice(P, xl, x1T, c0, xs, 512, 512)
        for which, wB, dst in (("k", wkB, None), ("v", wvB, None)):
            chunks = []
            for kc in range(16):
                ch_ = ws.chunk(wB, kc * 128, 0, 512,
                               row1=((kc + 1) * 128 if kc % 2 == 0 else None))
                ch_["lcols"] = lambda fl, ft: fl * 128
                ch_["rhs"] = xs[:, kc * 512: kc * 512 + 512]
                if kc == 0:
                    ch_["toks"] = ch_["toks"] + xt_toks
                chunks.append(ch_)

            if which == "k":
                def ev_k2(fl, ft, ps, c0=c0):
                    dst_ap = kT[ft][:, c0:c0 + 512]
                    p_ = fl % 2
                    rope_norm(P, sb, banks, src=ps, Cs=Cbuf[:, 0:512],
                              Ss=Sbuf[:, 0:512],
                              qs=(ro_qs, ro_qs2)[p_], tmp=(ro_t, ro_t2)[p_],
                              sq=(sqs, sqs2)[p_], rows=(rows, rows2)[p_],
                              scalemat=(smat, smat2)[p_], out_bf=dst_ap,
                              wrow=sb["gkrow_sb"], eps_col=None, extra_scale=1.0,
                              half=64, N=512, statbank=(banks[5], banks[4])[p_],
                              bcbank=(banks[6], banks[7])[p_],
                              vv=vv, ev=ev, lookback=2)

                gemm(P, f"p2k{c}", chunks=chunks, ftiles=list(range(4)), N=512,
                     banks=banks[0:4], evac=ev_k2, evac_eng="vector",
                     pre_toks=[tok(ev), tok(vv)])
            else:
                def ev_v(fl, ft, ps, c0=c0):
                    vtmp = ro_t
                    P.op("scalar", lambda e, o=vtmp[:, 0:512], i=ps:
                         e.activation(out=o, in_=i, func=AF.Identity, bias=0.0, scale=1.0),
                         inc=ev)
                    P.wait_tok("tensor", [tok(ev)])
                    for i4 in range(4):
                        P.op("tensor", lambda e, i4=i4: e.transpose(
                            banks[5][:, i4 * 128:(i4 + 1) * 128],
                            vtmp[:, i4 * 128:(i4 + 1) * 128], sb["ident_f_sb"][:]),
                            inc=(P.ctr("tp") if i4 == 3 else None))
                    P.wait("scalar", P.ctr("tp"))
                    P.op("scalar", lambda e, o=vN[ft][:, c0:c0 + 512], i=banks[5][:, 0:512]:
                         e.activation(out=o, in_=i, func=AF.Identity, bias=0.0, scale=1.0),
                         inc=ev)

                gemm(P, f"p2v{c}", chunks=chunks, ftiles=list(range(4)), N=512,
                     banks=banks[0:4], evac=ev_v, evac_eng="scalar",
                     pre_toks=[tok(ev), tok(vv)])
    P.barrier()

    for sr in range(P2_LIM["sr"]):
        q0 = HALO1 + sr * 512
        load_cs(q0)
        P.wait("vector", ws.use)
        P.wait("vector", P.ctr("sc"))
        P.wait("vector", st)
        P.wait("scalar", ws.use)
        xt_toks = load_cast_slice(P, xl, x1T, q0, xs, 512, 512)
        for qtr in range(4):
            fts = [qtr * 4 + i for i in range(4)]
            chunks = []
            for kc in range(16):
                ch_ = ws.chunk(wqB, kc * 128, qtr * 512, 512,
                               row1=((kc + 1) * 128 if kc % 2 == 0 else None))
                ch_["lcols"] = lambda fl, ft: fl * 128
                ch_["rhs"] = xs[:, kc * 512: kc * 512 + 512]
                if kc == 0:
                    ch_["toks"] = ch_["toks"] + xt_toks
                chunks.append(ch_)

            def ev_q(fl, ft, ps, q0=q0):
                dst = qbf[:, ft * 512:(ft + 1) * 512]
                p_ = fl % 2
                rope_norm(P, sb, banks, src=ps, Cs=Cbuf[:, 0:512],
                          Ss=Sbuf[:, 0:512],
                          qs=(ro_qs, ro_qs2)[p_], tmp=(ro_t, ro_t2)[p_],
                          sq=(sqs, sqs2)[p_], rows=(rows, rows2)[p_],
                          scalemat=(smat, smat2)[p_], out_bf=dst,
                          wrow=sb["gqrow_sb"], eps_col=None,
                          extra_scale=1.0 / np.sqrt(HD1), half=64, N=512,
                          statbank=(banks[5], banks[4])[p_],
                          bcbank=(banks[6], banks[7])[p_], vv=vv, ev=ev, lookback=2)

            vvc = P.ctr("vv")
            pre = ([tok(ev), tok(vv)] if qtr == 0
                   else [(vvc, vvc.val - 2)])
            gemm(P, f"p2q{sr}{qtr}", chunks=chunks, ftiles=fts, N=512,
                 banks=banks[0:4], evac=ev_q, evac_eng="vector",
                 pre_toks=pre)
        qtoks = [tok(vv)]

        pend = [None]
        mask_tok = [[tok(vv)]]

        def emit_tail(pd):
            if pd is None:
                return
            h, qt, g, pp = pd["h"], pd["qt"], pd["g"], pd["pp"]
            kh = h // 4
            P.wait_tok("tensor", pd["pn_tok"])
            for i9 in range(9):
                bnk = banks[3 + (i9 % 2)]
                if i9 >= 2:
                    P.wait("tensor", ev, ev.val - 1)
                P.op("tensor", lambda e, i9=i9, bnk=bnk, pp=pp: e.transpose(
                    bnk[:, 0:128], msc[:, pp * 1152 + i9 * 128: pp * 1152 + (i9 + 1) * 128],
                    sb["ident_f_sb"][:]), inc=P.ctr("tp2"))
                P.wait("scalar", P.ctr("tp2"))
                P.op("scalar", lambda e, i9=i9, bnk=bnk, pp=pp: e.activation(
                    out=pT[:, pp * 1152 + i9 * 128: pp * 1152 + (i9 + 1) * 128],
                    in_=bnk[:, 0:128], func=AF.Identity, bias=0.0, scale=1.0), inc=ev)
            P.wait_tok("tensor", [tok(ev)])
            ktile0 = g // 128
            for i9 in range(9):
                P.op("tensor", lambda e, i9=i9, kh=kh, ktile0=ktile0, pp=pp: e.matmul(
                    banks[5][:, 0:128],
                    vN[kh][:, (ktile0 + i9) * 128: (ktile0 + i9 + 1) * 128],
                    pT[:, pp * 1152 + i9 * 128: pp * 1152 + (i9 + 1) * 128],
                    start=(i9 == 0), stop=(i9 == 8)),
                    inc=(P.ctr("om") if i9 == 8 else None))
            P.wait("scalar", P.ctr("om"))
            P.op("scalar", lambda e, h=h, qt=qt: e.activation(
                out=att[:, h * 512 + qt * 128: h * 512 + qt * 128 + 128],
                in_=banks[5][:, 0:128], func=AF.Identity, bias=0.0, scale=1.0), inc=ev)

        it = [0]
        mask_tok[0] = [tok(vv)]
        for qt in range(P2_LIM["qt"]):
            g = sr * 512 + qt * 128
            segs = [(0, 512), (512, 1024), (1024, 1152)]
            P.wait_tok("tensor", [tok(vv), tok(ev)])
            for si, (a, b2) in enumerate(segs):
                if si > 0:
                    P.wait("tensor", vv)
                P.op("tensor", lambda e, si=si, a=a, b2=b2, g=g: e.matmul(
                    banks[6][:, 0:b2 - a],
                    sb["ones_row_bf_sb"][0:1, :], kval[0:1, g + a:g + b2],
                    start=True, stop=True), inc=P.ctr("bc"))
                P.wait("vector", P.ctr("bc"))
                P.op("vector", lambda e, si=si, a=a, b2=b2: e.tensor_add(
                    cmb[:, a:b2], sb["genmask1_sb"][:, a:b2], banks[6][:, 0:b2 - a]), inc=vv)
            for h in range(P2_LIM["h"]):
                kh = h // 4
                pp = it[0] % 2
                P.wait_tok("tensor", mask_tok[0])
                for si, (a, b2) in enumerate(segs):
                    P.op("tensor", lambda e, si=si, a=a, b2=b2, h=h, kh=kh, qt=qt, g=g:
                        e.matmul(
                            banks[si][:, 0:b2 - a],
                            qbf[:, h * 512 + qt * 128: h * 512 + qt * 128 + 128],
                            kT[kh][:, g + a: g + b2], start=True, stop=True),
                        inc=(P.ctr("sc") if si == 2 else None))
                P.wait("vector", P.ctr("sc"))
                for si, (a, b2) in enumerate(segs):
                    P.op("vector", lambda e, si=si, a=a, b2=b2, pp=pp: e.tensor_add(
                        msc[:, pp * 1152 + a: pp * 1152 + b2], banks[si][:, 0:b2 - a],
                        cmb[:, a:b2]), inc=(vv if si == 2 else None))
                mask_tok[0] = [tok(vv)]
                P.op("vector", lambda e, pp=pp: e.reduce_max(ncol[:, pp * 2: pp * 2 + 1],
                     msc[:, pp * 1152:(pp + 1) * 1152], axis=AX.X, negate=True))
                P.wait("scalar", P.ctr("vector_ser"))
                P.op("scalar", lambda e, pp=pp: e.activation(
                     out=msc[:, pp * 1152:(pp + 1) * 1152],
                     in_=msc[:, pp * 1152:(pp + 1) * 1152], func=AF.Exp,
                     bias=ncol[:, pp * 2: pp * 2 + 1], scale=1.0,
                     accum_out=ncol[:, pp * 2 + 1: pp * 2 + 2]), inc=ev)
                P.wait("vector", ev)
                P.op("vector", lambda e, pp=pp: e.reciprocal(
                     ncol[:, pp * 2 + 1: pp * 2 + 2], ncol[:, pp * 2 + 1: pp * 2 + 2]))
                P.op("vector", lambda e, pp=pp: e.tensor_scalar(
                     out=msc[:, pp * 1152:(pp + 1) * 1152],
                     in0=msc[:, pp * 1152:(pp + 1) * 1152],
                     scalar1=ncol[:, pp * 2 + 1: pp * 2 + 2], scalar2=None,
                     op0=ALU.mult), inc=vv)
                pn_tok = [tok(vv)]
                emit_tail(pend[0])
                pend[0] = {"h": h, "qt": qt, "g": g, "pp": pp, "pn_tok": pn_tok}
                it[0] += 1
        emit_tail(pend[0])
        pend[0] = None

        lnd = P.ctr("ln")
        wavetok = [None, None]
        pre0 = [tok(ev), tok(vv), tok(rl.cons)]
        for qw in range(4):
            par = qw % 2
            fts = list(range(qw * 4, (qw + 1) * 4))
            chunks = []
            for kc in range(16):
                ch_ = ws.chunk(woB, kc * 128, qw * 512, 512,
                               row1=((kc + 1) * 128 if kc % 2 == 0 else None))
                ch_["lcols"] = lambda fl, ft: fl * 128
                ch_["rhs"] = att[:, kc * 512: kc * 512 + 512]
                if kc == 0 and qw == 0:
                    ch_["toks"] = ch_["toks"] + [tok(ev)]
                chunks.append(ch_)

            def ev_o(fl, ft, ps, sr=sr):
                buf, toks = rl.load(x1T, ft * 128, HALO1 + sr * 512, 512)
                P.wait_tok("vector", [tok(ev)] + toks)
                P.op("vector", lambda e, b2=buf, o=rbuf[:, ft * 512:(ft + 1) * 512], i=ps:
                     e.tensor_add(o, i, b2[:, 0:512]), inc=rl.cons)

            pre = [wavetok[par]] if wavetok[par] else pre0
            gemm(P, f"p2o{sr}{qw}", chunks=chunks, ftiles=fts, N=512,
                 banks=banks[par * 4:(par + 1) * 4], evac=ev_o, evac_eng="vector",
                 pre_toks=pre)
            wavetok[par] = tok(rl.cons)
        layer_norm(P, f"p2ln{sr}", rbuf=rbuf, rT=512, t0=0, N=512,
                   out_fn=lambda ft: rbuf[:, ft * 512:(ft + 1) * 512],
                   out_dtype_note="bf16", banks=banks[5:8], sb=sb,
                   sq_scratch=sqln, rows=rows, done_ctr=lnd, ones_key="ones_col_bf_sb",
                   r_toks=[tok(rl.cons)],
                   eps_col=bpk[:, bp.col("ln_eps"):bp.col("ln_eps") + 1])
        P.wait("gpsimd", lnd)
        P.wait("vector", st)
        P.op("vector", lambda e: e.nop(), inc=vv)
        P.wait("gpsimd", vv)
        for ft in range(16):
            emit_dma(P, "gpsimd", x2T[ft * 128:(ft + 1) * 128, sr * 512:(sr + 1) * 512],
                     rbuf[:, ft * 512:(ft + 1) * 512], st)
        P.barrier()

    P.wait("sync", st)
    P.wait("gpsimd", st)


P2_LIM = {"sr": 4, "qt": 4, "h": 16}

P2_CONSTS = ["ident_f", "ones_col_f", "ones_col_bf", "ones_row_f", "ones_row_bf",
             "genmask1", "gqrow", "gkrow"]


def build_launch_p2(bp, x1_kind="ExternalInput", dbg=False):
    nc = bass.Bass("TRN2", target_bir_lowering=False)
    ctx = ExitStack()
    P = Prog(nc, ctx)
    with ctx:
        io = {}
        if dbg:
            io["dbg"] = nc.dram_tensor("dbg", [2048, 512], BF16, kind="ExternalOutput").ap()
        io["x1T"] = nc.dram_tensor("x1T", [D, T1], BF16, kind=x1_kind).ap()
        io["x2T"] = nc.dram_tensor("x2T", [D, OWN], BF16, kind="ExternalOutput").ap()
        wsrc = {}
        for nm, shape in [("wqT", [D, D]), ("wkT", [D, 512]), ("wvT", [D, 512]),
                          ("woT", [D, D])]:
            wsrc[nm] = nc.dram_tensor(nm, list(shape), F32, kind="ExternalInput").ap()
        io["wqB"] = nc.dram_tensor("wqB", [D, D], BF16, kind="Internal").ap()
        io["wkB"] = nc.dram_tensor("wkB", [D, 512], BF16, kind="Internal").ap()
        io["wvB"] = nc.dram_tensor("wvB", [D, 512], BF16, kind="Internal").ap()
        io["woB"] = nc.dram_tensor("woB", [D, D], BF16, kind="Internal").ap()

        sb = load_consts(P, ctx, nc, P2_CONSTS, bp.array_cols)
        banks = [ctx.enter_context(nc.psum_tensor(f"bank{i}", [128, 512], F32)) for i in range(8)]

        with ExitStack() as wctx:
            wcd = P.ctr("wconv_done")
            jobs = [(wsrc["wqT"], io["wqB"], D, 1024), (wsrc["wkT"], io["wkB"], D, 512),
                    (wsrc["wvT"], io["wvB"], D, 512), (wsrc["woT"], io["woB"], D, 1024)]
            jobs = [(wsrc["wqT"], io["wqB"], D, 2048), (wsrc["wkT"], io["wkB"], D, 512),
                    (wsrc["wvT"], io["wvB"], D, 512), (wsrc["woT"], io["woB"], D, 2048)]
            emit_wconv(P, wctx, jobs, wcd)
            P.wait("sync", wcd)

        ws = WStream(P, ctx)
        emit_p2(P, ctx, io, sb, banks, bp, ws)
        P.run()
    return nc




P1_LIM = {"sr": NSR, "h": H0}

A_SMALL = [
    ("kvc_w", D, 128), ("kvc_wz", D, 128), ("idx_pack", D, 128),
    ("k_proj", D, 128), ("v_proj", D, 128), ("w_dq", D, 64), ("w_iuq", 64, 256),
    ("w_w", D, 4), ("W_routerT", D, 8), ("A_allT", D, 64), ("B_allT_pad", 128, D),
]


def emit_p0p1(P, ctx, io, sb, banks, bp, ws, wsm):
    nc = P.nc
    hT_full, hT_loc, x1T = io["hT_full"], io["hT_loc"], io["x1T"]
    qcB, finB, wbB, gwB, qupB = io["qcB"], io["finB"], io["wbB"], io["gwB"], io["qupB"]
    bpk = sb["bpack_sb"]
    rms_col = bpk[:, bp.col("rms_eps"):bp.col("rms_eps") + 1]
    ln_col = bpk[:, bp.col("ln_eps"):bp.col("ln_eps") + 1]

    swkT = ctx.enter_context(nc.sbuf_tensor("p0_swkT", [128, TLOCP], BF16))
    vN0 = ctx.enter_context(nc.sbuf_tensor("p0_vN", [128, TLOCP], BF16))
    ckvnT = ctx.enter_context(nc.sbuf_tensor("p0_ckvnT", [128, NBL], BF16))
    ckvnN = ctx.enter_context(nc.sbuf_tensor("p0_ckvnN", [128, NBL], BF16))
    kcompT = ctx.enter_context(nc.sbuf_tensor("p0_kcompT", [128, NBL], BF16))
    hbf = ctx.enter_context(nc.sbuf_tensor("a_hbf", [128, 16 * 512], BF16))
    zbuf = ctx.enter_context(nc.sbuf_tensor("a_z", [128, 512], F32))
    ebuf = ctx.enter_context(nc.sbuf_tensor("a_e", [128, 512], F32))
    wbuf = ctx.enter_context(nc.sbuf_tensor("a_w", [128, 512], F32))
    segb = ctx.enter_context(nc.sbuf_tensor("a_seg", [128, 64], F32))

    qs0 = ctx.enter_context(nc.sbuf_tensor("a_qs0", [128, 512], F32))
    rot0 = ctx.enter_context(nc.sbuf_tensor("a_rot0", [128, 512], F32))
    sq0 = ctx.enter_context(nc.sbuf_tensor("a_sq0", [128, 512], F32))
    smat0 = ctx.enter_context(nc.sbuf_tensor("a_smat0", [128, 512], F32))
    rows = ctx.enter_context(nc.sbuf_tensor("a_rows", [1, 3 * 512], F32))
    Cb = ctx.enter_context(nc.sbuf_tensor("a_cb", [128, 512], F32))
    Sb = ctx.enter_context(nc.sbuf_tensor("a_sb", [128, 512], F32))
    C0d = nc.dram_tensor("C0", [128, TLOCP], F32, kind="ExternalInput").ap()
    S0d = nc.dram_tensor("S0", [128, TLOCP], F32, kind="ExternalInput").ap()
    kval0d = nc.dram_tensor("kval0b", [1, TLOCP], BF16, kind="ExternalInput").ap()
    kval0 = ctx.enter_context(nc.sbuf_tensor("a_kval0", [1, 384], BF16))
    kvld = P.ctr("kvld")
    tposd = nc.dram_tensor("tpos_cols", [128, NSR * 4], F32, kind="ExternalInput").ap()
    tpos = ctx.enter_context(nc.sbuf_tensor("a_tpos", [128, NSR * 4], F32))

    qcT = ctx.enter_context(nc.sbuf_tensor("a_qcT", [128, 4 * 512], BF16))
    dqT = ctx.enter_context(nc.sbuf_tensor("a_dqT", [128, 512], BF16))
    qiT = ctx.enter_context(nc.sbuf_tensor("a_qiT", [128, 4 * 512], BF16))
    wih = ctx.enter_context(nc.sbuf_tensor("wih", [128, 16], F32))
    isc = zbuf
    wkb = ctx.enter_context(nc.sbuf_tensor("a_wkb", [128, 512], F32))
    mbuf = ctx.enter_context(nc.sbuf_tensor("a_mbuf", [128, 512], F32))
    m8 = ctx.enter_context(nc.sbuf_tensor("a_m8", [128, 8], F32))
    selb = ctx.enter_context(nc.sbuf_tensor("a_selb", [128, 4 * 512], BF16))
    avb = ctx.enter_context(nc.sbuf_tensor("a_avb", [128, 4], F32))
    cmb0 = ctx.enter_context(nc.sbuf_tensor("a_cmb0", [128, 4 * 384], BF16))
    qbf = ctx.enter_context(nc.sbuf_tensor("a_qbf", [128, 512], BF16))
    msc0 = ctx.enter_context(nc.sbuf_tensor("a_msc0p", [128, 2 * 512], F32))
    pT0 = ctx.enter_context(nc.sbuf_tensor("a_pT0", [128, 2 * 512], BF16))
    msc1 = ctx.enter_context(nc.sbuf_tensor("a_msc1", [128, 2 * 384], F32))
    pT1 = ctx.enter_context(nc.sbuf_tensor("a_pT1", [128, 2 * 384], BF16))
    ncol = ctx.enter_context(nc.sbuf_tensor("a_ncol", [128, 8], F32))
    att0 = ctx.enter_context(nc.sbuf_tensor("a_att0", [128, 16 * 512], BF16))
    grp = ctx.enter_context(nc.sbuf_tensor("a_grp", [128, 16 * 512], BF16))
    rbf = ctx.enter_context(nc.sbuf_tensor("a_rbf", [128, 16 * 512], BF16))
    sqln = ctx.enter_context(nc.sbuf_tensor("a_sqln", [128, 2 * 512], BF16))
    glog = ctx.enter_context(nc.sbuf_tensor("a_glog", [128, 4 * 8], F32))
    gsc = ctx.enter_context(nc.sbuf_tensor("a_gsc", [128, 8], F32))
    gcol = ctx.enter_context(nc.sbuf_tensor("a_gcol", [128, 4], F32))
    hallf = ctx.enter_context(nc.sbuf_tensor("a_hallf", [128, 64], F32))
    hprT = ctx.enter_context(nc.sbuf_tensor("a_hprT", [128, 512], BF16))
    tmpf = ctx.enter_context(nc.sbuf_tensor("a_tmpf", [128, 2 * 512], F32))

    ckvf = msc0
    kcof = tmpf
    xl = ActLoader(P, ctx, "xl", nslots=2, width=512)
    rl = ActLoader(P, ctx, "rl", nslots=2, width=512)
    ev = P.ctr("ev")
    vv = P.ctr("vv")
    st = P.ctr("st")
    cld = P.ctr("cld")

    P.op("gpsimd", lambda e: e.memset(qs0[:], 0.0))
    P.op("gpsimd", lambda e: e.memset(hprT[:], 0.0))
    emit_dma(P, "sync", tpos[:], tposd, cld)
    for e_ in ("tensor", "vector", "scalar", "gpsimd"):
        P.wait(e_, cld)

    csld = [P.ctr("csld0"), P.ctr("csld1")]

    def load_cs0(c0):
        P.wait("gpsimd", vv)
        emit_dma(P, "gpsimd", Cb[:], C0d[:, c0:c0 + 512], csld[0])
        emit_dma(P, "gpsimd", Sb[:], S0d[:, c0:c0 + 512], csld[1])
        toks = [(csld[0], csld[0].val), (csld[1], csld[1].val)]
        P.wait_tok("vector", toks)
        P.wait_tok("scalar", toks)

    def rmsnorm_from_psum(ps, wrow, out_ap, N, extra=np.sqrt(128.0), pre=()):
        P.wait("scalar", P.ctr("rn"))
        P.wait("vector", P.ctr("rn"))
        P.wait_tok("scalar", list(pre))
        P.wait_tok("vector", list(pre))
        P.op("scalar", lambda e: e.activation(out=sq0[:, 0:N], in_=ps, func=AF.Square,
                                              bias=0.0, scale=1.0), inc=ev)
        P.wait_tok("tensor", [tok(ev)])
        P.op("tensor", lambda e: e.matmul(banks[6][0:1, 0:N], sb["ones_col_f_sb"][:, 0:1],
                                          sq0[:, 0:N], start=True, stop=True), inc=P.ctr("rn"))
        P.wait("scalar", P.ctr("rn"))
        P.op("scalar", lambda e: e.activation(out=rows[0:1, 0:N], in_=banks[6][0:1, 0:N],
                                              func=AF.Sqrt, bias=rms_col[0:1, 0:1],
                                              scale=1.0), inc=ev)
        P.wait("vector", ev)
        P.op("vector", lambda e: e.reciprocal(rows[0:1, 0:N], rows[0:1, 0:N]))
        if extra != 1.0:
            P.op("vector", lambda e: e.tensor_scalar_mul(rows[0:1, 0:N], rows[0:1, 0:N],
                                                         float(extra)))
        P.op("vector", lambda e: e.nop(), inc=vv)
        P.wait_tok("tensor", [tok(vv)])
        P.op("tensor", lambda e: e.matmul(banks[7][:, 0:N], wrow[0:1, :], rows[0:1, 0:N],
                                          start=True, stop=True), inc=P.ctr("rn"))
        P.wait("scalar", P.ctr("rn"))
        P.op("scalar", lambda e: e.activation(out=smat0[:, 0:N], in_=banks[7][:, 0:N],
                                              func=AF.Identity, bias=0.0, scale=1.0), inc=ev)
        P.wait("vector", ev)
        P.op("vector", lambda e, o=out_ap: e.tensor_mul(o, ps, smat0[:, 0:N]), inc=vv)

    for c in range(16):
        c0 = c * 512
        P.wait_tok("vector", [tok(P.ctr("gmm"))])
        toks = load_cast_slice(P, xl, hT_full, c0, hbf, 512, 512)
        chunks = []
        for kc in range(16):
            chunks.append({"lhsT": (lambda fl, ft, kc=kc:
                           [sb["kvc_w_bf"], sb["kvc_wz_bf"], sb["idx_pack_bf"]][ft]
                           [:, kc * 128:(kc + 1) * 128]),
                           "rhs": hbf[:, kc * 512: kc * 512 + 512],
                           "toks": toks if kc == 0 else ()})
        cm = gemm(P, f"p0a{c}", chunks=chunks, ftiles=[0, 1, 2], N=512,
                  banks=banks[0:3], evac=None, pre_toks=[tok(ev), tok(vv)])
        P.wait_tok("vector", [cm])
        P.op("vector", lambda e: e.tensor_add(zbuf[:], banks[1][:, 0:512],
                                              sb["kvc_bias_rep_sb"][:]), inc=vv)
        P.wait("scalar", vv)
        P.op("scalar", lambda e: e.activation(out=ebuf[:], in_=zbuf[:], func=AF.Exp,
                                              bias=0.0, scale=1.0), inc=ev)
        P.wait("vector", ev)
        P.op("vector", lambda e: e.tensor_mul(wbuf[:], ebuf[:], banks[0][:, 0:512]))
        P.op("vector", lambda e: e.reduce_sum(segb[:, 0:32],
                                              wbuf[:].rearrange("p (a b) -> p a b", b=16),
                                              axis=AX.X))
        P.op("vector", lambda e: e.reduce_sum(segb[:, 32:64],
                                              ebuf[:].rearrange("p (a b) -> p a b", b=16),
                                              axis=AX.X))
        P.op("vector", lambda e: e.reciprocal(segb[:, 32:64], segb[:, 32:64]))
        P.op("vector", lambda e, c=c: e.tensor_mul(ckvf[:, c * 32:(c + 1) * 32],
                                                   segb[:, 0:32], segb[:, 32:64]))
        P.op("vector", lambda e: e.tensor_add(zbuf[0:64, :], banks[2][64:128, 0:512],
                                              sb["idx_bias_rep_sb"][64:128, :]), inc=vv)
        P.wait("scalar", vv)
        P.op("scalar", lambda e: e.activation(out=ebuf[0:64, :], in_=zbuf[0:64, :],
                                              func=AF.Exp, bias=0.0, scale=1.0), inc=ev)
        P.wait("vector", ev)
        P.op("vector", lambda e: e.tensor_mul(wbuf[0:64, :], ebuf[0:64, :],
                                              banks[2][0:64, 0:512]))
        P.op("vector", lambda e: e.reduce_sum(segb[0:64, 0:32],
                                              wbuf[0:64, :].rearrange("p (a b) -> p a b", b=16),
                                              axis=AX.X))
        P.op("vector", lambda e: e.reduce_sum(segb[0:64, 32:64],
                                              ebuf[0:64, :].rearrange("p (a b) -> p a b", b=16),
                                              axis=AX.X))
        P.op("vector", lambda e: e.reciprocal(segb[0:64, 32:64], segb[0:64, 32:64]))
        P.op("vector", lambda e, c=c: e.tensor_mul(kcof[0:64, c * 32:(c + 1) * 32],
                                                   segb[0:64, 0:32], segb[0:64, 32:64]),
             inc=vv)
    rmsnorm_from_psum(ckvf[:, 0:NBL], sb["knrow_sb"], zbuf[:, 0:NBL], NBL, pre=[tok(vv)])
    P.op("vector", lambda e: e.tensor_copy(out=ckvnT[:], in_=zbuf[:, 0:NBL]), inc=vv)
    P.wait_tok("tensor", [tok(vv)])
    for i4 in range(4):
        P.op("tensor", lambda e, i4=i4: e.transpose(banks[0][:, i4 * 128:(i4 + 1) * 128],
                                                    zbuf[:, i4 * 128:(i4 + 1) * 128],
                                                    sb["ident_f_sb"][:]),
             inc=(P.ctr("tp") if i4 == 3 else None))
    P.wait("scalar", P.ctr("tp"))
    P.op("scalar", lambda e: e.activation(out=ckvnN[:, 0:512], in_=banks[0][:, 0:512],
                                          func=AF.Identity, bias=0.0, scale=1.0), inc=ev)
    P.op("vector", lambda e: e.tensor_copy(out=kcompT[0:64, :], in_=kcof[0:64, 0:NBL]), inc=vv)

    for c in range(7):
        c0 = c * 512
        load_cs0(c0)
        P.wait_tok("vector", [tok(P.ctr("gmm"))])
        toks = load_cast_slice(P, xl, hT_loc, c0, hbf, 512, 512)
        chunks = []
        for kc in range(16):
            chunks.append({"lhsT": (lambda fl, ft, kc=kc:
                           [sb["k_proj_bf"], sb["v_proj_bf"]][ft][:, kc * 128:(kc + 1) * 128]),
                           "rhs": hbf[:, kc * 512: kc * 512 + 512],
                           "toks": toks if kc == 0 else ()})
        cm = gemm(P, f"p0b{c}", chunks=chunks, ftiles=[0, 1], N=512,
                  banks=banks[0:2], evac=None, pre_toks=[tok(ev), tok(vv)])
        P.wait_tok("vector", [cm])
        rmsnorm_from_psum(banks[0][:, 0:512], sb["knrow_sb"], rot0[:, 0:512], 512, pre=[cm])
        P.wait("scalar", vv)
        P.op("scalar", lambda e: e.activation(out=qs0[0:32, 0:512], in_=rot0[32:64, 0:512],
                                              func=AF.Identity, bias=0.0, scale=1.0))
        P.op("scalar", lambda e: e.activation(out=qs0[32:64, 0:512], in_=rot0[0:32, 0:512],
                                              func=AF.Identity, bias=0.0, scale=1.0), inc=ev)
        P.wait("vector", ev)
        P.op("vector", lambda e: e.tensor_mul(qs0[0:64, 0:512], qs0[0:64, 0:512],
                                              Sb[0:64, 0:512]))
        P.op("vector", lambda e: e.tensor_mul(zbuf[:, 0:512], rot0[:, 0:512], Cb[:, 0:512]))
        P.op("vector", lambda e, c0=c0: e.tensor_add(swkT[:, c0:c0 + 512], zbuf[:, 0:512],
                                                     qs0[:, 0:512]), inc=vv)
        P.op("scalar", lambda e: e.activation(out=wbuf[:], in_=banks[1][:, 0:512],
                                              func=AF.Identity, bias=0.0, scale=1.0), inc=ev)
        P.wait_tok("tensor", [tok(ev)])
        for i4 in range(4):
            P.op("tensor", lambda e, i4=i4: e.transpose(banks[2][:, i4 * 128:(i4 + 1) * 128],
                                                        wbuf[:, i4 * 128:(i4 + 1) * 128],
                                                        sb["ident_f_sb"][:]),
                 inc=(P.ctr("tp") if i4 == 3 else None))
        P.wait("scalar", P.ctr("tp"))
        P.op("scalar", lambda e, c0=c0: e.activation(out=vN0[:, c0:c0 + 512],
                                                     in_=banks[2][:, 0:512],
                                                     func=AF.Identity, bias=0.0, scale=1.0),
             inc=ev)
    P.barrier()

    for r in range(P1_LIM["sr"]):
        g1 = r * 512
        hcol = HALO0 + g1
        load_cs0(hcol)
        P.wait("vector", ws.use)
        P.wait("vector", st)
        toks = load_cast_slice(P, xl, hT_loc, hcol, hbf, 512, 512)

        chunks = []
        for kc in range(16):
            ch_ = ws.chunk(qcB, kc * 128, 0, 512,
                           row1=((kc + 1) * 128 if kc % 2 == 0 else None))
            ch_["lcols"] = lambda fl, ft: fl * 128
            ch_["rhs"] = hbf[:, kc * 512: kc * 512 + 512]
            if kc == 0:
                ch_["toks"] = ch_["toks"] + toks
            chunks.append(ch_)

        def ev_qc(fl, ft, ps):
            P.op("scalar", lambda e, o=qcT[:, ft * 512:(ft + 1) * 512], i=ps:
                 e.activation(out=o, in_=i, func=AF.Identity, bias=0.0, scale=1.0), inc=ev)

        gemm(P, f"aqc{r}", chunks=chunks, ftiles=[0, 1, 2, 3], N=512,
             banks=banks[0:4], evac=ev_qc, pre_toks=[tok(ev), tok(vv)])

        P.wait_tok("tensor", [tok(ev), tok(vv)])
        for kc in range(16):
            P.op("tensor", lambda e, kc=kc: e.matmul(banks[4][0:64, 0:512],
                 sb["w_dq_bf"][:, kc * 64:(kc + 1) * 64],
                 hbf[:, kc * 512: kc * 512 + 512],
                 start=(kc == 0), stop=(kc == 15)),
                 inc=(P.ctr("gmm") if kc == 15 else None))
        P.wait_tok("scalar", [tok(P.ctr("gmm"))])
        P.op("scalar", lambda e: e.activation(out=dqT[0:64, 0:512], in_=banks[4][0:64, 0:512],
                                              func=AF.Identity, bias=0.0, scale=1.0), inc=ev)
        P.wait_tok("tensor", [tok(ev)])
        for ih in range(4):
            P.wait("tensor", ev)
            P.op("tensor", lambda e, ih=ih: e.matmul(banks[5][0:64, 0:512],
                 sb["w_iuq_bf"][0:64, ih * 64:(ih + 1) * 64], dqT[0:64, 0:512],
                 start=True, stop=True), inc=P.ctr("qi"))
            P.wait("scalar", P.ctr("qi"))
            P.op("scalar", lambda e, ih=ih: e.activation(out=qiT[0:64, ih * 512:(ih + 1) * 512],
                 in_=banks[5][0:64, 0:512], func=AF.Identity, bias=0.0, scale=1.0), inc=ev)

        for tt in range(4):
            P.wait("tensor", ev)
            for kc in range(16):
                P.op("tensor", lambda e, tt=tt, kc=kc: e.matmul(banks[6][:, 0:4],
                     hbf[:, kc * 512 + tt * 128: kc * 512 + tt * 128 + 128],
                     sb["w_w_bf"][:, kc * 4:(kc + 1) * 4],
                     start=(kc == 0), stop=(kc == 15)),
                     inc=(P.ctr("wih") if kc == 15 else None))
            P.wait("scalar", P.ctr("wih"))
            P.op("scalar", lambda e, tt=tt: e.activation(out=wih[:, tt * 4:(tt + 1) * 4],
                 in_=banks[6][:, 0:4], func=AF.Identity, bias=0.0, scale=1.0), inc=ev)

        for qt in range(4):
            tcol = tpos[:, r * 4 + qt: r * 4 + qt + 1]
            P.wait_tok("tensor", [tok(ev), tok(vv)])
            for ih in range(4):
                P.wait("tensor", ev)
                P.op("tensor", lambda e, ih=ih, qt=qt: e.matmul(banks[7][:, 0:512],
                     qiT[0:64, ih * 512 + qt * 128: ih * 512 + qt * 128 + 128],
                     kcompT[0:64, 0:NBL], start=True, stop=True), inc=P.ctr("hs"))
                P.wait("scalar", P.ctr("hs"))
                P.wait("scalar", vv)
                P.op("scalar", lambda e: e.activation(out=ebuf[:], in_=banks[7][:, 0:512],
                     func=AF.Relu, bias=0.0, scale=1.0), inc=ev)
                P.wait("vector", ev)
                if ih == 0:
                    P.op("vector", lambda e, ih=ih, qt=qt: e.tensor_scalar(
                        out=isc[:], in0=ebuf[:], scalar1=wih[:, qt * 4 + ih: qt * 4 + ih + 1],
                        scalar2=None, op0=ALU.mult), inc=vv)
                else:
                    P.op("vector", lambda e, ih=ih, qt=qt: e.scalar_tensor_tensor(
                        out=isc[:], in0=ebuf[:], scalar=wih[:, qt * 4 + ih: qt * 4 + ih + 1],
                        in1=isc[:], op0=ALU.mult, op1=ALU.add), inc=vv)
            P.op("vector", lambda e, tcol=tcol: e.tensor_scalar(out=mbuf[:], in0=sb["bn_row_sb"][:],
                 scalar1=tcol, scalar2=None, op0=ALU.is_lt))
            P.op("vector", lambda e: e.tensor_scalar(out=wkb[:], in0=mbuf[:], scalar1=1.0,
                 scalar2=1.0e30, op0=ALU.subtract, op1=ALU.mult))
            P.op("vector", lambda e: e.tensor_add(isc[:], isc[:], wkb[:]))
            for rnd in range(8):
                srcb = isc if rnd == 0 else wkb
                P.op("vector", lambda e, srcb=srcb: e.max(out=m8[:], in_=srcb[:]))
                P.op("vector", lambda e, srcb=srcb: e.match_replace(out=wkb[:], in_to_replace=m8[:],
                     in_values=srcb[:], imm_value=SEL_IMM))
            P.op("vector", lambda e: e.tensor_single_scalar(out=ebuf[:], in_=wkb[:],
                 scalar=-1.5e30, op=ALU.is_lt))
            P.op("vector", lambda e: e.tensor_mul(ebuf[:], ebuf[:], mbuf[:]))
            P.op("vector", lambda e, qt=qt: e.tensor_scalar(out=selb[:, qt * 512:(qt + 1) * 512],
                 in0=ebuf[:], scalar1=1.0, scalar2=1.0e30, op0=ALU.subtract, op1=ALU.mult))
            P.op("vector", lambda e, qt=qt, tcol=tcol: e.tensor_single_scalar(
                 out=avb[:, qt:qt + 1], in_=tcol, scalar=16.0, op=ALU.is_ge), inc=vv)
            Q = g1 + qt * 128
            P.wait("gpsimd", P.ctr("bc"))
            emit_dma(P, "gpsimd", kval0[0:1, 0:384], kval0d[0:1, Q + 256: Q + 640], kvld)
            P.wait_tok("tensor", [tok(vv), (kvld, kvld.val)])
            P.op("tensor", lambda e: e.matmul(banks[6][:, 0:384],
                 sb["ones_row_bf_sb"][0:1, :], kval0[0:1, 0:384],
                 start=True, stop=True), inc=P.ctr("bc"))
            P.wait("vector", P.ctr("bc"))
            P.op("vector", lambda e, qt=qt: e.tensor_add(cmb0[:, qt * 384:(qt + 1) * 384],
                 sb["genmask0_sb"][:], banks[6][:, 0:384]), inc=vv)

        pend = [None]
        mask_tok = [[tok(vv)]]

        def emit_tail(pd):
            if pd is None:
                return
            h, qt, Q, pp = pd["h"], pd["qt"], pd["Q"], pd["pp"]
            P.wait_tok("tensor", pd["pn_tok"])
            for i4 in range(4):
                bnk = banks[2 + (i4 % 2)]
                if i4 >= 2:
                    P.wait("tensor", ev, ev.val - 1)
                P.op("tensor", lambda e, i4=i4, bnk=bnk, pp=pp: e.transpose(bnk[:, 0:128],
                     msc0[:, pp * 512 + i4 * 128: pp * 512 + (i4 + 1) * 128],
                     sb["ident_f_sb"][:]), inc=P.ctr("tp2"))
                P.wait("scalar", P.ctr("tp2"))
                P.op("scalar", lambda e, i4=i4, bnk=bnk, pp=pp: e.activation(
                     out=pT0[:, pp * 512 + i4 * 128: pp * 512 + (i4 + 1) * 128],
                     in_=bnk[:, 0:128], func=AF.Identity, bias=0.0, scale=1.0), inc=ev)
            for i3 in range(3):
                bnk = banks[2 + (i3 % 2)]
                P.wait("tensor", ev, ev.val - 1)
                P.op("tensor", lambda e, i3=i3, bnk=bnk, pp=pp: e.transpose(bnk[:, 0:128],
                     msc1[:, pp * 384 + i3 * 128: pp * 384 + (i3 + 1) * 128],
                     sb["ident_f_sb"][:]), inc=P.ctr("tp2"))
                P.wait("scalar", P.ctr("tp2"))
                P.op("scalar", lambda e, i3=i3, bnk=bnk, pp=pp: e.activation(
                     out=pT1[:, pp * 384 + i3 * 128: pp * 384 + (i3 + 1) * 128],
                     in_=bnk[:, 0:128], func=AF.Identity, bias=0.0, scale=1.0), inc=ev)
            P.wait_tok("tensor", [tok(ev), tok(vv)])
            for i4 in range(4):
                P.op("tensor", lambda e, i4=i4, pp=pp: e.matmul(banks[4][:, 0:128],
                     ckvnN[:, i4 * 128:(i4 + 1) * 128],
                     pT0[:, pp * 512 + i4 * 128: pp * 512 + (i4 + 1) * 128],
                     start=(i4 == 0), stop=False))
            for i3 in range(3):
                kt = (Q + 256) // 128 + i3
                P.op("tensor", lambda e, i3=i3, kt=kt, pp=pp: e.matmul(banks[4][:, 0:128],
                     vN0[:, kt * 128:(kt + 1) * 128],
                     pT1[:, pp * 384 + i3 * 128: pp * 384 + (i3 + 1) * 128],
                     start=False, stop=(i3 == 2)),
                     inc=(P.ctr("om") if i3 == 2 else None))
            P.wait("scalar", P.ctr("om"))
            P.op("scalar", lambda e, h=h, qt=qt: e.activation(
                 out=att0[:, h * 512 + qt * 128: h * 512 + qt * 128 + 128],
                 in_=banks[4][:, 0:128], func=AF.Identity, bias=0.0, scale=1.0), inc=ev)

        it = [0]
        pend_h = [None]
        q_srctok = {0: None, 1: None}

        def emit_qt_loop(pdh):
            if pdh is None:
                return
            h, pq = pdh["h"], pdh["pq"]
            mask_tok[0] = mask_tok[0] + [pdh["qtok"]]
            for qt in range(4):
                Q = g1 + qt * 128
                pp = it[0] % 2
                P.wait_tok("tensor", mask_tok[0])
                qsl = (qbf, dqT)[pq]
                P.op("tensor", lambda e, qt=qt, qsl=qsl: e.matmul(banks[1][:, 0:512],
                     qsl[:, qt * 128:(qt + 1) * 128],
                     ckvnT[:, 0:NBL], start=True, stop=True))
                P.op("tensor", lambda e, Q=Q, qt=qt, qsl=qsl: e.matmul(banks[5][:, 0:384],
                     qsl[:, qt * 128:(qt + 1) * 128],
                     swkT[:, Q + 256: Q + 640],
                     start=True, stop=True), inc=P.ctr("sc"))
                P.wait("vector", P.ctr("sc"))
                P.op("vector", lambda e, qt=qt, pp=pp: e.tensor_add(
                     msc0[:, pp * 512:(pp + 1) * 512], banks[1][:, 0:512],
                     selb[:, qt * 512:(qt + 1) * 512]))
                P.op("vector", lambda e, pp=pp: e.reduce_max(ncol[:, pp * 4: pp * 4 + 1],
                     msc0[:, pp * 512:(pp + 1) * 512], axis=AX.X, negate=True))
                P.op("vector", lambda e, qt=qt, pp=pp: e.tensor_add(
                     msc1[:, pp * 384:(pp + 1) * 384], banks[5][:, 0:384],
                     cmb0[:, qt * 384:(qt + 1) * 384]))
                P.op("vector", lambda e, pp=pp: e.reduce_max(ncol[:, pp * 4 + 1: pp * 4 + 2],
                     msc1[:, pp * 384:(pp + 1) * 384], axis=AX.X, negate=True), inc=vv)
                mask_tok[0] = [tok(vv)]
                P.wait("scalar", vv)
                P.op("scalar", lambda e, pp=pp: e.activation(
                     out=msc0[:, pp * 512:(pp + 1) * 512],
                     in_=msc0[:, pp * 512:(pp + 1) * 512], func=AF.Exp,
                     bias=ncol[:, pp * 4: pp * 4 + 1], scale=1.0,
                     accum_out=ncol[:, pp * 4 + 2: pp * 4 + 3]), inc=ev)
                P.op("scalar", lambda e, pp=pp: e.activation(
                     out=msc1[:, pp * 384:(pp + 1) * 384],
                     in_=msc1[:, pp * 384:(pp + 1) * 384], func=AF.Exp,
                     bias=ncol[:, pp * 4 + 1: pp * 4 + 2], scale=1.0,
                     accum_out=ncol[:, pp * 4 + 3: pp * 4 + 4]), inc=ev)
                P.wait("vector", ev)
                P.op("vector", lambda e, pp=pp: e.reciprocal(
                     ncol[:, pp * 4 + 2: pp * 4 + 3], ncol[:, pp * 4 + 2: pp * 4 + 3]))
                P.op("vector", lambda e, qt=qt, pp=pp: e.tensor_mul(
                     ncol[:, pp * 4 + 2: pp * 4 + 3], ncol[:, pp * 4 + 2: pp * 4 + 3],
                     avb[:, qt:qt + 1]))
                P.op("vector", lambda e, pp=pp: e.reciprocal(
                     ncol[:, pp * 4 + 3: pp * 4 + 4], ncol[:, pp * 4 + 3: pp * 4 + 4]))
                P.op("vector", lambda e, pp=pp: e.tensor_scalar(
                     out=msc0[:, pp * 512:(pp + 1) * 512],
                     in0=msc0[:, pp * 512:(pp + 1) * 512],
                     scalar1=ncol[:, pp * 4 + 2: pp * 4 + 3], scalar2=None, op0=ALU.mult))
                P.op("vector", lambda e, pp=pp: e.tensor_scalar(
                     out=msc1[:, pp * 384:(pp + 1) * 384],
                     in0=msc1[:, pp * 384:(pp + 1) * 384],
                     scalar1=ncol[:, pp * 4 + 3: pp * 4 + 4], scalar2=None, op0=ALU.mult),
                     inc=vv)
                pn_tok = [tok(vv)]
                emit_tail(pend[0])
                pend[0] = {"h": h, "qt": qt, "Q": Q, "pp": pp, "pn_tok": pn_tok}
                it[0] += 1

        for h in range(P1_LIM["h"]):
            pq = h % 2
            bank_q = (banks[6], banks[0])[pq]
            chunks = []
            for kc in range(4):
                ch_ = ws.chunk(qupB, kc * 128, h * 128, 128,
                               row1=((kc + 1) * 128 if kc % 2 == 0 else None))
                ch_["lcols"] = lambda fl, ft: 0
                ch_["rhs"] = qcT[:, kc * 512: kc * 512 + 512]
                chunks.append(ch_)
            pre = [q_srctok[pq]] if q_srctok[pq] else [tok(ev), tok(vv)]
            cm = gemm(P, f"aq{r}{h}", chunks=chunks, ftiles=[0], N=512, banks=[bank_q],
                      evac=None, pre_toks=pre)
            emit_qt_loop(pend_h[0])
            P.wait_tok("vector", [cm])
            scc = P.ctr("sc")
            P.wait("vector", scc, max(0, scc.val - 4))
            rope_norm(P, sb, banks, src=bank_q[:, 0:512], Cs=Cb[:, 0:512], Ss=Sb[:, 0:512],
                      qs=qs0, tmp=rot0, sq=sq0, rows=rows, scalemat=smat0,
                      out_bf=(qbf, dqT)[pq][:, 0:512], wrow=sb["qnrow_sb"],
                      eps_col=rms_col, extra_scale=1.0, half=32, N=512,
                      statbank=banks[5], bcbank=banks[7], vv=vv, ev=ev)
            vvc = P.ctr("vv")
            q_srctok[pq] = (vvc, vvc.val - 2)
            pend_h[0] = {"h": h, "pq": pq, "qtok": tok(vv)}
        emit_qt_loop(pend_h[0])
        pend_h[0] = None
        emit_tail(pend[0])
        pend[0] = None

        P.wait_tok("tensor", [tok(vv), tok(ev)])
        for g in range(4):
            chunks = []
            for kc in range(4):
                ch_ = ws.chunk(gwB, (g * 4 + kc) * 128, 0, 512,
                               row1=((g * 4 + kc + 1) * 128 if kc % 2 == 0 else None))
                ch_["lcols"] = lambda fl, ft: fl * 128
                ch_["rhs"] = att0[:, (g * 4 + kc) * 512: (g * 4 + kc) * 512 + 512]
                chunks.append(ch_)

            def ev_grp(fl, ft, ps, g=g):
                P.op("scalar", lambda e, o=grp[:, (g * 4 + ft) * 512:(g * 4 + ft + 1) * 512],
                     i=ps: e.activation(out=o, in_=i, func=AF.Identity, bias=0.0, scale=1.0),
                     inc=ev)

            gemm(P, f"agrp{r}{g}", chunks=chunks, ftiles=[0, 1, 2, 3], N=512,
                 banks=banks[0:4], evac=ev_grp, pre_toks=[tok(ev), tok(vv)])

        wavetok = [None, None]
        pre0 = [tok(ev), tok(vv), tok(rl.cons)]
        for qw in range(4):
            par = qw % 2
            fts = list(range(qw * 4, (qw + 1) * 4))
            chunks = []
            for kc in range(16):
                ch_ = ws.chunk(finB, kc * 128, qw * 512, 512,
                               row1=((kc + 1) * 128 if kc % 2 == 0 else None))
                ch_["lcols"] = lambda fl, ft: fl * 128
                ch_["rhs"] = grp[:, kc * 512: kc * 512 + 512]
                if kc == 0 and qw == 0:
                    ch_["toks"] = ch_["toks"] + [tok(ev)]
                chunks.append(ch_)

            def ev_fin(fl, ft, ps, hcol=hcol):
                buf, toks = rl.load(hT_loc, ft * 128, hcol, 512)
                P.wait_tok("vector", toks + [tok(ev)])
                P.op("vector", lambda e, b2=buf, o=rbf[:, ft * 512:(ft + 1) * 512], i=ps:
                     e.tensor_add(o, i, b2[:, 0:512]), inc=rl.cons)

            pre = [wavetok[par]] if wavetok[par] else pre0
            gemm(P, f"afin{r}{qw}", chunks=chunks, ftiles=fts, N=512,
                 banks=banks[par * 4:(par + 1) * 4], evac=ev_fin, evac_eng="vector",
                 pre_toks=pre)
            wavetok[par] = tok(rl.cons)
        lnd = P.ctr("ln")
        layer_norm(P, f"aln{r}", rbuf=rbf, rT=512, t0=0, N=512,
                   out_fn=lambda ft: rbf[:, ft * 512:(ft + 1) * 512],
                   out_dtype_note="bf16", banks=banks[5:8], sb=sb,
                   sq_scratch=sqln, rows=rows, done_ctr=lnd,
                   r_toks=[tok(rl.cons)], eps_col=ln_col, ones_key="ones_col_bf_sb")

        P.wait_tok("tensor", [tok(lnd)])
        for tt in range(4):
            P.wait("vector", P.ctr("rt"))
            P.wait("tensor", vv)
            for kc in range(16):
                P.op("tensor", lambda e, tt=tt, kc=kc: e.matmul(banks[0][:, 0:8],
                     rbf[:, kc * 512 + tt * 128: kc * 512 + tt * 128 + 128],
                     sb["W_routerT_bf"][:, kc * 8:(kc + 1) * 8],
                     start=(kc == 0), stop=(kc == 15)),
                     inc=(P.ctr("rt") if kc == 15 else None))
            P.wait("vector", P.ctr("rt"))
            P.op("vector", lambda e, tt=tt: e.tensor_copy(out=glog[:, tt * 8:(tt + 1) * 8],
                 in_=banks[0][:, 0:8]), inc=vv)
        for tt in range(4):
            gl = glog[:, tt * 8:(tt + 1) * 8]
            P.op("vector", lambda e, gl=gl: e.reduce_max(gcol[:, 0:1], gl, axis=AX.X))
            P.op("vector", lambda e, gl=gl: e.tensor_scalar(out=gsc[:], in0=gl,
                 scalar1=gcol[:, 0:1], scalar2=None, op0=ALU.is_equal))
            P.op("vector", lambda e, gl=gl: e.scalar_tensor_tensor(out=gsc[:], in0=gsc[:],
                 scalar=-1.0e30, in1=gl, op0=ALU.mult, op1=ALU.add))
            P.op("vector", lambda e: e.reduce_max(gcol[:, 1:2], gsc[:], axis=AX.X))
            P.op("vector", lambda e, gl=gl: e.tensor_scalar(out=gsc[:], in0=gl,
                 scalar1=gcol[:, 1:2], scalar2=None, op0=ALU.is_ge))
            P.op("vector", lambda e, gl=gl: e.reduce_max(gcol[:, 2:3], gl, axis=AX.X,
                 negate=True), inc=vv)
            P.wait("scalar", vv)
            P.op("scalar", lambda e, gl=gl: e.activation(out=gl, in_=gl, func=AF.Exp,
                 bias=gcol[:, 2:3], scale=1.0), inc=ev)
            P.wait("vector", ev)
            P.op("vector", lambda e, gl=gl: e.tensor_mul(gl, gl, gsc[:]))
            P.op("vector", lambda e, gl=gl: e.reduce_sum(gcol[:, 3:4], gl, axis=AX.X))
            P.op("vector", lambda e: e.reciprocal(gcol[:, 3:4], gcol[:, 3:4]))
            P.op("vector", lambda e: e.tensor_scalar_mul(gcol[:, 3:4], gcol[:, 3:4],
                 float(SCALE0)))
            P.op("vector", lambda e, gl=gl: e.tensor_scalar(out=gl, in0=gl,
                 scalar1=gcol[:, 3:4], scalar2=None, op0=ALU.mult), inc=vv)
            P.wait_tok("tensor", [tok(vv)])
            P.wait("tensor", vv)
            for kc in range(16):
                P.op("tensor", lambda e, tt=tt, kc=kc: e.matmul(banks[1][:, 0:64],
                     rbf[:, kc * 512 + tt * 128: kc * 512 + tt * 128 + 128],
                     sb["A_allT_bf"][:, kc * 64:(kc + 1) * 64],
                     start=(kc == 0), stop=(kc == 15)),
                     inc=(P.ctr("ha") if kc == 15 else None))
            P.wait("vector", P.ctr("ha"))
            for e0 in range(8):
                P.op("vector", lambda e, e0=e0, gl=gl: e.tensor_scalar(
                     out=hallf[:, e0 * 8:(e0 + 1) * 8], in0=banks[1][:, e0 * 8:(e0 + 1) * 8],
                     scalar1=gl[:, e0:e0 + 1], scalar2=None, op0=ALU.mult),
                     inc=(vv if e0 == 7 else None))
            P.wait_tok("tensor", [tok(vv)])
            P.op("tensor", lambda e: e.transpose(banks[2][0:64, 0:128], hallf[:, 0:64],
                 sb["ident_f_sb"][:]), inc=P.ctr("tp3"))
            P.wait("scalar", P.ctr("tp3"))
            P.op("scalar", lambda e, tt=tt: e.activation(out=hprT[0:64, tt * 128:(tt + 1) * 128],
                 in_=banks[2][0:64, 0:128], func=AF.Identity, bias=0.0, scale=1.0), inc=ev)
        wavetok = [None, None]
        pre0 = [tok(ev), tok(vv)]
        for qw in range(4):
            par = qw % 2
            fts = list(range(qw * 4, (qw + 1) * 4))
            chunks = []
            for kc in range(16):
                ch_ = ws.chunk(wbB, kc * 128, qw * 512, 512,
                               row1=((kc + 1) * 128 if kc % 2 == 0 else None))
                ch_["lcols"] = lambda fl, ft: fl * 128
                ch_["rhs"] = rbf[:, kc * 512: kc * 512 + 512]
                if kc == 0 and qw == 0:
                    ch_["toks"] = ch_["toks"] + [tok(ev), tok(vv)]
                chunks.append(ch_)
            chunks.append({"lhsT": (lambda fl, ft, qw=qw:
                           sb["B_allT_pad_bf"][:, qw * 512 + fl * 128: qw * 512 + fl * 128 + 128]),
                           "rhs": hprT[:, 0:512],
                           "toks": ([tok(ev)] if qw == 0 else [])})

            def ev_wb(fl, ft, ps):
                bcol = bpk[:, bp.col("b_base", ft):bp.col("b_base", ft) + 1]
                P.wait("scalar", vv, max(0, vv.val - 1))
                P.op("scalar", lambda e, i=ps, bcol=bcol, ft=ft: e.activation(
                     out=tmpf[:, (ft % 2) * 512:(ft % 2) * 512 + 512], in_=i,
                     func=AF.Identity, bias=bcol, scale=1.0), inc=ev)
                P.wait_tok("vector", [tok(ev)])
                P.op("vector", lambda e, ft=ft: e.tensor_add(
                     att0[:, ft * 512:(ft + 1) * 512],
                     tmpf[:, (ft % 2) * 512:(ft % 2) * 512 + 512],
                     rbf[:, ft * 512:(ft + 1) * 512]), inc=vv)

            pre = [wavetok[par]] if wavetok[par] else pre0
            gemm(P, f"awb{r}{qw}", chunks=chunks, ftiles=fts, N=512,
                 banks=banks[par * 4:(par + 1) * 4], evac=ev_wb, pre_toks=pre)
            wavetok[par] = tok(ev)
        layer_norm(P, f"aln2{r}", rbuf=att0, rT=512, t0=0, N=512,
                   out_fn=lambda ft: att0[:, ft * 512:(ft + 1) * 512],
                   out_dtype_note="bf16", banks=banks[5:8], sb=sb,
                   sq_scratch=sqln, rows=rows, done_ctr=lnd,
                   r_toks=[tok(vv)], eps_col=ln_col, ones_key="ones_col_bf_sb")
        P.wait("gpsimd", lnd)
        P.wait("vector", st)
        P.op("vector", lambda e: e.nop(), inc=vv)
        P.wait("gpsimd", vv)
        for ft in range(16):
            emit_dma(P, "gpsimd", x1T[ft * 128:(ft + 1) * 128, g1:g1 + 512],
                     att0[:, ft * 512:(ft + 1) * 512], st)
        P.barrier()

    P.wait("sync", st)
    P.wait("gpsimd", st)


A_CONSTS = ["ident_f", "ones_col_f", "ones_col_bf", "ones_row_f", "ones_row_bf",
            "bn_row", "genmask0", "kvc_bias_rep", "idx_bias_rep", "qnrow", "knrow"]


def build_launch_A(bp):
    nc = bass.Bass("TRN2", target_bir_lowering=False)
    ctx = ExitStack()
    P = Prog(nc, ctx)
    with ctx:
        io = {}
        io["hT_full"] = nc.dram_tensor("hT_full", [D, T], F32, kind="ExternalInput").ap()
        io["hT_loc"] = nc.dram_tensor("hT_loc", [D, TLOCP], F32, kind="ExternalInput").ap()
        io["x1T"] = nc.dram_tensor("x1T", [D, T1], BF16, kind="ExternalOutput").ap()
        wsrc = {}
        for nm, shape in [("w_qc", [D, QC]), ("w_qup", [QC, D]), ("group_w", [G0 * DG, DG]),
                          ("final_w", [D, D]), ("W_baseT", [D, D])]:
            wsrc[nm] = nc.dram_tensor(nm, list(shape), F32, kind="ExternalInput").ap()
        smallsrc = {}
        for nm, R, Cc in A_SMALL:
            smallsrc[nm] = nc.dram_tensor(nm, [R, Cc], F32, kind="ExternalInput").ap()
        io["qcB"] = nc.dram_tensor("qcB", [D, QC], BF16, kind="Internal").ap()
        io["qupB"] = nc.dram_tensor("qupB", [QC, D], BF16, kind="Internal").ap()
        io["gwB"] = nc.dram_tensor("gwB", [G0 * DG, DG], BF16, kind="Internal").ap()
        io["finB"] = nc.dram_tensor("finB", [D, D], BF16, kind="Internal").ap()
        io["wbB"] = nc.dram_tensor("wbB", [D, D], BF16, kind="Internal").ap()

        sb = load_consts(P, ctx, nc, A_CONSTS, bp.array_cols)
        banks = [ctx.enter_context(nc.psum_tensor(f"bank{i}", [128, 512], F32)) for i in range(8)]

        for nm, R, Cc in A_SMALL:
            nt = max(1, R // 128)
            sb[nm + "_bf"] = ctx.enter_context(nc.sbuf_tensor(nm + "_bf", [128, nt * Cc], BF16))
        with ExitStack() as wctx:
            smstage = wctx.enter_context(nc.sbuf_tensor("a_smst", [128, 2048], F32))
            sm = P.ctr("smw")
            smc = P.ctr("smc")
            for nm, R, Cc in A_SMALL:
                nt = max(1, R // 128)
                t = sb[nm + "_bf"]
                for i in range(nt):
                    rr = min(128, R - i * 128)
                    P.wait("sync", smc)
                    emit_dma(P, "sync", smstage[0:rr, 0:Cc],
                             smallsrc[nm][i * 128:i * 128 + rr, 0:Cc], sm)
                    P.wait("vector", sm)
                    P.op("vector", lambda e, t=t, i=i, Cc=Cc, rr=rr: e.tensor_copy(
                         out=t[0:rr, i * Cc:(i + 1) * Cc], in_=smstage[0:rr, 0:Cc]), inc=smc)
            for e_ in ("tensor", "scalar", "gpsimd"):
                P.wait(e_, smc)
            wcd = P.ctr("wconv_done")
            jobs = [(wsrc["w_qc"], io["qcB"], D, QC), (wsrc["w_qup"], io["qupB"], QC, D),
                    (wsrc["group_w"], io["gwB"], G0 * DG, DG),
                    (wsrc["final_w"], io["finB"], D, D), (wsrc["W_baseT"], io["wbB"], D, D)]
            emit_wconv(P, wctx, jobs, wcd)
            P.wait("sync", wcd)

        ws = WStream(P, ctx)
        emit_p0p1(P, ctx, io, sb, banks, bp, ws, None)
        P.run()
    return nc




FUSED_CONSTS = sorted(set(A_CONSTS + ["ones_row_f", "ones_row_bf", "ident_f",
                                      "ones_col_f", "ones_col_bf"]))
FUSED_P2_EXTRA = ["genmask1", "gqrow", "gkrow"]
FUSED_P3_EXTRA = ["selbc", "bg0", "bg1"]


def build_fused(bp):
    nc = bass.Bass("TRN2", target_bir_lowering=False)
    ctx = ExitStack()
    P = Prog(nc, ctx)
    with ctx:
        io = {}
        io["hT_full"] = nc.dram_tensor("hT_full", [D, T], F32, kind="ExternalInput").ap()
        io["hT_loc"] = nc.dram_tensor("hT_loc", [D, TLOCP], F32, kind="ExternalInput").ap()
        io["x1T"] = nc.dram_tensor("x1T", [D, T1], BF16, kind="Internal").ap()
        io["x2T"] = nc.dram_tensor("x2T", [D, OWN], BF16, kind="Internal").ap()
        io["xtT"] = nc.dram_tensor("xtT", [D, OWN], BF16, kind="Internal").ap()
        io["outT"] = nc.dram_tensor("outT", [D, OWN], F32, kind="ExternalOutput").ap()
        wsrc = {}
        for nm, shape in [("w_qc", [D, QC]), ("w_qup", [QC, D]), ("group_w", [G0 * DG, DG]),
                          ("final_w", [D, D]), ("W_baseT", [D, D]),
                          ("wqT", [D, D]), ("wkT", [D, 512]), ("wvT", [D, 512]), ("woT", [D, D]),
                          ("Wup2d", [NEXP1 * D, E1]), ("Wsw2d", [NEXP1 * E1, E1]),
                          ("Wdn2d", [NEXP1 * E1, D]), ("Wsu", [D, D]), ("Wsa", [E1, E1]),
                          ("Wsd", [D, D]), ("Wg0", [D, 8]), ("Wg1", [D, 8]),
                          ("bdn_pad", [128, D])]:
            wsrc[nm] = nc.dram_tensor(nm, list(shape), F32, kind="ExternalInput").ap()
        for nm, shape in [("qcB", [D, QC]), ("qupB", [QC, D]), ("gwB", [G0 * DG, DG]),
                          ("finB", [D, D]), ("wbB", [D, D]),
                          ("wqB", [D, D]), ("wkB", [D, 512]), ("wvB", [D, 512]), ("woB", [D, D]),
                          ("WupB", [NEXP1 * D, E1]), ("WswB", [NEXP1 * E1, E1]),
                          ("WdnB", [NEXP1 * E1, D]), ("WsuB", [D, D]), ("WsaB", [E1, E1]),
                          ("WsdB", [D, D])]:
            io[nm] = nc.dram_tensor(nm, list(shape), BF16, kind="Internal").ap()
        smallsrc = {nm: wsrc.get(nm) for nm in []}
        a_smallsrc = {}
        for nm, R, Cc in A_SMALL:
            a_smallsrc[nm] = nc.dram_tensor(nm, [R, Cc], F32, kind="ExternalInput").ap()

        sb = load_consts(P, ctx, nc, FUSED_CONSTS, bp.array_cols)
        banks = [ctx.enter_context(nc.psum_tensor(f"bank{i}", [128, 512], F32)) for i in range(8)]

        with ExitStack() as actx:
            for nm, R, Cc in A_SMALL:
                nt = max(1, R // 128)
                sb[nm + "_bf"] = actx.enter_context(
                    nc.sbuf_tensor(nm + "_bf", [128, nt * Cc], BF16))
            with ExitStack() as wctx:
                smstage = wctx.enter_context(nc.sbuf_tensor("a_smst", [128, 2048], F32))
                sm = P.ctr("smw")
                smc = P.ctr("smc")
                for nm, R, Cc in A_SMALL:
                    nt = max(1, R // 128)
                    t = sb[nm + "_bf"]
                    for i in range(nt):
                        rr = min(128, R - i * 128)
                        P.wait("sync", smc)
                        emit_dma(P, "sync", smstage[0:rr, 0:Cc],
                                 a_smallsrc[nm][i * 128:i * 128 + rr, 0:Cc], sm)
                        P.wait("vector", sm)
                        P.op("vector", lambda e, t=t, i=i, Cc=Cc, rr=rr, s2=smstage:
                             e.tensor_copy(out=t[0:rr, i * Cc:(i + 1) * Cc],
                                           in_=s2[0:rr, 0:Cc]), inc=smc)
                for e_ in ("tensor", "scalar", "gpsimd"):
                    P.wait(e_, smc)
                wcd = P.ctr("wconv_done")
                jobs = [(wsrc["w_qc"], io["qcB"], D, QC), (wsrc["w_qup"], io["qupB"], QC, D),
                        (wsrc["group_w"], io["gwB"], G0 * DG, DG),
                        (wsrc["final_w"], io["finB"], D, D), (wsrc["W_baseT"], io["wbB"], D, D),
                        (wsrc["wqT"], io["wqB"], D, 2048), (wsrc["wkT"], io["wkB"], D, 512),
                        (wsrc["wvT"], io["wvB"], D, 512), (wsrc["woT"], io["woB"], D, 2048),
                        (wsrc["Wup2d"], io["WupB"], NEXP1 * D, E1),
                        (wsrc["Wsw2d"], io["WswB"], NEXP1 * E1, E1),
                        (wsrc["Wdn2d"], io["WdnB"], NEXP1 * E1, D),
                        (wsrc["Wsu"], io["WsuB"], D, D), (wsrc["Wsa"], io["WsaB"], E1, E1),
                        (wsrc["Wsd"], io["WsdB"], D, D)]
                emit_wconv(P, wctx, jobs, wcd)
                P.wait("sync", wcd)

            ws = WStream(P, actx, name="slabA")
            emit_p0p1(P, actx, io, sb, banks, bp, ws, None)
            P.barrier()

        with ExitStack() as bctx:
            sb.update(load_consts(P, bctx, nc, FUSED_P2_EXTRA, bp.array_cols,
                                  with_bpack=False))
            ws2 = WStream(P, bctx, name="slabB")
            emit_p2(P, bctx, io, sb, banks, bp, ws2)
            P.barrier()

        with ExitStack() as cctx:
            sb.update(load_consts(P, cctx, nc, FUSED_P3_EXTRA, bp.array_cols,
                                  with_bpack=False))
            sb["bdn_pad_bf"] = cctx.enter_context(nc.sbuf_tensor("bdnp_sb", [128, D], BF16))
            sb["Wg0_bf"] = cctx.enter_context(nc.sbuf_tensor("wg0_sb", [128, 16 * 8], BF16))
            sb["Wg1_bf"] = cctx.enter_context(nc.sbuf_tensor("wg1_sb", [128, 16 * 8], BF16))
            with ExitStack() as wctx2:
                smstage = wctx2.enter_context(nc.sbuf_tensor("c_smst", [128, 2048], F32))
                sm = P.ctr("smw")
                smc = P.ctr("smc")
                P.op("gpsimd", lambda e: e.memset(smstage[:], 0.0), inc=smc)
                P.wait("sync", smc)
                emit_dma(P, "sync", smstage[:, 0:D], wsrc["bdn_pad"][0:128, 0:D], sm)
                P.wait("vector", sm)
                P.op("vector", lambda e, s2=smstage: e.tensor_copy(out=sb["bdn_pad_bf"][:],
                     in_=s2[:, 0:D]), inc=smc)
                for nm, key in [("Wg0", "Wg0_bf"), ("Wg1", "Wg1_bf")]:
                    P.wait("sync", smc)
                    for kc in range(16):
                        emit_dma(P, "sync", smstage[:, kc * 8:(kc + 1) * 8],
                                 wsrc[nm][kc * 128:(kc + 1) * 128, 0:8], sm)
                    P.wait("vector", sm)
                    P.op("vector", lambda e, d=sb[key], s2=smstage: e.tensor_copy(
                         out=d[:], in_=s2[:, 0:128]), inc=smc)
                for e_ in ("tensor", "scalar", "gpsimd"):
                    P.wait(e_, smc)
                P.wait("sync", smc)
            ws3 = WStream(P, cctx, name="slabC")
            emit_p3(P, cctx, io, sb, banks, bp, ws3)
        P.run()
    return nc


FUSED_INPUT_NAMES = None


def kernel_fused(per_core, bp):
    nc = build_fused(bp)
    names = ["hT_full", "hT_loc", "w_qc", "w_qup", "group_w", "final_w", "W_baseT",
             "wqT", "wkT", "wvT", "woT", "Wup2d", "Wsw2d", "Wdn2d", "Wsu", "Wsa", "Wsd",
             "Wg0", "Wg1", "bdn_pad",
             "kvc_w", "kvc_wz", "idx_pack", "k_proj", "v_proj", "w_dq", "w_iuq", "w_w",
             "W_routerT", "A_allT", "B_allT_pad", "bpack",
             "C0", "S0", "kval0b", "tpos_cols", "C1", "S1", "kval1b"] + \
            list(FUSED_CONSTS) + FUSED_P2_EXTRA + FUSED_P3_EXTRA
    in_maps = [{k: per_core[c][k] for k in names} for c in range(NCORES)]
    res = run_bass_kernel_spmd(nc, in_maps, core_ids=list(range(NCORES)))
    out = np.zeros((B, T, D), np.float32)
    for c in range(NCORES):
        b, ch = c // CHUNKS, c % CHUNKS
        s = ch * OWN
        out[b, s:s + OWN, :] = np.asarray(res.results[c]["outT"], np.float32).T
    return out


A_INPUT_NAMES = ["hT_full", "hT_loc", "w_qc", "w_qup", "group_w", "final_w", "W_baseT",
                 "kvc_w", "kvc_wz", "idx_pack", "k_proj", "v_proj", "w_dq", "w_iuq", "w_w",
                 "W_routerT", "A_allT", "B_allT_pad",
                 "ident_f", "ones_col_f", "ones_col_bf", "ones_row_f", "ones_row_bf",
                 "bn_row", "genmask0", "kvc_bias_rep", "idx_bias_rep", "qnrow", "knrow",
                 "bpack", "C0", "S0", "kval0b", "tpos_cols"]
P2_INPUT_NAMES = ["wqT", "wkT", "wvT", "woT", "kval1b", "C1", "S1",
                  "ident_f", "ones_col_f", "ones_row_f", "ones_row_bf", "genmask1",
                  "gqrow", "gkrow", "bpack"]
P3_INPUT_NAMES = ["Wup2d", "Wsw2d", "Wdn2d", "Wsu", "Wsa", "Wsd", "Wg0", "Wg1", "bdn_pad",
                  "bg0", "bg1", "ident_f", "ones_col_f", "ones_col_bf", "ones_row_f",
                  "ones_row_bf", "selbc", "bpack"]


def kernel(**inputs):
    per_core, bp = host_prep(inputs)
    return kernel_fused(per_core, bp)


def kernel_multilaunch(**inputs):
    per_core, bp = host_prep(inputs)

    ncA = build_launch_A(bp)
    resA = run_bass_kernel_spmd(
        ncA, [{k: per_core[c][k] for k in A_INPUT_NAMES} for c in range(NCORES)],
        core_ids=list(range(NCORES)))

    ncB = build_launch_p2(bp)
    in_b = []
    for c in range(NCORES):
        m = {k: per_core[c][k] for k in P2_INPUT_NAMES}
        m["x1T"] = np.asarray(resA.results[c]["x1T"])
        in_b.append(m)
    resB = run_bass_kernel_spmd(ncB, in_b, core_ids=list(range(NCORES)))

    ncC = build_launch_p3(bp)
    in_c = []
    for c in range(NCORES):
        m = {k: per_core[c][k] for k in P3_INPUT_NAMES}
        m["x2T"] = np.asarray(resB.results[c]["x2T"], dtype=np.float32)
        in_c.append(m)
    resC = run_bass_kernel_spmd(ncC, in_c, core_ids=list(range(NCORES)))

    out = np.zeros((B, T, D), np.float32)
    for c in range(NCORES):
        b, ch = c // CHUNKS, c % CHUNKS
        s = ch * OWN
        out[b, s:s + OWN, :] = np.asarray(resC.results[c]["outT"], np.float32).T
    return out
